# version 18
# baseline (speedup 1.0000x reference)
import contextlib
import math
import numpy as np

import concourse.bass as bass
import concourse.mybir as mybir
from concourse.bass_utils import run_bass_kernel_spmd

F32 = mybir.dt.float32
BF16 = mybir.dt.bfloat16
U8 = mybir.dt.uint8
I32 = mybir.dt.int32
AF = mybir.ActivationFunctionType
OP = mybir.AluOpType
AX = mybir.AxisListType

NCORE = 8
T = 4096
D = 1024
LN_EPS = 1e-5

ENG2BLOCK = {"sp": "sync", "act": "scalar", "dve": "vector",
             "pool": "gpsimd", "pe": "tensor"}


class Prog:

    NDMASEM = 48

    def __init__(self, nc, es):
        self.nc = nc
        self.es = es
        self.sems = {e: es.enter_context(nc.semaphore("sem_" + e))
                     for e in ENG2BLOCK}
        self.cnt = {e: 0 for e in ENG2BLOCK}
        self.dsems = [es.enter_context(nc.semaphore(f"dsem{i}"))
                      for i in range(self.NDMASEM)]
        self.dcum = [0] * self.NDMASEM
        self._dma_pool = {"sp": list(range(0, 24)),
                          "pool": list(range(24, 36)),
                          "act": list(range(36, 44)),
                          "dve": list(range(44, 46)),
                          "pe": list(range(46, 48))}
        self._dma_rri = {k: 0 for k in self._dma_pool}
        self.thunks = {e: [] for e in ENG2BLOCK}
        self.deps = {}
        self.wm = {e: {} for e in ENG2BLOCK}
        self.names = set()
        self.regions = []
        self._cpool = self.sbuf("k_cpool", [128, 16], F32)
        self._cvstg = (self.sbuf("g_cvs", [128, 1024], F32),
                       self.sbuf("g_cvb", [128, 1024], BF16))
        self._cvstg2 = (self.sbuf("g_cvs2", [128, 1024], F32),
                        self.sbuf("g_cvb2", [128, 1024], BF16))
        self._cvals = {}

    def cconst(self, value):
        if value in self._cvals:
            return self._cvals[value]
        i = len(self._cvals)
        assert i < 16
        ap = self._cpool[:, i:i + 1]
        key = ("cconst", i)
        self.emit("dve", lambda E, ap=ap, value=value: E.memset(ap, value),
                  writes=[key])
        self._cvals[value] = (ap, key)
        return self._cvals[value]

    def sbuf(self, name, shape, dtype):
        name = getattr(self, "prefix", "") + name
        assert name not in self.names, name
        self.names.add(name)
        st = self.regions[-1] if self.regions else self.es
        return st.enter_context(self.nc.sbuf_tensor(name, shape, dtype))

    def psum(self, name, shape, dtype=F32):
        name = getattr(self, "prefix", "") + name
        assert name not in self.names, name
        self.names.add(name)
        st = self.regions[-1] if self.regions else self.es
        return st.enter_context(self.nc.psum_tensor(name, shape, dtype))

    def dram(self, name, shape, dtype=F32, kind="Internal"):
        name = getattr(self, "prefix", "") + name
        return self.nc.dram_tensor(name, list(shape), dtype, kind=kind).ap()

    def push_region(self):
        self.regions.append(contextlib.ExitStack())
        if not hasattr(self, "_region_marks"):
            self._region_marks = []
        self._region_marks.append((dict(self.cnt), list(self.dcum)))

    def pop_region(self):
        cnt0, dcum0 = self._region_marks.pop()
        for e in ENG2BLOCK:
            for s in ENG2BLOCK:
                if s != e and self.cnt[s] > cnt0[s]:
                    self._wait(e, s, self.cnt[s])
            for i in range(self.NDMASEM):
                if self.dcum[i] > dcum0[i]:
                    self._wait(e, ("d", i), self.dcum[i])
        self.regions.pop().close()

    def _wait(self, eng, src, val):
        if val <= 0:
            return
        if eng == "pe" and src == "pe":
            return
        if self.wm[eng].get(src, 0) >= val:
            return
        self.wm[eng][src] = val
        if getattr(self, "log", None) is not None:
            self._last_waits = getattr(self, "_last_waits", [])
            self._last_waits.append((eng, src, val))
        sem = self.dsems[src[1]] if isinstance(src, tuple) else self.sems[src]
        self.thunks[eng].append(lambda E, sem=sem, val=val: E.wait_ge(sem, val))

    def _dep_waits(self, eng, reads, writes):
        for k in reads:
            d = self.deps.get(k)
            if d and d["w"]:
                self._wait(eng, *d["w"])
        for k in writes:
            d = self.deps.get(k)
            if d:
                if d["w"]:
                    self._wait(eng, *d["w"])
                for src, c in d["r"].items():
                    self._wait(eng, src, c)

    def _record(self, reads, writes, dep):
        src, c = dep
        for k in reads:
            d = self.deps.setdefault(k, {"w": None, "r": {}})
            d["r"][src] = c
        for k in writes:
            self.deps[k] = {"w": dep, "r": {}}

    def emit(self, eng, fn, reads=(), writes=()):
        self._log_pre(eng)
        self._dep_waits(eng, reads, writes)
        self.cnt[eng] += 1
        c = self.cnt[eng]
        sem = self.sems[eng]

        def thunk(E, fn=fn, sem=sem):
            fn(E).then_inc(sem, 1)

        self.thunks[eng].append(thunk)
        self._record(reads, writes, (eng, c))
        self._log(eng, c, reads, writes)
        return c

    def _log_pre(self, eng):
        if getattr(self, "log", None) is not None:
            self._log_mark = len(self.thunks[eng])

    def _log(self, eng, c, reads, writes):
        if getattr(self, "log", None) is None:
            return
        import traceback
        fr = traceback.extract_stack(limit=6)
        site = "|".join(f"{f.filename.split('/')[-1]}:{f.lineno}"
                        for f in fr[:-2][-3:])
        waits = tuple(getattr(self, "_last_waits", []))
        self._last_waits = []
        self.log.append((eng, c, tuple(reads), tuple(writes), waits, site))

    def dma(self, eng, fns, reads=(), writes=()):
        if callable(fns):
            fns = [fns]
        self._log_pre(eng)
        pool = self._dma_pool[eng]
        j = self._dma_rri[eng]
        self._dma_rri[eng] = (j + 1) % len(pool)
        i = pool[j]
        sem = self.dsems[i]
        self._wait(eng, ("d", i), self.dcum[i])
        self._dep_waits(eng, reads, writes)
        self.dcum[i] += 16 * len(fns)
        target = self.dcum[i]

        def thunk(E, fns=tuple(fns), sem=sem):
            for f in fns:
                f(E).then_inc(sem, 16)

        self.thunks[eng].append(thunk)
        self._record(reads, writes, (("d", i), target))
        self._log(eng, ("d", i, target), reads, writes)
        return (("d", i), target)

    def barrier(self, eng, srcs=None):
        for s in (srcs or ENG2BLOCK):
            if s != eng:
                self._wait(eng, s, self.cnt[s])
        if srcs is None:
            for i in range(self.NDMASEM):
                self._wait(eng, ("d", i), self.dcum[i])

    def finalize(self):
        for i in range(self.NDMASEM):
            self._wait("sp", ("d", i), self.dcum[i])
        for e in ENG2BLOCK:
            if self.cnt[e] > 0:
                sem, val = self.sems[e], self.cnt[e]
                self.thunks[e].append(
                    lambda E, sem=sem, val=val: E.wait_ge(sem, val))
        block = self.es.enter_context(self.nc.Block())
        for e, bname in ENG2BLOCK.items():
            fns = self.thunks[e]
            if not fns:
                continue
            deco = getattr(block, bname)

            def body(eng, fns=fns):
                for f in fns:
                    f(eng)
            deco(body)



def host_constants():
    import ml_dtypes
    c = {}
    c["c_ident"] = np.eye(128, dtype=ml_dtypes.bfloat16)
    c["c_ident_f32"] = np.eye(128, dtype=np.float32)
    c["c_ones_f32"] = np.ones((128, 128), dtype=np.float32)
    tri = np.zeros((128, 4 * 512), dtype=ml_dtypes.bfloat16)
    for j in range(4):
        kk = np.arange(128)[:, None]
        qq = np.arange(512)[None, :]
        tri[:, j * 512:(j + 1) * 512] = (kk + 128 * j <= qq).astype(
            ml_dtypes.bfloat16)
    c["c_tri"] = tri
    sel = np.zeros((16, 16 * 128), dtype=ml_dtypes.bfloat16)
    for e in range(16):
        sel[e, e * 128:(e + 1) * 128] = 1.0
    c["c_sel"] = sel
    sel32 = np.zeros((32, 32 * 128), dtype=np.float32)
    for i in range(32):
        sel32[i, i * 128:(i + 1) * 128] = 1.0
    c["c_sel32"] = sel32
    return c


def declare_constants(P):
    nc = P.nc
    cts = {}
    cts["c_ident"] = nc.dram_tensor("c_ident", [128, 128], BF16,
                                    kind="ExternalInput").ap()
    cts["c_ones_f32"] = nc.dram_tensor("c_ones_f32", [128, 128], F32,
                                       kind="ExternalInput").ap()
    cts["c_ident_f32"] = nc.dram_tensor("c_ident_f32", [128, 128], F32,
                                        kind="ExternalInput").ap()
    cts["c_tri"] = nc.dram_tensor("c_tri", [128, 4 * 512], BF16,
                                  kind="ExternalInput").ap()
    cts["c_sel"] = nc.dram_tensor("c_sel", [16, 16 * 128], BF16,
                                  kind="ExternalInput").ap()
    cts["c_sel32"] = nc.dram_tensor("c_sel32", [32, 32 * 128], F32,
                                    kind="ExternalInput").ap()
    sb = {}
    sb["ident"] = P.sbuf("k_ident", [128, 128], BF16)
    sb["ones_f32"] = P.sbuf("k_ones_f32", [128, 128], F32)
    sb["ident_f32"] = P.sbuf("k_ident_f32", [128, 128], F32)
    sb["tri"] = P.sbuf("k_tri", [128, 4 * 512], BF16)
    sb["sel"] = P.sbuf("k_sel", [16, 16 * 128], BF16)
    for k, hbm in cts.items():
        key = k.replace("c_", "")
        if key not in sb:
            continue
        P.dma("sp", lambda E, d=sb[key], s=hbm: E.dma_start(out=d[:], in_=s),
              writes=[("const", key)])
    sb["sel32_hbm"] = cts["c_sel32"]
    return sb


def load_sel32(P, consts):
    consts["sel32"] = P.sbuf("k_sel32", [32, 32 * 128], F32)
    P.dma("sp", lambda E, d=consts["sel32"], s=consts["sel32_hbm"]:
              E.dma_start(out=d[:], in_=s),
          writes=[("const", "sel32")])



def emit_ln(P, name, x_hbm, xn_sb, consts, TB=512, xbuf=None,
            xn_f32_hbm=None):
    nt = T // TB
    if xbuf is None:
        xbuf = P.sbuf(name + "_xb", [128, 2 * 8 * TB], F32)
    sq = P.sbuf(name + "_sq", [128, TB], F32)
    mu = P.sbuf(name + "_mu", [128, 2 * TB], F32)
    rstd = P.sbuf(name + "_rstd", [128, 2 * TB], F32)
    var = P.sbuf(name + "_var", [128, TB], F32)
    ps_s = P.psum(name + "_ps_s", [128, TB], F32)
    ps_q = P.psum(name + "_ps_q", [128, TB], F32)
    ones = consts["ones_f32"]
    eps_ap, eps_key = P.cconst(LN_EPS)
    for tb in range(nt):
        buf = tb % 2
        xks = [xbuf[:, (buf * 8 + k) * TB:(buf * 8 + k + 1) * TB]
               for k in range(8)]
        P.dma("sp",
              [(lambda E, d=xks[k], s=x_hbm[k * 128:(k + 1) * 128,
                                            tb * TB:(tb + 1) * TB]:
                    E.dma_start(out=d, in_=s)) for k in range(8)],
              writes=[(name, "xb", buf, k) for k in range(8)])
        for k in range(8):
            P.emit("act",
                   lambda E, xk=xks[k]: E.activation(sq[:], xk, AF.Square),
                   reads=[(name, "xb", buf, k)], writes=[(name, "sq")])
            P.emit("pe",
                   lambda E, xk=xks[k], st=(k == 0), sp=(k == 7):
                       E.matmul(ps_s[:], ones[:], xk, start=st, stop=sp),
                   reads=[(name, "xb", buf, k), ("const", "ones_f32")],
                   writes=[(name, "ps_s")] if k in (0, 7) else [])
            P.emit("pe",
                   lambda E, st=(k == 0), sp=(k == 7):
                       E.matmul(ps_q[:], ones[:], sq[:], start=st, stop=sp),
                   reads=[(name, "sq"), ("const", "ones_f32")],
                   writes=[(name, "ps_q")] if k in (0, 7) else [])
        mub = mu[:, buf * TB:(buf + 1) * TB]
        rstdb = rstd[:, buf * TB:(buf + 1) * TB]
        P.emit("act", lambda E, mub=mub: E.activation(mub, ps_s[:], AF.Copy,
                                                      scale=1.0 / D),
               reads=[(name, "ps_s")], writes=[(name, "mu", buf)])
        P.emit("dve", lambda E, mub=mub: E.tensor_tensor(var[:], mub, mub,
                                                         OP.mult),
               reads=[(name, "mu", buf)], writes=[(name, "var")])
        P.emit("dve", lambda E: E.scalar_tensor_tensor(
                   var[:], ps_q[:], 1.0 / D, var[:], OP.mult, OP.subtract),
               reads=[(name, "ps_q"), (name, "var")], writes=[(name, "var")])
        P.emit("act", lambda E: E.activation(var[:], var[:], AF.Ln,
                                             bias=eps_ap),
               reads=[(name, "var"), eps_key], writes=[(name, "var")])
        P.emit("act", lambda E, rstdb=rstdb: E.activation(rstdb, var[:],
                                                          AF.Exp, scale=-0.5),
               reads=[(name, "var")], writes=[(name, "rstd", buf)])
        for k in range(8):
            xnk = xn_sb[:, k * T + tb * TB: k * T + (tb + 1) * TB]
            P.emit("dve",
                   lambda E, xk=xks[k], mub=mub: E.tensor_sub(sq[:], xk, mub),
                   reads=[(name, "xb", buf, k), (name, "mu", buf)],
                   writes=[(name, "sq")])
            P.emit("dve",
                   lambda E, xnk=xnk, rstdb=rstdb:
                       E.tensor_mul(xnk, sq[:], rstdb),
                   reads=[(name, "sq"), (name, "rstd", buf)],
                   writes=[(name, "xn", k)])
            if xn_f32_hbm is not None:
                P.emit("dve",
                       lambda E, rstdb=rstdb: E.tensor_mul(sq[:], sq[:], rstdb),
                       reads=[(name, "sq"), (name, "rstd", buf)],
                       writes=[(name, "sq")])
                P.dma("sp",
                      lambda E, d=xn_f32_hbm[k * 128:(k + 1) * 128,
                                             tb * TB:(tb + 1) * TB]:
                          E.dma_start(out=d, in_=sq[:]),
                      reads=[(name, "sq")], writes=[(name, "xn32")])
    return xn_sb



class FMT:

    def __init__(self, P, name, nfeat, Tl, dtype):
        assert nfeat % 128 == 0
        self.P, self.name, self.nk, self.Tl, self.dtype = P, name, nfeat // 128, Tl, dtype
        self.sb = P.sbuf(name, [128, self.nk * Tl], dtype)

    def tile(self, k, a=0, b=None):
        b = self.Tl if b is None else b
        return self.sb[:, k * self.Tl + a: k * self.Tl + b]

    def key(self, k):
        return (self.name, k)

    def keys(self):
        return [(self.name, k) for k in range(self.nk)]


class PsRing:

    def __init__(self, P, name, n, shape, dtype=F32):
        self.ts = [P.psum(f"{name}{i}", shape, dtype) for i in range(n)]
        self.keys = [(name, i) for i in range(n)]
        self.n = n
        self.i = 0

    def get(self):
        i = self.i
        self.i = (self.i + 1) % self.n
        return self.ts[i], self.keys[i]


class WRing:

    def __init__(self, P, name, n, shape, dtype=BF16, q="sp"):
        self.ts = [P.sbuf(f"{name}{i}", shape, dtype) for i in range(n)]
        self.keys = [(name, i) for i in range(n)]
        self.n = n
        self.i = 0
        self.P = P
        self.q = q

    def load(self, src_ap, rows=None, cols=None, reads=()):
        i = self.i
        self.i = (self.i + 1) % self.n
        if hasattr(self, "slot2key"):
            old = self.slot2key.pop(i, None)
            if old is not None:
                self.cache.pop(old, None)
        r = rows if rows is not None else src_ap.shape[0]
        c = cols if cols is not None else src_ap.shape[1]
        dst = self.ts[i][0:r, 0:c]
        self.P.dma(self.q, lambda E, d=dst, s=src_ap: E.dma_start(out=d, in_=s),
                   reads=list(reads), writes=[self.keys[i]])
        return dst, self.keys[i], i

    def load_cached(self, ckey, src_ap, reads=()):
        if not hasattr(self, "slot2key"):
            self.slot2key, self.cache = {}, {}
        hit = self.cache.get(ckey)
        if hit is not None:
            return hit
        ap, dkey, slot = self.load(src_ap, reads=reads)
        self.slot2key[slot] = ckey
        self.cache[ckey] = (ap, dkey)
        return ap, dkey


def emit_bf16_scratch(P, name, src_hbm, rows, cols, eng=None, CW=None,
                      dmaq=None):
    dst = P.dram(name + "_bf", (rows, cols), BF16)
    total = rows * cols
    src_f = src_hbm.rearrange("a b -> (a b)")
    dst_f = dst.rearrange("a b -> (a b)")
    SP = 1 << 21
    fns = []
    for o0 in range(0, total, SP):
        n = min(SP, total - o0)
        fns.append(lambda E, d=dst_f[o0:o0 + n], s=src_f[o0:o0 + n]:
                   E.dma_start(out=d, in_=s))
    for i in range(0, len(fns), 4):
        P.dma("pool", fns[i:i + 4], writes=[(name, "out")])
    return dst, (name, "out")


def emit_gemm(P, name, xtile, xkey, Kt, wscr, F, ws, ps_ring, epi,
              f0=0, t0=0, t1=T, TMAC=1024, NSUB=512, extra_mm=None,
              fm_order=None):
    WWIN = ws.ts[0].shape[1]
    wsrc, wsrckey = wscr
    Ftot = wsrc.shape[1]
    wid = (wsrc.name, wsrc.offset)
    for ta in range(t0, t1, TMAC):
        tb = min(ta + TMAC, t1)
        for fm in (fm_order if fm_order is not None else range(F // 128)):
            ps, pskey = ps_ring.get()
            if extra_mm is not None:
                extra_mm(ps, pskey, fm, ta, tb)
            col = f0 + fm * 128
            w0 = (col // WWIN) * WWIN
            w1 = min(w0 + WWIN, Ftot)
            for k in range(Kt):
                w_ap, wkey = ws.load_cached(
                    (wsrckey, wid, k, w0),
                    wsrc[k * 128:(k + 1) * 128, w0:w1],
                    reads=[wsrckey])
                lhsT = w_ap[:, col - w0:col - w0 + 128]
                for ts0 in range(ta, tb, NSUB):
                    ts1 = min(ts0 + NSUB, tb)
                    st = (k == 0 and extra_mm is None)
                    sp = (k == Kt - 1)
                    xap = xtile(k, ts0, ts1)
                    xdep = xkey(k)
                    P.emit("pe",
                           lambda E, ps=ps, w=lhsT, x=xap,
                                  a=ts0 - ta, bnd=ts1 - ta, st=st, sp=sp:
                               E.matmul(ps[:, a:bnd], w, x, start=st, stop=sp),
                           reads=[wkey] + ([xdep] if xdep else []),
                           writes=[pskey] if (k == 0 or k == Kt - 1) else [])
            epi(ps, pskey, fm, ta, tb)



ATTN_C = 30.0


def emit_attention(P, name, xn, wqkv_scr, bqkv, consts, ctx_fm, nheads=16,
                   dhead=64, head_stride=None, rope=None, qb=512,
                   qk_fm=None, qk_hbm=None, v_off=2):
    nc = P.nc
    q_h = FMT(P, name + "_qh", 128, T, BF16)
    k_h = FMT(P, name + "_kh", 128, T, BF16)
    v_h = FMT(P, name + "_vh", 128, T, BF16)
    v_tm = P.sbuf(name + "_vtm", [128, 32 * 65], BF16)
    l_sb = P.sbuf(name + "_l", [1, qb], F32)
    linv = P.sbuf(name + "_linv", [1, qb], F32)
    ctx32 = P.sbuf(name + "_c32", [64, qb], F32)
    ws = WRing(P, name + "_ws", 9, [128, 3 * dhead])
    psP = PsRing(P, name + "_psP", 2, [128, qb])
    psS = PsRing(P, name + "_psS", 3, [128, qb])
    psC = PsRing(P, name + "_psC", 2, [128, qb])
    negc, negc_key = P.cconst(-ATTN_C)
    ptb = [P.sbuf(name + "_pt0", [128, qb], BF16),
           P.sbuf(name + "_pt1", [128, qb], BF16)]
    ones_col = v_tm[:].rearrange("p (i c) -> p i c", c=65)[:, :, 64:65]
    P.emit("dve", lambda E, a=ones_col: E.memset(a, 1.0),
           writes=[(name, "vtm_ones")])

    for h in range(nheads):
        hs = h * (head_stride if head_stride is not None else 3 * dhead)
        if qk_hbm is not None:
            Qh, Kh, qdep, kdep = qk_hbm
            P.dma("sp", lambda E, d=q_h.tile(0)[0:dhead, :],
                         s=Qh[h * dhead:(h + 1) * dhead, :]:
                      E.dma_start(out=d, in_=s),
                  reads=[qdep], writes=[q_h.key(0)])
            P.dma("sp", lambda E, d=k_h.tile(0)[0:dhead, :],
                         s=Kh[h * dhead:(h + 1) * dhead, :]:
                      E.dma_start(out=d, in_=s),
                  reads=[kdep], writes=[k_h.key(0)])
            def qap(a, b, q_h=q_h):
                return q_h.tile(0, a, b)[0:dhead, :]
            def kap(a, b, k_h=k_h):
                return k_h.tile(0, a, b)[0:dhead, :]
            qkey, kkey = q_h.key(0), k_h.key(0)
            projs = ((v_off, v_h),)
        elif qk_fm is None:
            projs = ((0, q_h), (1, k_h), (v_off, v_h))
            def qap(a, b, q_h=q_h):
                return q_h.tile(0, a, b)[0:dhead, :]
            def kap(a, b, k_h=k_h):
                return k_h.tile(0, a, b)[0:dhead, :]
            qkey, kkey = q_h.key(0), k_h.key(0)
        else:
            Qf, Kf = qk_fm
            ct0 = (h * dhead) // 128
            co0 = (h * dhead) % 128
            def qap(a, b, Qf=Qf, ct0=ct0, co0=co0):
                return Qf.tile(ct0, a, b)[co0:co0 + dhead, :]
            def kap(a, b, Kf=Kf, ct0=ct0, co0=co0):
                return Kf.tile(ct0, a, b)[co0:co0 + dhead, :]
            qkey, kkey = Qf.key(ct0), Kf.key(ct0)
            projs = ((v_off, v_h),)
        for j, dst in projs:
            for tb in range(0, T, qb):
                ps, pskey = psP.get()
                wsrc, wsrckey = wqkv_scr
                for k in range(8):
                    w_ap, wkey = ws.load_cached(
                        (wsrckey, h, j, k),
                        wsrc[k * 128:(k + 1) * 128, hs + j * dhead:
                             hs + (j + 1) * dhead], reads=[wsrckey])
                    P.emit("pe",
                           lambda E, ps=ps, w=w_ap, x=xn.tile(k, tb, tb + qb),
                                  st=(k == 0), sp=(k == 7):
                               E.matmul(ps[0:dhead, :], w, x, start=st, stop=sp),
                           reads=[wkey, xn.key(k)],
                           writes=[pskey] if k in (0, 7) else [])
                if bqkv is not None:
                    bsb, bbase, bkey = bqkv
                    col = (hs + j * dhead) // 128 + bbase
                    off = (hs + j * dhead) % 128
                    P.emit("dve",
                           lambda E, d=dst.tile(0, tb, tb + qb), s=ps,
                                  b=bsb[off:off + dhead, col:col + 1]:
                               E.tensor_scalar_add(d[0:dhead, :],
                                                   s[0:dhead, :], b),
                           reads=[pskey, bkey], writes=[dst.key(0)])
                else:
                    P.emit("act",
                           lambda E, d=dst.tile(0, tb, tb + qb), s=ps:
                               E.activation(d[0:dhead, :], s[0:dhead, :], AF.Copy),
                           reads=[pskey], writes=[dst.key(0)])
        for i in range(32):
            ps, pskey = psS.get()
            psbf = ps[:, 0:32].bitcast(BF16)
            P.emit("pe",
                   lambda E, ps=psbf, s=v_h.tile(0, i * 128, (i + 1) * 128):
                       E.transpose(ps, s[0:64, :], consts["ident"][0:64, 0:64]),
                   reads=[v_h.key(0), ("const", "ident")], writes=[pskey])
            P.emit("act",
                   lambda E, d=v_tm[:, i * 65:i * 65 + 64], s=psbf:
                       E.activation(d, s, AF.Copy),
                   reads=[pskey], writes=[(name, "vtm", i)])
        for qbi in range(T // qb):
            nkt = (qbi + 1) * (qb // 128)
            ctxps, ctxkey = psC.get()
            smm = {}

            def score(kt, qbi=qbi, smm=smm):
                ps, pskey = psS.get()
                smm[kt] = (ps, pskey)
                P.emit("pe",
                       lambda E, ps=ps, w=kap(kt * 128, (kt + 1) * 128),
                              x=qap(qbi * qb, (qbi + 1) * qb):
                           E.matmul(ps[:], w, x, start=True, stop=True),
                       reads=[kkey, qkey], writes=[pskey])

            def ctx(kt, qbi=qbi, nkt=nkt, ctxps=ctxps, ctxkey=ctxkey,
                    smm=smm):
                ps, pskey = smm.pop(kt)
                pt = ptb[kt % 2]
                ptkey = (name, "pt", kt % 2)
                P.emit("act",
                       lambda E, d=pt[:], s=ps: E.activation(
                           d, s[:], AF.Exp, bias=negc, scale=0.125),
                       reads=[pskey, negc_key], writes=[ptkey])
                j = kt - 4 * qbi
                if j >= 0:
                    P.emit("dve",
                           lambda E, d=pt[:],
                                  m=consts["tri"][:, j * 512:(j + 1) * 512]:
                               E.tensor_mul(d, d, m),
                           reads=[ptkey, ("const", "tri")], writes=[ptkey])
                P.emit("pe",
                       lambda E, ps=ctxps, w=v_tm[:, kt * 65:(kt + 1) * 65],
                              x=pt[:], st=(kt == 0), sp=(kt == nkt - 1):
                           E.matmul(ps[0:65, :], w, x, start=st, stop=sp),
                       reads=[(name, "vtm", kt), (name, "vtm_ones"), ptkey],
                       writes=[ctxkey] if kt in (0, nkt - 1) else [])

            for kt in range(nkt + 1):
                if kt < nkt:
                    score(kt)
                if kt >= 1:
                    ctx(kt - 1)
            P.emit("act", lambda E, s=ctxps: E.activation(
                       l_sb[:], s[64:65, :], AF.Ln),
                   reads=[ctxkey], writes=[(name, "l")])
            P.emit("act", lambda E: E.activation(linv[:], l_sb[:], AF.Exp,
                                                 scale=-1.0),
                   reads=[(name, "l")], writes=[(name, "linv")])
            P.emit("act", lambda E, s=ctxps: E.activation(
                       ctx32[:], s[0:64, :], AF.Copy),
                   reads=[ctxkey], writes=[(name, "c32")])
            ps, pskey = psS.get()
            P.emit("pe", lambda E, ps=ps: E.matmul(
                       ps[0:64, :], consts["ones_f32"][0:1, 0:64], linv[:],
                       start=True, stop=True),
                   reads=[(name, "linv"), ("const", "ones_f32")],
                   writes=[pskey])
            ct = h // 2
            coff = (h % 2) * 64
            P.emit("dve",
                   lambda E, d=ctx_fm.tile(ct, qbi * qb, (qbi + 1) * qb),
                          s=ps, coff=coff:
                       E.tensor_mul(d[coff:coff + 64, :], ctx32[:], s[0:64, :]),
                   reads=[pskey, (name, "c32")], writes=[ctx_fm.key(ct)])



def emit_router_l1(P, name, xn, rw_bf, consts, ind_fm, w32):
    rw_sb = P.sbuf(name + "_rw", [128, 8 * 16], BF16)
    rw_ap, rw_key = rw_bf
    for k in range(8):
        P.dma("sp", lambda E, d=rw_sb[:, k * 16:(k + 1) * 16],
                     s=rw_ap[k * 128:(k + 1) * 128, :]: E.dma_start(out=d, in_=s),
              reads=[rw_key], writes=[(name, "rw", k)])
    ltm = P.sbuf(name + "_ltm", [128, 16], F32)
    m = P.sbuf(name + "_m", [128, 1], F32)
    w_tm = P.sbuf(name + "_wtm", [128, 32], F32)
    ind_bf = P.sbuf(name + "_indbf", [128, 16], BF16)
    psR = PsRing(P, name + "_psR", 2, [128, 16])
    psTi = PsRing(P, name + "_psTi", 2, [16, 128], BF16)
    for tt in range(32):
        ps, pskey = psR.get()
        for k in range(8):
            P.emit("pe",
                   lambda E, ps=ps, w=xn.tile(k, tt * 128, (tt + 1) * 128),
                          x=rw_sb[:, k * 16:(k + 1) * 16], st=(k == 0), sp=(k == 7):
                       E.matmul(ps[:], w, x, start=st, stop=sp),
                   reads=[xn.key(k), (name, "rw", k)],
                   writes=[pskey] if k in (0, 7) else [])
        P.emit("act", lambda E, s=ps: E.activation(ltm[:], s[:], AF.Copy),
               reads=[pskey], writes=[(name, "ltm")])
        P.emit("dve", lambda E: E.reduce_max(m[:], ltm[:], AX.X),
               reads=[(name, "ltm")], writes=[(name, "m")])
        P.emit("act", lambda E, d=w_tm[:, tt:tt + 1]: E.activation(
                   d, m[:], AF.Sigmoid),
               reads=[(name, "m")], writes=[(name, "wtm")])
        P.emit("dve", lambda E: E.tensor_single_scalar(
                   ind_bf[:], ltm[:], m[:, 0:1], OP.is_ge),
               reads=[(name, "ltm"), (name, "m")], writes=[(name, "indbf")])
        psT, psTkey = psTi.get()
        P.emit("pe", lambda E, ps=psT: E.transpose(
                   ps[:], ind_bf[:], consts["ident"][:]),
               reads=[(name, "indbf"), ("const", "ident")], writes=[psTkey])
        P.emit("act", lambda E, d=ind_fm[0:16, tt * 128:(tt + 1) * 128], s=psT:
                   E.activation(d, s[:], AF.Copy),
               reads=[psTkey], writes=[(name, "ind_fm")])
    psWT = P.psum(name + "_psWT", [32, 128], F32)
    P.emit("pe", lambda E: E.transpose(psWT[:], w_tm[:],
                                       consts["ident_f32"][:]),
           reads=[(name, "wtm"), ("const", "ident_f32")],
           writes=[(name, "psWT")])
    P.emit("act", lambda E: E.activation(w32[:], psWT[:], AF.Copy),
           reads=[(name, "psWT")], writes=[(name, "w32")])


def emit_moe_l1(P, name, xn, w1_bf, w2_bf, b1_sb, b1_key, b2_sb, b2_key,
                ind_fm, w32, consts, acc, w_rep, TMAC=1024):
    hh = FMT(P, name + "_hh", 2048, TMAC, BF16)
    mask = P.sbuf(name + "_mask", [128, TMAC], BF16)
    ind32 = P.sbuf(name + "_ind32", [16, TMAC], F32)
    ws = WRing(P, name + "_ws", 18, [128, 1024])
    psA = PsRing(P, name + "_psA", 2, [128, TMAC])
    psB = PsRing(P, name + "_psB", 2, [128, 512])
    psM = PsRing(P, name + "_psM", 2, [128, 512])

    def run(finish):
        for tm in range(0, T, TMAC):
            P.emit("dve", lambda E: E.memset(acc.sb[:], 0.0),
                   writes=[acc.key(k) for k in range(acc.nk)])
            P.emit("act", lambda E, s=ind_fm[0:16, tm:tm + TMAC]:
                       E.activation(ind32[:], s, AF.Copy),
                   reads=[("l1r", "ind_fm")], writes=[(name, "ind32")])
            for j in range(TMAC // 128):
                tt = (tm // 128) + j
                ps, pskey = psM.get()
                P.emit("pe", lambda E, ps=ps,
                              s=consts["sel32"][:, tt * 128:(tt + 1) * 128]:
                           E.matmul(ps[:, 0:128], s, w32[:],
                                    start=True, stop=True),
                       reads=[("l1r", "w32"), ("const", "sel32")],
                       writes=[pskey])
                P.emit("act", lambda E, d=w_rep[:, j * 128:(j + 1) * 128], s=ps:
                           E.activation(d, s[:, 0:128], AF.Copy),
                       reads=[pskey], writes=[(name, "w_rep")])
            for e in range(16):
                for half in range(TMAC // 512):
                    ps, pskey = psM.get()
                    P.emit("pe",
                           lambda E, ps=ps,
                                  s=consts["sel"][:, e * 128:(e + 1) * 128],
                                  x=ind_fm[0:16, tm + half * 512:tm + (half + 1) * 512]:
                               E.matmul(ps[:], s, x, start=True, stop=True),
                           reads=[("l1r", "ind_fm"), ("const", "sel")],
                           writes=[pskey])
                    P.emit("act",
                           lambda E, d=mask[:, half * 512:(half + 1) * 512], s=ps:
                               E.activation(d, s[:], AF.Copy),
                           reads=[pskey], writes=[(name, "mask")])
                def epiA(ps, pskey, fm, ta, tb, e=e):
                    P.emit("act",
                           lambda E, d=hh.tile(fm), s=ps,
                                  b=b1_sb[:, e * 16 + fm:e * 16 + fm + 1]:
                               E.activation(d, s[:], AF.Gelu, bias=b),
                           reads=[pskey, b1_key], writes=[hh.key(fm)])
                    P.emit("dve",
                           lambda E, d=hh.tile(fm): E.tensor_mul(d, d, mask[:]),
                           reads=[hh.key(fm), (name, "mask")],
                           writes=[hh.key(fm)])
                emit_gemm(P, name + "A", lambda k, a, b: xn.tile(k, a, b),
                          xn.key, 8,
                          (w1_bf[0][e * 1024:(e + 1) * 1024, :], w1_bf[1]),
                          2048, ws, psA, epiA, t0=tm, t1=tm + TMAC, TMAC=TMAC)
                def extraB(ps, pskey, fm, ta, tb, tm=tm, e=e):
                    if e != 0:
                        return
                    P.emit("pe",
                           lambda E, ps=ps, w=b2_sb[:, fm * 128:(fm + 1) * 128],
                                  x=ind32[0:16, ta - tm:tb - tm]:
                               E.matmul(ps[:, 0:tb - ta], w, x,
                                        start=True, stop=False),
                           reads=[b2_key, (name, "ind32")], writes=[pskey])

                def epiB(ps, pskey, fm, ta, tb):
                    P.emit("dve",
                           lambda E, d=acc.tile(fm, ta - tm, tb - tm), s=ps:
                               E.tensor_add(d, d, s[:, 0:tb - ta]),
                           reads=[pskey, acc.key(fm)], writes=[acc.key(fm)])
                emit_gemm(P, name + "B",
                          lambda k, a, b: hh.tile(k, a - tm, b - tm), hh.key,
                          16,
                          (w2_bf[0][e * 2048:(e + 1) * 2048, :], w2_bf[1]),
                          1024, ws, psB, epiB, t0=tm, t1=tm + TMAC, TMAC=512,
                          extra_mm=None if e != 0 else extraB)
            finish(tm)
    return run


def emit_layer1_pre(P):
    f = lambda n, shape: P.dram(n, shape, F32, kind="ExternalInput")
    pre = {}
    pre["W1_bf"] = emit_bf16_scratch(P, "W1s", f("W1", (D, 3 * D)), D, 3 * D)
    pre["W2_bf"] = emit_bf16_scratch(P, "W2s", f("W2", (D, D)), D, D)
    pre["rw_bf"] = emit_bf16_scratch(P, "rws", f("rw", (D, 16)), D, 16)
    pre["w1_bf"] = emit_bf16_scratch(P, "w1s", f("w1", (16 * D, 2048)),
                                     16 * D, 2048)
    pre["w2_bf"] = emit_bf16_scratch(P, "w2s", f("w2", (16 * 2048, D)),
                                     16 * 2048, D)
    return pre


def emit_layer1(P, x_h, xo_h, consts, pre, debug=False):
    nc = P.nc
    f = lambda n, shape: P.dram(n, shape, F32, kind="ExternalInput")
    b1a_h = f("b1a_h", (128, 24))
    b2a_h = f("b2a_h", (128, 8))
    b1m_h = f("b1m_h", (128, 256))
    b2m_h = f("b2m_h", (16, D))
    s_h = P.dram("s_park", (D, T), F32,
                 kind="ExternalOutput" if debug else "Internal")
    if debug:
        dbg_ind = P.dram("dbg_ind", (16, T), BF16, kind="ExternalOutput")
        dbg_w = P.dram("dbg_w", (32, 128), F32, kind="ExternalOutput")
    W1_bf, W2_bf, rw_bf, w1_bf, w2_bf = (pre["W1_bf"], pre["W2_bf"],
                                         pre["rw_bf"], pre["w1_bf"],
                                         pre["w2_bf"])

    b1a_sb = P.sbuf("b1a", [128, 24], F32)
    P.dma("sp", lambda E: E.dma_start(out=b1a_sb[:], in_=b1a_h),
          writes=[("b1a",)])
    b2a_sb = P.sbuf("b2a", [128, 8], F32)
    P.dma("sp", lambda E: E.dma_start(out=b2a_sb[:], in_=b2a_h),
          writes=[("b2a",)])
    b1m_sb = P.sbuf("b1m", [128, 256], F32)
    P.dma("sp", lambda E: E.dma_start(out=b1m_sb[:], in_=b1m_h),
          writes=[("b1m",)])
    b2m_sb = P.sbuf("b2m", [16, D], F32)
    P.dma("sp", lambda E: E.dma_start(out=b2m_sb[:], in_=b2m_h),
          writes=[("b2m",)])

    xn = FMT(P, "xn", D, T, BF16)
    ind_fm = P.sbuf("ind_fm", [16, T], BF16)
    w32 = P.sbuf("w32g", [32, 128], F32)

    P.push_region()
    emit_ln(P, "ln", x_h, xn.sb, consts)
    P.pop_region()

    P.push_region()
    emit_router_l1(P, "l1r", xn, rw_bf, consts, ind_fm, w32)
    P.pop_region()
    if debug:
        P.dma("sp", lambda E: E.dma_start(out=dbg_ind, in_=ind_fm[:]),
              reads=[("l1r", "ind_fm")])
        P.dma("sp", lambda E: E.dma_start(out=dbg_w, in_=w32[:]),
              reads=[("l1r", "w32")])

    P.push_region()
    ctx_fm = FMT(P, "ctx", D, T, BF16)
    P.push_region()
    emit_attention(P, "att", xn, W1_bf, (b1a_sb, 0, ("b1a",)), consts, ctx_fm,
                   nheads=16, dhead=64, head_stride=192)
    P.pop_region()
    s32 = P.sbuf("s32", [128, 512], F32)
    wsO = WRing(P, "wsO", 9, [128, 512])
    psO = PsRing(P, "psO", 2, [128, 512])

    def epiO(ps, pskey, fm, ta, tb):
        P.emit("dve",
               lambda E, s=ps, b=b2a_sb[:, fm:fm + 1]:
                   E.tensor_scalar_add(s32[:], s[:], b),
               reads=[pskey, ("b2a",)], writes=[("s32",)])
        P.dma("sp", lambda E, d=s_h[fm * 128:(fm + 1) * 128, ta:tb]:
                  E.dma_start(out=d, in_=s32[:]),
              reads=[("s32",)])
    emit_gemm(P, "outp", lambda k, a, b: ctx_fm.tile(k, a, b), ctx_fm.key,
              8, W2_bf, D, wsO, psO, epiO, TMAC=512)
    P.pop_region()

    P.push_region()
    load_sel32(P, consts)
    TMAC = 512
    acc = FMT(P, "acc", D, TMAC, F32)
    w_rep = P.sbuf("w_rep", [128, TMAC], F32)
    xc = P.sbuf("xc", [128, 512], F32)
    sc = P.sbuf("sc", [128, 512], F32)
    gc = P.sbuf("gc", [128, 512], F32)

    def finish(tm):
        for fm in range(8):
            for c0 in range(tm, tm + TMAC, 512):
                P.dma("sp", lambda E, d=xc[:],
                             s=x_h[fm * 128:(fm + 1) * 128, c0:c0 + 512]:
                          E.dma_start(out=d, in_=s), writes=[("xc",)])
                P.dma("sp", lambda E, d=sc[:],
                             s=s_h[fm * 128:(fm + 1) * 128, c0:c0 + 512]:
                          E.dma_start(out=d, in_=s), writes=[("sc",)])
                P.emit("dve",
                       lambda E, a=acc.tile(fm, c0 - tm, c0 - tm + 512),
                              w=w_rep[:, c0 - tm:c0 - tm + 512]:
                           E.tensor_mul(gc[:], a, w),
                       reads=[acc.key(fm), ("moe", "w_rep")], writes=[("gc",)])
                P.emit("act", lambda E: E.activation(gc[:], gc[:], AF.Silu),
                       reads=[("gc",)], writes=[("gc",)])
                P.emit("dve", lambda E: E.tensor_mul(gc[:], gc[:], sc[:]),
                       reads=[("gc",), ("sc",)], writes=[("gc",)])
                P.emit("dve", lambda E: E.tensor_add(gc[:], gc[:], xc[:]),
                       reads=[("gc",), ("xc",)], writes=[("gc",)])
                P.dma("sp", lambda E, d=xo_h[fm * 128:(fm + 1) * 128, c0:c0 + 512]:
                          E.dma_start(out=d, in_=gc[:]),
                      reads=[("gc",)], writes=[("xo", fm, c0)])
    run = emit_moe_l1(P, "moe", xn, w1_bf, w2_bf, b1m_sb, ("b1m",),
                      b2m_sb, ("b2m",), ind_fm, w32, consts, acc, w_rep,
                      TMAC=TMAC)
    run(finish)
    P.pop_region()


def host_prep_layer1(inputs):
    c = host_constants()
    base = dict(
        W1=np.ascontiguousarray(inputs["L1_attn_W1"]),
        b1a_h=np.ascontiguousarray(inputs["L1_attn_b1"].reshape(24, 128).T),
        W2=np.ascontiguousarray(inputs["L1_attn_W2"]),
        b2a_h=np.ascontiguousarray(inputs["L1_attn_b2"].reshape(8, 128).T),
        rw=np.ascontiguousarray(inputs["L1_ffn_router_w"]),
        w1=np.ascontiguousarray(inputs["L1_ffn_w1"].reshape(16 * 1024, 2048)),
        b1m_h=np.ascontiguousarray(
            inputs["L1_ffn_b1"].reshape(16, 16, 128).transpose(2, 0, 1)
            .reshape(128, 256)),
        w2=np.ascontiguousarray(inputs["L1_ffn_w2"].reshape(16 * 2048, 1024)),
        b2m_h=np.ascontiguousarray(inputs["L1_ffn_b2"]),
        **c)
    return base



def emit_rope_qk(P, name, xn, w_scr, b_sb, b_key, out_hbm, sincos, consts,
                 TBR=512):
    sin_h, cos_h = sincos
    ws = WRing(P, name + "_ws", 3, [128, 128])
    ps = PsRing(P, name + "_ps", 2, [128, TBR])
    raw = P.sbuf(name + "_raw", [128, 8 * TBR], F32)
    trig = P.sbuf(name + "_trig", [128, 2 * 4 * TBR], F32)
    m1 = P.sbuf(name + "_m1", [128, TBR], F32)
    m2 = P.sbuf(name + "_m2", [128, TBR], F32)
    rot = P.sbuf(name + "_rot", [128, 8 * TBR], BF16)
    for c0 in range(0, T, TBR):
        P.dma("sp",
              [(lambda E, d=trig[:, (j * 4 + i) * TBR:(j * 4 + i + 1) * TBR],
                       s=(sin_h if j == 0 else cos_h)[i * 128:(i + 1) * 128,
                                                      c0:c0 + TBR]:
                    E.dma_start(out=d, in_=s))
               for j in range(2) for i in range(4)],
              writes=[(name, "trig")])
        def epiP(psap, pskey, fm, ta, tb):
            P.emit("dve",
                   lambda E, d=raw[:, fm * TBR:(fm + 1) * TBR], s=psap,
                          b=b_sb[:, fm:fm + 1]:
                       E.tensor_scalar_add(d, s[:], b),
                   reads=[pskey, b_key], writes=[(name, "raw", fm)])
        emit_gemm(P, name + "P", lambda k, a, b: xn.tile(k, a, b), xn.key,
                  8, w_scr, 1024, ws, ps, epiP, t0=c0, t1=c0 + TBR, TMAC=TBR)
        for i in range(4):
            e_ap = raw[:, i * TBR:(i + 1) * TBR]
            o_ap = raw[:, (i + 4) * TBR:(i + 5) * TBR]
            s_ap = trig[:, i * TBR:(i + 1) * TBR]
            c_ap = trig[:, (4 + i) * TBR:(5 + i) * TBR]
            t1k, t2k = (name, "m1"), (name, "m2")
            P.emit("dve", lambda E, a=e_ap, b=c_ap: E.tensor_mul(m1[:], a, b),
                   reads=[(name, "raw", i), (name, "trig")], writes=[t1k])
            P.emit("pool", lambda E, a=o_ap, b=s_ap: E.tensor_mul(m2[:], a, b),
                   reads=[(name, "raw", i + 4), (name, "trig")], writes=[t2k])
            P.emit("dve",
                   lambda E, d=rot[:, i * TBR:(i + 1) * TBR]:
                       E.tensor_sub(d, m1[:], m2[:]),
                   reads=[t1k, t2k], writes=[(name, "rot", i)])
            P.emit("dve", lambda E, a=e_ap, b=s_ap: E.tensor_mul(m1[:], a, b),
                   reads=[(name, "raw", i), (name, "trig"), t1k], writes=[t1k])
            P.emit("pool", lambda E, a=o_ap, b=c_ap: E.tensor_mul(m2[:], a, b),
                   reads=[(name, "raw", i + 4), (name, "trig"), t2k],
                   writes=[t2k])
            P.emit("dve",
                   lambda E, d=rot[:, (i + 4) * TBR:(i + 5) * TBR]:
                       E.tensor_sub(d, m1[:], m2[:]),
                   reads=[t1k, t2k], writes=[(name, "rot", i + 4)])
        P.dma("sp",
              [(lambda E, d=out_hbm[i * 128:(i + 1) * 128, c0:c0 + TBR],
                       s=rot[:, i * TBR:(i + 1) * TBR]:
                    E.dma_start(out=d, in_=s)) for i in range(8)],
              reads=[(name, "rot", i) for i in range(8)],
              writes=[(name, "park")])


def emit_router_l2(P, name, xn, wg_bf, we_bf, gb_h, eb_h, consts, wse_fm):
    wall = P.sbuf(name + "_wall", [128, 10 * 8], BF16)
    for k in range(8):
        P.dma("sp", [
            (lambda E, d=wall[:, k * 10:k * 10 + 2],
                    s=wg_bf[0][k * 128:(k + 1) * 128, :]:
                 E.dma_start(out=d, in_=s)),
            (lambda E, d=wall[:, k * 10 + 2:k * 10 + 10],
                    s=we_bf[0][k * 128:(k + 1) * 128, :]:
                 E.dma_start(out=d, in_=s))], reads=[wg_bf[1], we_bf[1]],
            writes=[(name, "wall", k)])
    gbeb = P.sbuf(name + "_gbeb", [1, 10], F32)
    P.dma("sp", [lambda E: E.dma_start(out=gbeb[0:1, 0:2], in_=gb_h),
                 lambda E: E.dma_start(out=gbeb[0:1, 2:10], in_=eb_h)],
          writes=[(name, "gbeb")])
    gbeb_rep = P.sbuf(name + "_gbebr", [128, 10], F32)
    psb = P.psum(name + "_psb", [128, 16])
    P.emit("pe", lambda E: E.matmul(psb[:, 0:10], consts["ones_f32"][0:1, :],
                                    gbeb[:], start=True, stop=True),
           reads=[(name, "gbeb"), ("const", "ones_f32")],
           writes=[(name, "psb")])
    P.emit("act", lambda E: E.activation(gbeb_rep[:], psb[:, 0:10], AF.Copy),
           reads=[(name, "psb")], writes=[(name, "gbebr")])
    gl = P.sbuf(name + "_gl", [128, 2], F32)
    el = P.sbuf(name + "_el", [128, 8], F32)
    mg = P.sbuf(name + "_mg", [128, 1], F32)
    tmp8 = P.sbuf(name + "_tmp8", [128, 8], F32)
    tmp8b = P.sbuf(name + "_t8b", [128, 8], F32)
    sc1 = P.sbuf(name + "_sc1", [128, 1], F32)
    wse_bf = P.sbuf(name + "_wsebf", [128, 8], BF16)
    psR = PsRing(P, name + "_psR", 2, [128, 16])
    psT = PsRing(P, name + "_psT", 2, [8, 128], BF16)
    for tt in range(32):
        ps, pskey = psR.get()
        for k in range(8):
            P.emit("pe",
                   lambda E, ps=ps, w=xn.tile(k, tt * 128, (tt + 1) * 128),
                          x=wall[:, k * 10:(k + 1) * 10], st=(k == 0),
                          sp=(k == 7):
                       E.matmul(ps[:, 0:10], w, x, start=st, stop=sp),
                   reads=[xn.key(k), (name, "wall", k)],
                   writes=[pskey] if k in (0, 7) else [])
        P.emit("dve", lambda E, s=ps: E.tensor_add(
                   gl[:], s[:, 0:2], gbeb_rep[:, 0:2]),
               reads=[pskey, (name, "gbebr")], writes=[(name, "gl")])
        P.emit("dve", lambda E, s=ps: E.tensor_add(
                   el[:], s[:, 2:10], gbeb_rep[:, 2:10]),
               reads=[pskey, (name, "gbebr")], writes=[(name, "el")])
        P.emit("dve", lambda E: E.reduce_max(mg[:], gl[:], AX.X),
               reads=[(name, "gl")], writes=[(name, "mg")])
        P.emit("dve", lambda E: E.tensor_scalar(
                   gl[:], gl[:], mg[:, 0:1], None, OP.subtract),
               reads=[(name, "gl"), (name, "mg")], writes=[(name, "gl")])
        P.emit("act", lambda E: E.activation(gl[:], gl[:], AF.Exp,
                                             accum_out=sc1[:, 0:1]),
               reads=[(name, "gl")], writes=[(name, "gl"), (name, "sc1")])
        P.emit("act", lambda E: E.activation(sc1[:], sc1[:], AF.Ln),
               reads=[(name, "sc1")], writes=[(name, "sc1")])
        P.emit("act", lambda E: E.activation(sc1[:], sc1[:], AF.Exp,
                                             scale=-1.0),
               reads=[(name, "sc1")], writes=[(name, "sc1")])
        P.emit("dve", lambda E: E.tensor_single_scalar(
                   gl[:], gl[:], 1.0, OP.is_ge),
               reads=[(name, "gl")], writes=[(name, "gl")])
        P.emit("dve", lambda E: E.reduce_max(mg[:], el[:], AX.X),
               reads=[(name, "el")], writes=[(name, "mg")])
        P.emit("dve", lambda E: E.tensor_scalar(
                   tmp8[:], el[:], mg[:, 0:1], None, OP.subtract),
               reads=[(name, "el"), (name, "mg")], writes=[(name, "tmp8")])
        P.emit("act", lambda E: E.activation(tmp8[:], tmp8[:], AF.Exp),
               reads=[(name, "tmp8")], writes=[(name, "tmp8")])
        P.emit("dve", lambda E,
                      m3=gl[:].unsqueeze(2).broadcast_to([128, 2, 4]),
                      o3=tmp8[:].rearrange("p (g j) -> p g j", g=2):
                   E.tensor_tensor(o3, o3, m3, OP.mult),
               reads=[(name, "tmp8"), (name, "gl")], writes=[(name, "tmp8")])
        P.emit("dve", lambda E: E.reduce_sum(mg[:].unsqueeze(2),
                                             tmp8[:].unsqueeze(1), AX.X),
               reads=[(name, "tmp8")], writes=[(name, "mg")])
        P.emit("act", lambda E: E.activation(mg[:], mg[:], AF.Ln),
               reads=[(name, "mg")], writes=[(name, "mg")])
        P.emit("act", lambda E: E.activation(mg[:], mg[:], AF.Exp, scale=-1.0),
               reads=[(name, "mg")], writes=[(name, "mg")])
        P.emit("dve", lambda E: E.tensor_mul(sc1[:], sc1[:], mg[:]),
               reads=[(name, "sc1"), (name, "mg")], writes=[(name, "sc1")])
        P.emit("dve", lambda E: E.tensor_scalar(
                   tmp8[:], tmp8[:], sc1[:, 0:1], None, OP.mult),
               reads=[(name, "tmp8"), (name, "sc1")], writes=[(name, "tmp8")])
        P.emit("dve", lambda E: E.reduce_max(mg[:], tmp8[:], AX.X),
               reads=[(name, "tmp8")], writes=[(name, "mg")])
        P.emit("dve", lambda E: E.tensor_single_scalar(
                   tmp8b[:], tmp8[:], mg[:, 0:1], OP.is_lt),
               reads=[(name, "tmp8"), (name, "mg")], writes=[(name, "t8b")])
        P.emit("dve", lambda E: E.tensor_mul(tmp8b[:], tmp8b[:], tmp8[:]),
               reads=[(name, "t8b"), (name, "tmp8")], writes=[(name, "t8b")])
        P.emit("dve", lambda E: E.reduce_max(mg[:], tmp8b[:], AX.X),
               reads=[(name, "t8b")], writes=[(name, "mg")])
        P.emit("dve", lambda E: E.tensor_single_scalar(
                   tmp8b[:], tmp8[:], mg[:, 0:1], OP.is_ge),
               reads=[(name, "tmp8"), (name, "mg")], writes=[(name, "t8b")])
        P.emit("dve", lambda E: E.tensor_mul(tmp8b[:], tmp8b[:], tmp8[:]),
               reads=[(name, "t8b"), (name, "tmp8")], writes=[(name, "t8b")])
        P.emit("dve", lambda E: E.tensor_copy(wse_bf[:], tmp8b[:]),
               reads=[(name, "t8b")], writes=[(name, "wsebf")])
        psTp, psTkey = psT.get()
        P.emit("pe", lambda E, ps=psTp: E.transpose(
                   ps[:], wse_bf[:], consts["ident"][:]),
               reads=[(name, "wsebf"), ("const", "ident")], writes=[psTkey])
        P.emit("act", lambda E, d=wse_fm[0:8, tt * 128:(tt + 1) * 128], s=psTp:
                   E.activation(d, s[:], AF.Copy),
               reads=[psTkey], writes=[(name, "wse_fm")])


def emit_layer2_pre(P):
    f = lambda n, shape: P.dram(n, shape, F32, kind="ExternalInput")
    pre = {}
    pre["Wq_bf"] = emit_bf16_scratch(P, "Wqs", f("Wq", (D, D)), D, D)
    pre["Wk_bf"] = emit_bf16_scratch(P, "Wks", f("Wk", (D, D)), D, D)
    pre["Wv_bf"] = emit_bf16_scratch(P, "Wvs", f("Wv", (D, D)), D, D)
    pre["Wo_bf"] = emit_bf16_scratch(P, "Wos", f("Wo", (D, D)), D, D)
    pre["w_in_bf"] = emit_bf16_scratch(P, "wins", f("w_in", (D, 4096)),
                                       D, 4096)
    pre["w_out_bf"] = emit_bf16_scratch(P, "wouts", f("w_out", (2048, D)),
                                        2048, D)
    pre["w1_bf"] = emit_bf16_scratch(P, "w1s", f("w1", (D, 4096)), D, 4096)
    pre["w2_bf"] = emit_bf16_scratch(P, "w2s", f("w2", (8 * 2048, D)),
                                     8 * 2048, D)
    pre["wg_bf"] = emit_bf16_scratch(P, "wgs", f("wg", (D, 2)), D, 2)
    pre["we_bf"] = emit_bf16_scratch(P, "wes", f("we", (D, 8)), D, 8)
    return pre


def emit_layer2(P, x_h, xo_h, consts, pre, debug=False):
    nc = P.nc
    f = lambda n, shape: P.dram(n, shape, F32, kind="ExternalInput")
    bq_h = f("bq", (128, 8))
    bk_h = f("bk", (128, 8))
    bv_h = f("bv", (128, 8))
    bo_h = f("bo", (128, 8))
    gb_h = f("gb", (1, 2))
    eb_h = f("eb", (1, 8))
    sin_h = f("c_sin", (512, T))
    cos_h = f("c_cos", (512, T))
    s_h = P.dram("s_park", (D, T), F32,
                 kind="ExternalOutput" if debug else "Internal")
    if debug:
        dbg_wse = P.dram("dbg_wse", (8, T), BF16, kind="ExternalOutput")
    (Wq_bf, Wk_bf, Wv_bf, Wo_bf, w_in_bf, w_out_bf, w1_bf, w2_bf, wg_bf,
     we_bf) = (pre["Wq_bf"], pre["Wk_bf"], pre["Wv_bf"], pre["Wo_bf"],
               pre["w_in_bf"], pre["w_out_bf"], pre["w1_bf"], pre["w2_bf"],
               pre["wg_bf"], pre["we_bf"])

    bsb = {}
    for nm, hb in (("bq", bq_h), ("bk", bk_h), ("bv", bv_h), ("bo", bo_h)):
        bsb[nm] = P.sbuf("sb_" + nm, [128, 8], F32)
        P.dma("sp", lambda E, d=bsb[nm], s=hb: E.dma_start(out=d[:], in_=s),
              writes=[(nm,)])

    xn = FMT(P, "xn", D, T, BF16)
    wse_fm = P.sbuf("wse_fm", [8, T], BF16)

    P.push_region()
    emit_ln(P, "ln", x_h, xn.sb, consts)
    P.pop_region()

    P.push_region()
    emit_router_l2(P, "l2r", xn, wg_bf, we_bf, gb_h, eb_h, consts, wse_fm)
    P.pop_region()
    if debug:
        P.dma("sp", lambda E: E.dma_start(out=dbg_wse, in_=wse_fm[:]),
              reads=[("l2r", "wse_fm")])

    P.push_region()
    Qr_h = P.dram("Qr_park", (D, T), BF16)
    Kr_h = P.dram("Kr_park", (D, T), BF16)
    P.push_region()
    emit_rope_qk(P, "ropq", xn, Wq_bf, bsb["bq"], ("bq",), Qr_h,
                 (sin_h, cos_h), consts)
    P.pop_region()
    P.push_region()
    emit_rope_qk(P, "ropk", xn, Wk_bf, bsb["bk"], ("bk",), Kr_h,
                 (sin_h, cos_h), consts)
    P.pop_region()
    ctx_fm = FMT(P, "ctx", D, T, BF16)
    P.push_region()
    emit_attention(P, "att", xn, Wv_bf, (bsb["bv"], 0, ("bv",)), consts,
                   ctx_fm, nheads=16, dhead=64, head_stride=64,
                   qk_hbm=(Qr_h, Kr_h, ("ropq", "park"), ("ropk", "park")),
                   v_off=0)
    P.pop_region()
    s32 = P.sbuf("s32", [128, 512], F32)
    wsO = WRing(P, "wsO", 9, [128, 512])
    psO = PsRing(P, "psO", 2, [128, 512])

    def epiO(ps, pskey, fm, ta, tb):
        P.emit("dve",
               lambda E, s=ps, b=bsb["bo"][:, fm:fm + 1]:
                   E.tensor_scalar_add(s32[:], s[:], b),
               reads=[pskey, ("bo",)], writes=[("s32",)])
        P.dma("sp", lambda E, d=s_h[fm * 128:(fm + 1) * 128, ta:tb]:
                  E.dma_start(out=d, in_=s32[:]),
              reads=[("s32",)])
    emit_gemm(P, "outp", lambda k, a, b: ctx_fm.tile(k, a, b), ctx_fm.key,
              8, Wo_bf, D, wsO, psO, epiO, TMAC=512)
    P.pop_region()

    P.push_region()
    TMAC = 512
    acc = FMT(P, "acc", D, TMAC, F32)
    hsh = FMT(P, "hsh", 2048, TMAC, BF16)
    hrt = FMT(P, "hrt", 2048, TMAC, BF16)
    hmk = hsh
    mask = P.sbuf("l2mask", [128, TMAC], BF16)
    swi = P.sbuf("l2swi", [128, TMAC], F32)
    ws = WRing(P, "l2ws", 18, [128, 1024])
    psA = PsRing(P, "l2psA", 2, [128, TMAC])
    psB = PsRing(P, "l2psB", 2, [128, 512])
    psM = PsRing(P, "l2psM", 2, [128, 512])
    xc = P.sbuf("xc", [128, 512], F32)
    sc = P.sbuf("sc", [128, 512], F32)
    gc = P.sbuf("gc", [128, 512], F32)

    SWIGLU_ORDER = [v for i in range(16) for v in (i, i + 16)]

    def swiglu_epi(hdst):
        def epi(ps, pskey, fm, ta, tb, hdst=hdst):
            if fm < 16:
                P.emit("act", lambda E, s=ps,
                              d=swi[:, 0:tb - ta]:
                           E.activation(d, s[:, 0:tb - ta], AF.Silu),
                       reads=[pskey], writes=[("l2swi",)])
            else:
                P.emit("dve",
                       lambda E, s=ps, d=hdst.tile(fm - 16, 0, tb - ta),
                              a=swi[:, 0:tb - ta]:
                           E.tensor_mul(d, a, s[:, 0:tb - ta]),
                       reads=[pskey, ("l2swi",)],
                       writes=[hdst.key(fm - 16)])
        return epi

    def run_moe(finish):
        for tm in range(0, T, TMAC):
            P.emit("dve", lambda E: E.memset(acc.sb[:], 0.0),
                   writes=[acc.key(k) for k in range(acc.nk)])
            emit_gemm(P, "l2sh", lambda k, a, b: xn.tile(k, a, b), xn.key,
                      8, w_in_bf, 4096, ws, psA, swiglu_epi(hsh),
                      t0=tm, t1=tm + TMAC, TMAC=TMAC, fm_order=SWIGLU_ORDER)
            emit_gemm(P, "l2rt", lambda k, a, b: xn.tile(k, a, b), xn.key,
                      8, w1_bf, 4096, ws, psA, swiglu_epi(hrt),
                      t0=tm, t1=tm + TMAC, TMAC=TMAC, fm_order=SWIGLU_ORDER)
            def epiAcc(ps, pskey, fm, ta, tb):
                P.emit("dve",
                       lambda E, d=acc.tile(fm, ta - tm, tb - tm), s=ps:
                           E.tensor_add(d, d, s[:, 0:tb - ta]),
                       reads=[pskey, acc.key(fm)], writes=[acc.key(fm)])
            emit_gemm(P, "l2so", lambda k, a, b: hsh.tile(k, a - tm, b - tm),
                      hsh.key, 16, w_out_bf, D, ws, psB, epiAcc,
                      t0=tm, t1=tm + TMAC, TMAC=512)
            for e in range(8):
                for half in range(TMAC // 512):
                    ps, pskey = psM.get()
                    P.emit("pe",
                           lambda E, ps=ps,
                                  s=consts["sel"][0:8, e * 128:(e + 1) * 128],
                                  x=wse_fm[0:8, tm + half * 512:
                                           tm + (half + 1) * 512]:
                               E.matmul(ps[:], s, x, start=True, stop=True),
                           reads=[("l2r", "wse_fm"), ("const", "sel")],
                           writes=[pskey])
                    P.emit("act",
                           lambda E, d=mask[:, half * 512:(half + 1) * 512],
                                  s=ps:
                               E.activation(d, s[:], AF.Copy),
                           reads=[pskey], writes=[("l2mask",)])
                for fm in range(16):
                    P.emit("dve",
                           lambda E, d=hmk.tile(fm), s=hrt.tile(fm):
                               E.tensor_mul(d, s, mask[:]),
                           reads=[hrt.key(fm), ("l2mask",)],
                           writes=[hmk.key(fm)])
                emit_gemm(P, "l2e", lambda k, a, b: hmk.tile(k, a - tm, b - tm),
                          hmk.key, 16,
                          (w2_bf[0][e * 2048:(e + 1) * 2048, :], w2_bf[1]),
                          D, ws, psB, epiAcc, t0=tm, t1=tm + TMAC, TMAC=512)
            finish(tm)

    def finish(tm):
        for fm in range(8):
            for c0 in range(tm, tm + TMAC, 512):
                P.dma("sp", lambda E, d=xc[:],
                             s=x_h[fm * 128:(fm + 1) * 128, c0:c0 + 512]:
                          E.dma_start(out=d, in_=s), writes=[("xc",)])
                P.dma("sp", lambda E, d=sc[:],
                             s=s_h[fm * 128:(fm + 1) * 128, c0:c0 + 512]:
                          E.dma_start(out=d, in_=s), writes=[("sc",)])
                P.emit("act",
                       lambda E, a=acc.tile(fm, c0 - tm, c0 - tm + 512):
                           E.activation(gc[:], a, AF.Silu),
                       reads=[acc.key(fm)], writes=[("gc",)])
                P.emit("dve", lambda E: E.tensor_mul(gc[:], gc[:], sc[:]),
                       reads=[("gc",), ("sc",)], writes=[("gc",)])
                P.emit("dve", lambda E: E.tensor_add(gc[:], gc[:], xc[:]),
                       reads=[("gc",), ("xc",)], writes=[("gc",)])
                P.dma("sp", lambda E, d=xo_h[fm * 128:(fm + 1) * 128,
                                             c0:c0 + 512]:
                          E.dma_start(out=d, in_=gc[:]),
                      reads=[("gc",)])
    run_moe(finish)
    P.pop_region()


def host_prep_layer2(inputs):
    c = host_constants()
    perm = np.concatenate([np.arange(0, D, 2), np.arange(1, D, 2)])
    fr = np.arange(D // 2, dtype=np.float64)
    fc = np.power(10000.0, -2.0 * fr / D)
    pos = np.arange(T, dtype=np.float64)
    fr2 = np.outer(pos, fc)
    sin_t = np.sin(fr2).astype(np.float32).T.copy()
    cos_t = np.cos(fr2).astype(np.float32).T.copy()
    colsb = lambda v: np.ascontiguousarray(v.reshape(8, 128).T)
    base = dict(
        Wq=np.ascontiguousarray(inputs["L2_attn_Wq"][:, perm]),
        bq=colsb(inputs["L2_attn_bq"][perm]),
        Wk=np.ascontiguousarray(inputs["L2_attn_Wk"][:, perm]),
        bk=colsb(inputs["L2_attn_bk"][perm]),
        Wv=np.ascontiguousarray(inputs["L2_attn_Wv"]),
        bv=colsb(inputs["L2_attn_bv"]),
        Wo=np.ascontiguousarray(inputs["L2_attn_Wo"]),
        bo=colsb(inputs["L2_attn_bo"]),
        w_in=np.ascontiguousarray(inputs["L2_ffn_w_in"].T),
        w_out=np.ascontiguousarray(inputs["L2_ffn_w_out"].T),
        w1=np.ascontiguousarray(inputs["L2_ffn_w1"].T),
        w2=np.ascontiguousarray(
            inputs["L2_ffn_w2"].transpose(0, 2, 1).reshape(8 * 2048, D)),
        wg=np.ascontiguousarray(inputs["L2_ffn_wg"].T),
        we=np.ascontiguousarray(inputs["L2_ffn_we"].T),
        gb=np.ascontiguousarray(inputs["L2_ffn_gb"].reshape(1, 2)),
        eb=np.ascontiguousarray(inputs["L2_ffn_eb"].reshape(1, 8)),
        c_sin=sin_t, c_cos=cos_t,
        **c)
    return base



def make_hbm_xtile(P, name, hbm_ap, dep, nslots=10, W=512, dtype=BF16):
    ring = WRing(P, name + "_xr", nslots, [128, W], dtype)
    cache = {}
    lastk = {}

    def xtile(k, a, b):
        key = (k, a, b)
        if key in cache:
            lastk[k] = cache[key][1]
            return cache[key][0]
        ap, rkey, _slot = ring.load(hbm_ap[k * 128:(k + 1) * 128, a:b], reads=[dep])
        for kk, (aap, rk) in list(cache.items()):
            if rk == rkey:
                del cache[kk]
        cache[key] = (ap, rkey)
        lastk[k] = rkey
        return ap

    def xkey(k):
        return lastk.get(k)

    return xtile, xkey


def emit_rms(P, name, src_hbm, src_dep, w_sb, w_key, out_fm, consts,
             eps=1e-6, TB=512, out_hbm=None, out32_hbm=None):
    xb = P.sbuf(name + "_xb", [128, 8 * TB], F32)
    sq = P.sbuf(name + "_sq", [128, TB], F32)
    rstd = P.sbuf(name + "_rstd", [128, TB], F32)
    ob = P.sbuf(name + "_ob", [128, TB], F32) if (out_hbm or out32_hbm) else None
    obb = P.sbuf(name + "_obb", [128, TB], BF16) if out_hbm else None
    ps = P.psum(name + "_ps", [128, TB], F32)
    eps_ap, eps_key = P.cconst(eps)
    ones = consts["ones_f32"]
    for tb in range(T // TB):
        P.dma("sp",
              [(lambda E, d=xb[:, k * TB:(k + 1) * TB],
                       s=src_hbm[k * 128:(k + 1) * 128, tb * TB:(tb + 1) * TB]:
                    E.dma_start(out=d, in_=s)) for k in range(8)],
              reads=[src_dep], writes=[(name, "xb")])
        for k in range(8):
            P.emit("act", lambda E, s=xb[:, k * TB:(k + 1) * TB]:
                       E.activation(sq[:], s, AF.Square),
                   reads=[(name, "xb")], writes=[(name, "sq")])
            P.emit("pe", lambda E, st=(k == 0), sp=(k == 7):
                       E.matmul(ps[:], ones[:], sq[:],
                                start=st, stop=sp),
                   reads=[(name, "sq"), ("const", "ones_f32")],
                   writes=[(name, "ps")] if k in (0, 7) else [])
        P.emit("act", lambda E: E.activation(rstd[:], ps[:], AF.Ln,
                                             scale=1.0 / D, bias=eps_ap),
               reads=[(name, "ps"), eps_key], writes=[(name, "rstd")])
        P.emit("act", lambda E: E.activation(rstd[:], rstd[:], AF.Exp,
                                             scale=-0.5),
               reads=[(name, "rstd")], writes=[(name, "rstd")])
        for k in range(8):
            dsts = []
            if out_fm is not None:
                dsts.append((out_fm.tile(k, tb * TB, (tb + 1) * TB),
                             out_fm.key(k), True))
            srcap = xb[:, k * TB:(k + 1) * TB]
            if w_sb is not None:
                P.emit("dve", lambda E, s=srcap: E.tensor_mul(sq[:], s, rstd[:]),
                       reads=[(name, "xb"), (name, "rstd")],
                       writes=[(name, "sq")])
                for dap, dkey, _ in dsts:
                    P.emit("dve",
                           lambda E, d=dap, w=w_sb[:, k:k + 1]:
                               E.tensor_scalar_mul(d, sq[:], w),
                           reads=[(name, "sq"), w_key], writes=[dkey])
                if ob is not None:
                    P.emit("dve",
                           lambda E, w=w_sb[:, k:k + 1]:
                               E.tensor_scalar_mul(ob[:], sq[:], w),
                           reads=[(name, "sq"), w_key], writes=[(name, "ob")])
            else:
                for dap, dkey, _ in dsts:
                    P.emit("dve", lambda E, d=dap, s=srcap:
                               E.tensor_mul(d, s, rstd[:]),
                           reads=[(name, "xb"), (name, "rstd")], writes=[dkey])
                if ob is not None:
                    P.emit("dve", lambda E, s=srcap:
                               E.tensor_mul(ob[:], s, rstd[:]),
                           reads=[(name, "xb"), (name, "rstd")],
                           writes=[(name, "ob")])
            if out32_hbm is not None:
                P.dma("sp", lambda E, d=out32_hbm[k * 128:(k + 1) * 128,
                                                  tb * TB:(tb + 1) * TB]:
                          E.dma_start(out=d, in_=ob[:]),
                      reads=[(name, "ob")])
            if out_hbm is not None:
                P.emit("dve", lambda E: E.tensor_copy(obb[:], ob[:]),
                       reads=[(name, "ob")], writes=[(name, "obb")])
                P.dma("sp", lambda E, d=out_hbm[k * 128:(k + 1) * 128,
                                                tb * TB:(tb + 1) * TB]:
                          E.dma_start(out=d, in_=obb[:]),
                      reads=[(name, "obb")])


def emit_gla(P, name, h_fm, scr, cols, consts, o_hbm):
    CH = 64
    NH = 4
    dk, dv = 128, 256
    q_h = P.sbuf(name + "_qh", [128, T], BF16)
    k_h = P.sbuf(name + "_kh", [128, T], BF16)
    v_h = FMT(P, name + "_vh", 256, T, BF16)
    g_h = FMT(P, name + "_gh", 256, T, BF16)
    v_tm = P.sbuf(name + "_vtm", [64, 64 * 256], BF16)
    t1 = P.sbuf(name + "_t1", [16, T], BF16)
    glog = P.sbuf(name + "_glog", [128, T], F32)
    gl32 = P._cvstg[0][:, 0:512]
    gabs = P._cvstg[0][:, 512:1024]
    S32 = P.sbuf(name + "_S", [128, 256], F32)
    Sbf = P.sbuf(name + "_Sbf", [128, 256], BF16)
    onesc = P.sbuf(name + "_ones", [128, CH], F32)
    bchunk = P.sbuf(name + "_b", [128, CH], F32)
    ebs = P.sbuf(name + "_ebs", [128, CH], F32)
    qe = P.sbuf(name + "_qe", [128, CH], BF16)
    ke = P.sbuf(name + "_ke", [128, CH], BF16)
    kd = P.sbuf(name + "_kd", [128, CH], BF16)
    kdtm = P.sbuf(name + "_kdtm", [64, 128], BF16)
    atb = P.sbuf(name + "_atb", [64, CH], BF16)
    ebl = P.sbuf(name + "_ebl", [128, 1], F32)
    osq = P.sbuf(name + "_osq", [64, 256], F32)
    ossq = P.sbuf(name + "_ossq", [64, 1], F32)
    o32 = P.sbuf(name + "_o32", [64, 256], F32)
    gsil = P.sbuf(name + "_gsil", [64, 256], F32)
    otm = P.sbuf(name + "_otm", [64, 256], BF16)
    obf = P.sbuf(name + "_obf", [128, 2 * CH], BF16)
    gnr = P.sbuf(name + "_gnr", [128, 256], F32)
    ws = WRing(P, name + "_ws", 10, [128, 128])
    psP = PsRing(P, name + "_psP", 2, [128, 512])
    psA = PsRing(P, name + "_psA", 2, [64, 64])
    psO = PsRing(P, name + "_psO", 2, [64, 256])
    psSd = PsRing(P, name + "_psSd", 2, [128, 256])
    lns = math.log(dk ** -0.5)
    lns_ap, lns_key = P.cconst(lns)
    P.emit("dve", lambda E: E.memset(onesc[:], 1.0), writes=[(name, "onesc")])
    psg, psgkey = psP.get()
    P.emit("pe", lambda E, ps=psg: E.matmul(
               ps[:, 0:256], consts["ones_f32"][0:1, :], cols["gnorm_row"],
               start=True, stop=True),
           reads=[("gnorm_row",), ("const", "ones_f32")], writes=[psgkey])
    P.emit("act", lambda E, s=psg: E.activation(gnr[:], s[:, 0:256], AF.Copy),
           reads=[psgkey], writes=[(name, "gnr")])

    for h in range(NH):
        def proj(dst_ap_fn, dkey, wname, f0):
            for tb in range(0, T, 512):
                ps, pskey = psP.get()
                wsrc, wsrckey = scr[wname]
                for k in range(8):
                    w_ap, wkey = ws.load_cached(
                        (wsrckey, f0, k),
                        wsrc[k * 128:(k + 1) * 128, f0:f0 + 128],
                        reads=[wsrckey])
                    P.emit("pe",
                           lambda E, ps=ps, w=w_ap,
                                  x=h_fm.tile(k, tb, tb + 512),
                                  st=(k == 0), sp=(k == 7):
                               E.matmul(ps[:], w, x, start=st, stop=sp),
                           reads=[wkey, h_fm.key(k)],
                           writes=[pskey] if k in (0, 7) else [])
                dst_ap_fn(ps, pskey, tb)
        for nm, dst, dkey in (("Wq", q_h, (name, "qh")),
                              ("Wk", k_h, (name, "kh"))):
            def mk(dst=dst, dkey=dkey):
                def f(ps, pskey, tb):
                    P.emit("act", lambda E, d=dst[:, tb:tb + 512], s=ps:
                               E.activation(d, s[:], AF.Copy),
                           reads=[pskey], writes=[dkey])
                return f
            proj(mk(), dkey, nm, h * 128)
        for nm, dst in (("Wv", v_h), ("Wg", g_h)):
            for fi in range(2):
                def mkf(dst=dst, fi=fi):
                    def f(ps, pskey, tb):
                        P.emit("act",
                               lambda E, d=dst.tile(fi, tb, tb + 512), s=ps:
                                   E.activation(d, s[:], AF.Copy),
                               reads=[pskey], writes=[dst.key(fi)])
                    return f
                proj(mkf(), None, nm, h * 256 + fi * 128)
        for tb in range(0, T, 512):
            ps, pskey = psP.get()
            wsrc, wsrckey = scr["Wgk1"]
            for k in range(8):
                w_ap, wkey = ws.load_cached(
                    (wsrckey, k), wsrc[k * 128:(k + 1) * 128, 0:16],
                    reads=[wsrckey])
                P.emit("pe",
                       lambda E, ps=ps, w=w_ap, x=h_fm.tile(k, tb, tb + 512),
                              st=(k == 0), sp=(k == 7):
                           E.matmul(ps[0:16, :], w, x, start=st, stop=sp),
                       reads=[wkey, h_fm.key(k)],
                       writes=[pskey] if k in (0, 7) else [])
            P.emit("act", lambda E, d=t1[0:16, tb:tb + 512], s=ps:
                       E.activation(d, s[0:16, :], AF.Copy),
                   reads=[pskey], writes=[(name, "t1")])
        for tb in range(0, T, 512):
            ps, pskey = psP.get()
            wsrc, wsrckey = scr["Wgk2"]
            w_ap, wkey = ws.load_cached(
                (wsrckey, "gk2", h), wsrc[0:16, h * 128:(h + 1) * 128],
                reads=[wsrckey])
            P.emit("pe", lambda E, ps=ps, w=w_ap, x=t1[0:16, tb:tb + 512]:
                       E.matmul(ps[:], w, x, start=True, stop=True),
                   reads=[wkey, (name, "t1")], writes=[pskey])
            P.emit("dve", lambda E, s=ps, b=cols["bgk2"][:, h:h + 1]:
                       E.tensor_scalar_add(gl32, s[:], b),
                   reads=[pskey, ("bgk2",)], writes=[(name, "gl32")])
            P.emit("act", lambda E: E.activation(gabs, gl32, AF.Abs),
                   reads=[(name, "gl32")], writes=[(name, "gabs")])
            P.emit("act", lambda E: E.activation(gabs, gabs,
                                                 AF.Exp, scale=-1.0),
                   reads=[(name, "gabs")], writes=[(name, "gabs")])
            one_ap, one_key = P.cconst(1.0)
            P.emit("act", lambda E: E.activation(gabs, gabs,
                                                 AF.Ln, bias=one_ap),
                   reads=[(name, "gabs"), one_key], writes=[(name, "gabs")])
            P.emit("dve", lambda E: E.tensor_scalar_min(gl32, gl32, 0.0),
                   reads=[(name, "gl32")], writes=[(name, "gl32")])
            P.emit("dve", lambda E, d=glog[:, tb:tb + 512]:
                       E.tensor_sub(d, gl32, gabs),
                   reads=[(name, "gl32"), (name, "gabs")],
                   writes=[(name, "glog")])
            P.emit("dve", lambda E, d=glog[:, tb:tb + 512]:
                       E.tensor_scalar_mul(d, d, 1.0 / 16.0),
                   reads=[(name, "glog")], writes=[(name, "glog")])
        for i in range(64):
            for fi in range(2):
                ps, pskey = psP.get()
                psbf = ps[0:64, 0:64].bitcast(BF16)
                P.emit("pe",
                       lambda E, ps=psbf,
                              s=v_h.tile(fi, i * 64, (i + 1) * 64):
                           E.transpose(ps, s, consts["ident"][:]),
                       reads=[v_h.key(fi), ("const", "ident")],
                       writes=[pskey])
                P.emit("act",
                       lambda E, d=v_tm[:, i * 256 + fi * 128:
                                        i * 256 + (fi + 1) * 128], s=psbf:
                           E.activation(d, s, AF.Copy),
                       reads=[pskey], writes=[(name, "vtm", i)])
        P.emit("dve", lambda E: E.memset(S32[:], 0.0), writes=[(name, "S")])
        for ci in range(64):
            c0 = ci * CH
            P.emit("dve", lambda E, g=glog[:, c0:c0 + CH]:
                       E.tensor_tensor_scan(bchunk[:], onesc[:], g, 0.0,
                                            OP.mult, OP.add),
                   reads=[(name, "glog"), (name, "onesc")],
                   writes=[(name, "b")])
            P.emit("act", lambda E: E.activation(ebs[:], bchunk[:], AF.Exp,
                                                 bias=lns_ap),
                   reads=[(name, "b"), lns_key], writes=[(name, "ebs")])
            P.emit("dve", lambda E, q=q_h[:, c0:c0 + CH]:
                       E.tensor_mul(qe[:], q, ebs[:]),
                   reads=[(name, "qh"), (name, "ebs")], writes=[(name, "qe")])
            P.emit("act", lambda E: E.activation(ebs[:], bchunk[:], AF.Exp,
                                                 scale=-1.0),
                   reads=[(name, "b"), (name, "qe")], writes=[(name, "ebs")])
            P.emit("dve", lambda E, k=k_h[:, c0:c0 + CH]:
                       E.tensor_mul(ke[:], k, ebs[:]),
                   reads=[(name, "kh"), (name, "ebs")], writes=[(name, "ke")])
            P.emit("act", lambda E, bl=bchunk[:, CH - 1:CH]:
                       E.activation(ebs[:], bchunk[:], AF.Exp, scale=-1.0,
                                    bias=bl),
                   reads=[(name, "b"), (name, "ke")], writes=[(name, "ebs")])
            P.emit("dve", lambda E, k=k_h[:, c0:c0 + CH]:
                       E.tensor_mul(kd[:], k, ebs[:]),
                   reads=[(name, "kh"), (name, "ebs")], writes=[(name, "kd")])
            P.emit("act", lambda E, bl=bchunk[:, CH - 1:CH]:
                       E.activation(ebl[:], bl, AF.Exp),
                   reads=[(name, "b")], writes=[(name, "ebl")])
            ps, pskey = psA.get()
            P.emit("pe", lambda E, ps=ps: E.matmul(ps[:], ke[:], qe[:],
                                                   start=True, stop=True),
                   reads=[(name, "ke"), (name, "qe")], writes=[pskey])
            P.emit("dve", lambda E, s=ps,
                          m=consts["tri"][0:64, 0:64]:
                       E.tensor_mul(atb[:], s[:], m),
                   reads=[pskey, ("const", "tri")], writes=[(name, "atb")])
            P.emit("dve", lambda E: E.tensor_copy(Sbf[:], S32[:]),
                   reads=[(name, "S")], writes=[(name, "Sbf")])
            po, pokey = psO.get()
            P.emit("pe",
                   lambda E, ps=po,
                          v=v_tm[:, ci * 256:(ci + 1) * 256]:
                       E.matmul(ps[:], atb[:], v, start=True, stop=False),
                   reads=[(name, "atb"), (name, "vtm", ci)],
                   writes=[pokey])
            P.emit("pe", lambda E, ps=po: E.matmul(ps[:], qe[:], Sbf[:],
                                                   start=False, stop=True),
                   reads=[(name, "qe"), (name, "Sbf")], writes=[pokey])
            ps2, ps2key = psA.get()
            psbf = ps2[:, 0:64].bitcast(BF16)
            P.emit("pe", lambda E, ps=psbf: E.transpose(
                       ps, kd[:], consts["ident"][:]),
                   reads=[(name, "kd"), ("const", "ident")], writes=[ps2key])
            P.emit("act", lambda E, s=psbf: E.activation(kdtm[:], s, AF.Copy),
                   reads=[ps2key], writes=[(name, "kdtm")])
            psd, psdkey = psSd.get()
            P.emit("pe",
                   lambda E, ps=psd,
                          v=v_tm[:, ci * 256:(ci + 1) * 256]:
                       E.matmul(ps[:], kdtm[:], v, start=True, stop=True),
                   reads=[(name, "kdtm"), (name, "vtm", ci)],
                   writes=[psdkey])
            P.emit("dve", lambda E: E.tensor_scalar_mul(S32[:], S32[:],
                                                        ebl[:, 0:1]),
                   reads=[(name, "S"), (name, "ebl"), (name, "Sbf")],
                   writes=[(name, "S")])
            P.emit("dve", lambda E, s=psd: E.tensor_add(S32[:], S32[:], s[:]),
                   reads=[(name, "S"), psdkey], writes=[(name, "S")])
            P.emit("act", lambda E, s=po: E.activation(
                       osq[:], s[:], AF.Square, accum_out=ossq[:, 0:1]),
                   reads=[pokey], writes=[(name, "ossq")])
            P.emit("act", lambda E, s=po: E.activation(o32[:], s[:], AF.Copy),
                   reads=[pokey], writes=[(name, "o32")])
            eps_ap, eps_key = P.cconst(1e-6)
            P.emit("act", lambda E, b=eps_ap[0:64, :]:
                       E.activation(ossq[:], ossq[:], AF.Ln,
                                    scale=1.0 / 256, bias=b),
                   reads=[(name, "ossq"), eps_key], writes=[(name, "ossq")])
            P.emit("act", lambda E: E.activation(ossq[:], ossq[:], AF.Exp,
                                                 scale=-0.5),
                   reads=[(name, "ossq")], writes=[(name, "ossq")])
            P.emit("dve", lambda E: E.tensor_scalar_mul(o32[:], o32[:],
                                                        ossq[:, 0:1]),
                   reads=[(name, "o32"), (name, "ossq")],
                   writes=[(name, "o32")])
            P.emit("dve", lambda E: E.tensor_mul(o32[:], o32[:],
                                                 gnr[0:64, :]),
                   reads=[(name, "o32"), (name, "gnr")],
                   writes=[(name, "o32")])
            pg, pgkey = psO.get()
            for fi in range(2):
                psbf2 = pg[:, 64 * fi:64 * fi + 64].bitcast(BF16)
                P.emit("pe",
                       lambda E, ps=psbf2,
                              s=g_h.tile(fi, c0, c0 + CH):
                           E.transpose(ps, s, consts["ident"][:]),
                       reads=[g_h.key(fi), ("const", "ident")],
                       writes=[pgkey] if fi == 1 else [])
            P.emit("act", lambda E, s=pg[:, 0:128].bitcast(BF16):
                       E.activation(gsil[:], s, AF.Silu),
                   reads=[pgkey], writes=[(name, "gsil")])
            P.emit("dve", lambda E: E.tensor_mul(otm[:], o32[:], gsil[:]),
                   reads=[(name, "o32"), (name, "gsil")],
                   writes=[(name, "otm")])
            for fi in range(2):
                ps3, ps3key = psSd.get()
                psbf3 = ps3[:, 0:32].bitcast(BF16)
                P.emit("pe",
                       lambda E, ps=psbf3, s=otm[:, fi * 128:(fi + 1) * 128]:
                           E.transpose(ps, s, consts["ident"][0:64, 0:64]),
                       reads=[(name, "otm"), ("const", "ident")],
                       writes=[ps3key])
                P.emit("act",
                       lambda E, d=obf[:, fi * CH:(fi + 1) * CH], s=psbf3:
                           E.activation(d, s, AF.Copy),
                       reads=[ps3key], writes=[(name, "obf", fi)])
            P.dma("sp",
                  [(lambda E, d=o_hbm[h * 256 + fi * 128:
                                      h * 256 + (fi + 1) * 128, c0:c0 + CH],
                           s=obf[:, fi * CH:(fi + 1) * CH]:
                        E.dma_start(out=d, in_=s)) for fi in range(2)],
                  reads=[(name, "obf", 0), (name, "obf", 1)],
                  writes=[(name, "opark")])
    return (name, "opark")


def emit_layer3_pre(P):
    f = lambda n, shape: P.dram(n, shape, F32, kind="ExternalInput")
    scr = {}
    for nm, r, c in (("Wq", D, 512), ("Wk", D, 512), ("Wv", D, D),
                     ("Wg", D, D), ("Wgk1", D, 16), ("Wgk2", 16, 512),
                     ("Wo", D, D), ("Wgate", D, 2816), ("Wup", D, 2816),
                     ("Wdown", 2816, D), ("la", D, 16), ("lb", 16, D)):
        scr[nm] = emit_bf16_scratch(P, nm + "s", f(nm, (r, c)), r, c)
    return scr


def emit_layer3(P, x_h, xo_h, consts, scr, debug=False):
    nc = P.nc
    f = lambda n, shape: P.dram(n, shape, F32, kind="ExternalInput")
    anw_h = f("anw", (128, 8))
    bgk2_h = f("bgk2", (128, 4))
    gnw_h = f("gnw", (1, 256))
    mnw_h = f("mnw", (128, 8))
    qk_h = f("qkern", (D, D))
    qb_h = f("qbias", (128, 8))
    s_h = P.dram("s_park", (D, T), F32,
                 kind="ExternalOutput" if debug else "Internal")
    xn32_h = P.dram("xn32_park", (D, T), F32)
    xnbf_h = P.dram("xnbf_park", (D, T), BF16)
    o_park = P.dram("o_park", (D, T), BF16)
    x2_h = P.dram("x2_park", (D, T), F32)
    wqs_h = P.dram("wq_scr", (D, D), BF16)

    cols = {}
    for nm, hb, w in (("anw", anw_h, 8), ("bgk2", bgk2_h, 4),
                      ("mnw", mnw_h, 8), ("qbias", qb_h, 8)):
        cols[nm] = P.sbuf("sb_" + nm, [128, w], F32)
        P.dma("sp", lambda E, d=cols[nm], s=hb: E.dma_start(out=d[:], in_=s),
              writes=[(nm,)])
    gnorm_row = P.sbuf("sb_gnr", [1, 256], F32)
    P.dma("sp", lambda E: E.dma_start(out=gnorm_row[:], in_=gnw_h),
          writes=[("gnorm_row",)])
    cols["gnorm_row"] = gnorm_row[:]

    P.push_region()
    xn = FMT(P, "xn", D, T, BF16)
    emit_ln(P, "ln", x_h, xn.sb, consts, xn_f32_hbm=xn32_h)
    P.dma("sp", [(lambda E, d=xnbf_h[k * 128:(k + 1) * 128, :],
                         s=xn.tile(k): E.dma_start(out=d, in_=s))
                 for k in range(8)],
          reads=[xn.key(k) for k in range(8)], writes=[("xnbf_park",)])
    P.pop_region()

    P.push_region()
    h_fm = FMT(P, "h_fm", D, T, BF16)
    P.push_region()
    emit_rms(P, "rms1", xn32_h, ("ln", "xn32"), cols["anw"], ("anw",),
             h_fm, consts, eps=1e-6)
    P.pop_region()
    opark_key = emit_gla(P, "gla", h_fm, scr, cols, consts, o_park)
    P.pop_region()

    P.push_region()
    xt, xk = make_hbm_xtile(P, "oat", o_park, opark_key)
    wsO = WRing(P, "l3wsO", 9, [128, 1024])
    psO2 = PsRing(P, "l3psO2", 2, [128, 512])
    a32 = P.sbuf("l3a32", [128, 512], F32)
    xc0 = P.sbuf("l3xc0", [128, 512], F32)

    def epiO(ps, pskey, fm, ta, tb):
        P.dma("sp", lambda E, d=xc0[:],
                     s=xn32_h[fm * 128:(fm + 1) * 128, ta:tb]:
                  E.dma_start(out=d, in_=s), writes=[("l3xc0",)])
        P.emit("dve", lambda E, s=ps: E.tensor_add(a32[:], s[:], xc0[:]),
               reads=[pskey, ("l3xc0",)], writes=[("l3a32",)])
        P.dma("sp", lambda E, d=x2_h[fm * 128:(fm + 1) * 128, ta:tb]:
                  E.dma_start(out=d, in_=a32[:]),
              reads=[("l3a32",)], writes=[("x2",)])
    emit_gemm(P, "l3o", xt, xk, 8, scr["Wo"], D, wsO, psO2, epiO, TMAC=512)
    P.pop_region()

    P.push_region()
    n_fm = FMT(P, "n_fm", D, T, BF16)
    P.push_region()
    emit_rms(P, "rms2", x2_h, ("x2",), cols["mnw"], ("mnw",), n_fm, consts,
             eps=1e-6)
    P.pop_region()
    TM3 = 1024
    mm = FMT(P, "l3mm", 2816, TM3, BF16)
    swi3 = P.sbuf("l3swi", [128, TM3], F32)
    ws3 = WRing(P, "l3ws3", 18, [128, 1024])
    psA3 = PsRing(P, "l3psA3", 2, [128, TM3])
    psB3 = PsRing(P, "l3psB3", 2, [128, 512])
    sc3 = P.sbuf("l3sc", [128, 512], F32)

    for tm in range(0, T, TM3):
        def epiG(ps, pskey, fm, ta, tb):
            P.emit("act", lambda E, s=ps: E.activation(
                       swi3[:, 0:tb - ta], s[:, 0:tb - ta], AF.Silu),
                   reads=[pskey], writes=[("l3swi",)])
        def epiU(ps, pskey, fm, ta, tb):
            P.emit("dve",
                   lambda E, s=ps, d=mm.tile(fm, 0, tb - ta):
                       E.tensor_mul(d, swi3[:, 0:tb - ta], s[:, 0:tb - ta]),
                   reads=[pskey, ("l3swi",)], writes=[mm.key(fm)])
        for fm in range(22):
            emit_gemm(P, "l3g", lambda k, a, b: n_fm.tile(k, a, b), n_fm.key,
                      8, scr["Wgate"], 128, ws3, psA3, 
                      lambda ps, pk, _fm, ta, tb, fm=fm: epiG(ps, pk, fm, ta, tb),
                      f0=fm * 128, t0=tm, t1=tm + TM3, TMAC=TM3)
            emit_gemm(P, "l3u", lambda k, a, b: n_fm.tile(k, a, b), n_fm.key,
                      8, scr["Wup"], 128, ws3, psA3,
                      lambda ps, pk, _fm, ta, tb, fm=fm: epiU(ps, pk, fm, ta, tb),
                      f0=fm * 128, t0=tm, t1=tm + TM3, TMAC=TM3)
        def epiD(ps, pskey, fm, ta, tb):
            P.dma("sp", lambda E, d=sc3[:],
                         s=x2_h[fm * 128:(fm + 1) * 128, ta:tb]:
                      E.dma_start(out=d, in_=s), writes=[("l3sc",)])
            P.emit("dve", lambda E, s=ps: E.tensor_add(sc3[:], sc3[:], s[:]),
                   reads=[pskey, ("l3sc",)], writes=[("l3sc",)])
            P.dma("sp", lambda E, d=s_h[fm * 128:(fm + 1) * 128, ta:tb]:
                      E.dma_start(out=d, in_=sc3[:]),
                  reads=[("l3sc",)], writes=[("s",)])
        emit_gemm(P, "l3d", lambda k, a, b: mm.tile(k, a - tm, b - tm),
                  mm.key, 22, scr["Wdown"], D, ws3, psB3, epiD,
                  t0=tm, t1=tm + TM3, TMAC=512)
    P.pop_region()

    P.push_region()
    qt = P.sbuf("l3qt", [128, 2 * D], F32)
    qbf = P.sbuf("l3qbf", [128, D], BF16)
    gmax = P.sbuf("l3gmax", [128, 1], F32)
    tmax = P.sbuf("l3tmax", [128, 1], F32)
    scol = P.sbuf("l3scol", [128, 2], F32)
    psQ = P.psum("l3psQ", [128, 128], F32)
    for kk in range(8):
        b = kk % 2
        P.dma("sp", lambda E, d=qt[:, b * D:(b + 1) * D],
                     s=qk_h[kk * 128:(kk + 1) * 128, :]:
                  E.dma_start(out=d, in_=s), writes=[("l3qt", b)])
        P.emit("dve", lambda E, s=qt[:, b * D:(b + 1) * D]:
                   E.reduce_max(tmax[:], s, AX.X, apply_absolute_value=True),
               reads=[("l3qt", b)], writes=[("l3tmax",)])
        if kk == 0:
            P.emit("dve", lambda E: E.tensor_copy(gmax[:], tmax[:]),
                   reads=[("l3tmax",)], writes=[("l3gmax",)])
        else:
            P.emit("dve", lambda E: E.tensor_max(gmax[:], gmax[:], tmax[:]),
                   reads=[("l3tmax",), ("l3gmax",)], writes=[("l3gmax",)])
    P.emit("pe", lambda E: E.transpose(psQ[0:1, 0:128], gmax[:],
                                       consts["ident_f32"][:]),
           reads=[("l3gmax",), ("const", "ident_f32")], writes=[("l3psQ",)])
    rowb = P.sbuf("l3rowb", [1, 128], F32)
    P.emit("act", lambda E: E.activation(rowb[:], psQ[0:1, 0:128], AF.Copy),
           reads=[("l3psQ",)], writes=[("l3rowb",)])
    rowm = P.sbuf("l3rowm", [1, 1], F32)
    P.emit("dve", lambda E: E.reduce_max(rowm[:], rowb[:], AX.X),
           reads=[("l3rowb",)], writes=[("l3rowm",)])
    P.emit("pe", lambda E: E.matmul(psQ[:, 0:1], consts["ones_f32"][0:1, :],
                                    rowm[:], start=True, stop=True),
           reads=[("l3rowm",), ("const", "ones_f32")], writes=[("l3psQ",)])
    P.emit("act", lambda E: E.activation(scol[:, 0:1], psQ[:, 0:1], AF.Copy,
                                         scale=1.0 / 7.0),
           reads=[("l3psQ",)], writes=[("l3scol",)])
    P.emit("dve", lambda E: E.reciprocal(scol[:, 1:2], scol[:, 0:1]),
           reads=[("l3scol",)], writes=[("l3srec",)])
    RNE = 12582912.0
    for kk in range(8):
        b = kk % 2
        P.dma("sp", lambda E, d=qt[:, b * D:(b + 1) * D],
                     s=qk_h[kk * 128:(kk + 1) * 128, :]:
                  E.dma_start(out=d, in_=s), writes=[("l3qt", b)])
        ap = qt[:, 0:D] if b == 0 else qt[:, D:2 * D]
        P.emit("dve", lambda E, a=ap: E.tensor_scalar(
                   a, a, scol[:, 1:2], None, OP.mult),
               reads=[("l3qt", b), ("l3srec",)], writes=[("l3qt", b)])
        P.emit("dve", lambda E, a=ap: E.tensor_scalar(
                   a, a, RNE, RNE, OP.add, OP.subtract),
               reads=[("l3qt", b)], writes=[("l3qt", b)])
        P.emit("dve", lambda E, a=ap: E.tensor_scalar(
                   qbf[:], a, scol[:, 0:1], None, OP.mult),
               reads=[("l3qt", b), ("l3scol",)], writes=[("l3qbf",)])
        P.dma("sp", lambda E, d=wqs_h[kk * 128:(kk + 1) * 128, :]:
                  E.dma_start(out=d, in_=qbf[:]),
              reads=[("l3qbf",)], writes=[("wq_scr",)])
    tl = P.sbuf("l3tl", [16, T], BF16)
    xs, xskey = make_hbm_xtile(P, "l3x", xnbf_h, ("xnbf_park",))
    wsF = WRing(P, "l3wsF", 10, [128, 1024])
    psF = PsRing(P, "l3psF", 2, [128, 512])
    for tb in range(0, T, 512):
        ps, pskey = psF.get()
        lsrc, lkey = scr["la"]
        for k in range(8):
            w_ap, wkey = wsF.load_cached(
                (lkey, k), lsrc[k * 128:(k + 1) * 128, 0:16],
                reads=[lkey])
            xap = xs(k, tb, tb + 512)
            xdep = xskey(k)
            P.emit("pe", lambda E, ps=ps, w=w_ap, x=xap,
                          st=(k == 0), sp=(k == 7):
                       E.matmul(ps[0:16, :], w, x, start=st, stop=sp),
                   reads=[wkey] + ([xdep] if xdep else []),
                   writes=[pskey] if k in (0, 7) else [])
        P.emit("act", lambda E, d=tl[0:16, tb:tb + 512], s=ps:
                   E.activation(d, s[0:16, :], AF.Copy),
               reads=[pskey], writes=[("l3tl",)])
    gc3 = P.sbuf("l3gc", [128, 512], F32)
    xcc = P.sbuf("l3xcc", [128, 512], F32)
    scc = P.sbuf("l3scc", [128, 512], F32)

    def extraL(ps, pskey, fm, ta, tb):
        lbsrc, lbkey = scr["lb"]
        w_ap, wkey = wsF.load_cached(
            (lbkey, "lb", fm), lbsrc[0:16, fm * 128:(fm + 1) * 128],
            reads=[lbkey])
        P.emit("pe", lambda E, ps=ps, w=w_ap, x=tl[0:16, ta:tb]:
                   E.matmul(ps[:, 0:tb - ta], w, x, start=True, stop=False),
               reads=[wkey, ("l3tl",)], writes=[pskey])

    def epiF(ps, pskey, fm, ta, tb):
        P.dma("sp", [(lambda E, d=xcc[:],
                             s=x_h[fm * 128:(fm + 1) * 128, ta:tb]:
                          E.dma_start(out=d, in_=s)),
                     (lambda E, d=scc[:],
                             s=s_h[fm * 128:(fm + 1) * 128, ta:tb]:
                          E.dma_start(out=d, in_=s))],
              reads=[("s",)], writes=[("l3xcc",)])
        P.emit("dve", lambda E, s=ps, b=cols["qbias"][:, fm:fm + 1]:
                   E.tensor_scalar_add(gc3[:], s[:], b),
               reads=[pskey, ("qbias",)], writes=[("l3gc",)])
        P.emit("act", lambda E: E.activation(gc3[:], gc3[:], AF.Silu),
               reads=[("l3gc",)], writes=[("l3gc",)])
        P.emit("dve", lambda E: E.tensor_mul(gc3[:], gc3[:], scc[:]),
               reads=[("l3gc",), ("l3xcc",)], writes=[("l3gc",)])
        P.emit("dve", lambda E: E.tensor_add(gc3[:], gc3[:], xcc[:]),
               reads=[("l3gc",), ("l3xcc",)], writes=[("l3gc",)])
        P.dma("sp", lambda E, d=xo_h[fm * 128:(fm + 1) * 128, ta:tb]:
                  E.dma_start(out=d, in_=gc3[:]),
              reads=[("l3gc",)])
    xs2, xs2key = make_hbm_xtile(P, "l3x2", xnbf_h, ("xnbf_park",))
    emit_gemm(P, "l3f", xs2, xs2key, 8, (wqs_h, ("wq_scr",)), D, wsF, psF,
              epiF, TMAC=512, extra_mm=extraL)
    P.pop_region()


def host_prep_layer3(inputs):
    c = host_constants()
    colsb = lambda v: np.ascontiguousarray(np.asarray(v).reshape(-1, 128).T)
    base = dict(
        anw=colsb(inputs["L3_attn_attn_norm_w"]),
        Wq=np.ascontiguousarray(inputs["L3_attn_Wq"]),
        Wk=np.ascontiguousarray(inputs["L3_attn_Wk"]),
        Wv=np.ascontiguousarray(inputs["L3_attn_Wv"]),
        Wg=np.ascontiguousarray(inputs["L3_attn_Wg"]),
        Wgk1=np.ascontiguousarray(inputs["L3_attn_Wgk1"]),
        Wgk2=np.ascontiguousarray(inputs["L3_attn_Wgk2"]),
        bgk2=colsb(inputs["L3_attn_bgk2"]),
        gnw=np.ascontiguousarray(inputs["L3_attn_gnorm_w"].reshape(1, 256)),
        Wo=np.ascontiguousarray(inputs["L3_attn_Wo"]),
        mnw=colsb(inputs["L3_attn_mlp_norm_w"]),
        Wgate=np.ascontiguousarray(inputs["L3_attn_W_gate"]),
        Wup=np.ascontiguousarray(inputs["L3_attn_W_up"]),
        Wdown=np.ascontiguousarray(inputs["L3_attn_W_down"]),
        qkern=np.ascontiguousarray(inputs["L3_ffn_kernel"]),
        qbias=colsb(inputs["L3_ffn_bias"]),
        la=np.ascontiguousarray(inputs["L3_ffn_a"]),
        lb=np.ascontiguousarray(inputs["L3_ffn_b"]),
        **c)
    return base



def emit_layer0_pre(P):
    f = lambda n, shape, dt=F32: P.dram(n, shape, dt, kind="ExternalInput")
    pre = {}
    pre["inw_bf"] = emit_bf16_scratch(P, "inws", f("inw", (D, 4096)), D, 4096)
    pre["xpw_bf"] = emit_bf16_scratch(P, "xpws", f("xpw", (2048, 96)),
                                      2048, 96)
    pre["dtw_bf"] = emit_bf16_scratch(P, "dtws", f("dtw", (64, 2048)),
                                      64, 2048)
    pre["outw_bf"] = emit_bf16_scratch(P, "outws", f("outw", (2048, D)),
                                       2048, D)
    k0_h = f("k0", (2048, 4096), U8)
    k1_h = f("k1", (4096, 4096), U8)
    k2_h = f("k2", (4096, D), U8)
    wd_h = P.dram("wd_scr", (D, 4096), BF16)
    w1_h = P.dram("w1pm_scr", (4096, 4096), BF16)
    w2_h = P.dram("w2pm_scr", (4096, D), BF16)
    cva = P.sbuf("l0cva", [128, 2 * 1024], U8)
    cvo = P.sbuf("l0cvo", [128, 2 * 512], BF16)
    nb = 0
    for r0 in range(0, 1024, 128):
        for c0 in range(0, 4096, 512):
            b = nb % 2
            nb += 1
            a_ap = cva[:, b * 1024:(b + 1) * 1024]
            o_ap = cvo[:, b * 512:(b + 1) * 512]
            P.dma("pool", [
                (lambda E, d=a_ap[0:128, 0:512],
                        s=k0_h[r0:r0 + 128, c0:c0 + 512]:
                     E.dma_start(out=d, in_=s)),
                (lambda E, d=a_ap[0:128, 512:1024],
                        s=k0_h[1024 + r0:1024 + r0 + 128, c0:c0 + 512]:
                     E.dma_start(out=d, in_=s))], writes=[("l0cv", b)])
            P.emit("pool", lambda E, a=a_ap, o=o_ap:
                       E.tensor_sub(o, a[0:128, 0:512], a[0:128, 512:1024]),
                   reads=[("l0cv", b)], writes=[("l0cvo", b)])
            P.dma("pool", lambda E, d=wd_h[r0:r0 + 128, c0:c0 + 512], s=o_ap:
                      E.dma_start(out=d, in_=s),
                  reads=[("l0cvo", b)], writes=[("wd_scr",)])
    for (kh, wh, rows, colsn, wkey) in ((k1_h, w1_h, 4096, 4096, "w1pm"),
                                        (k2_h, w2_h, 4096, D, "w2pm")):
        for r0 in range(0, rows, 128):
            for c0 in range(0, colsn, 512):
                cc = min(512, colsn - c0)
                b = nb % 2
                nb += 1
                a_ap = cva[:, b * 1024:b * 1024 + cc]
                o_ap = cvo[:, b * 512:b * 512 + cc]
                P.dma("pool", lambda E, d=a_ap, s=kh[r0:r0 + 128, c0:c0 + cc]:
                          E.dma_start(out=d, in_=s), writes=[("l0cv", b)])
                P.emit("pool", lambda E, a=a_ap, o=o_ap: E.tensor_scalar(
                           o, a, 2.0, -1.0, OP.mult, OP.add),
                       reads=[("l0cv", b)], writes=[("l0cvo", b)])
                P.dma("pool", lambda E, d=wh[r0:r0 + 128, c0:c0 + cc], s=o_ap:
                          E.dma_start(out=d, in_=s),
                      reads=[("l0cvo", b)], writes=[(wkey,)])
    pre["wd"] = (wd_h, ("wd_scr",))
    pre["w1pm"] = (w1_h, ("w1pm",))
    pre["w2pm"] = (w2_h, ("w2pm",))
    return pre


def emit_layer0(P, x_h, xo_h, consts, pre, debug=False):
    nc = P.nc
    f = lambda n, shape, dt=F32: P.dram(n, shape, dt, kind="ExternalInput")
    ng_h = f("ng", (128, 8))
    nb_h = f("nb", (128, 8))
    convw_h = f("convw", (128, 64))
    convb_h = f("convb", (128, 16))
    dtb_h = f("dtb", (128, 16))
    alog_h = f("alog", (128, 256))
    dcol_h = f("dcol", (128, 16))
    s0_h = f("s0", (1, 1))
    s1_h = f("s1", (1, 1))
    s2_h = f("s2", (1, 1))
    s_h = P.dram("s_park", (D, T), F32,
                 kind="ExternalOutput" if debug else "Internal")
    g_h = P.dram("g_park", (D, T), F32,
                 kind="ExternalOutput" if debug else "Internal")
    xn32_h = P.dram("xn32_park", (D, T), F32)
    xnbf_h = P.dram("xnbf_park", (D, T), BF16)
    u_h = P.dram("u_park", (2048, T), F32)
    ubf_h = P.dram("ubf_park", (2048, T), BF16)
    z_h = P.dram("z_park", (2048, T), F32)
    dl_h = P.dram("dl_park", (2048, T), F32)
    inw_bf, xpw_bf, dtw_bf, outw_bf = (pre["inw_bf"], pre["xpw_bf"],
                                       pre["dtw_bf"], pre["outw_bf"])
    wd_h, wd_key = pre["wd"]
    w1_h, w1pm_key = pre["w1pm"]
    w2_h, w2pm_key = pre["w2pm"]

    load_sel32(P, consts)

    cols = {}
    for nm, hb, w in (("ng", ng_h, 8), ("nb", nb_h, 8), ("convw", convw_h, 64),
                      ("convb", convb_h, 16), ("dtb", dtb_h, 16),
                      ("alog", alog_h, 256), ("dcol", dcol_h, 16)):
        cols[nm] = P.sbuf("sb_" + nm, [128, w], F32)
        P.dma("sp", lambda E, d=cols[nm], s=hb: E.dma_start(out=d[:], in_=s),
              writes=[(nm,)])
    negA = P.sbuf("negA", [128, 256], F32)
    P.emit("act", lambda E: E.activation(negA[:], cols["alog"][:], AF.Exp),
           reads=[("alog",)], writes=[("negA",)])
    P.emit("dve", lambda E: E.tensor_scalar_mul(negA[:], negA[:], -1.0),
           reads=[("negA",)], writes=[("negA",)])

    scl = P.sbuf("l0scl", [1, 4], F32)
    ccol = P.sbuf("l0ccol", [128, 1], F32)
    psC0 = P.psum("l0psC0", [128, 8], F32)
    P.dma("sp", [lambda E: E.dma_start(out=scl[0:1, 0:1], in_=s0_h),
                 lambda E: E.dma_start(out=scl[0:1, 1:2], in_=s1_h),
                 lambda E: E.dma_start(out=scl[0:1, 2:3], in_=s2_h)],
          writes=[("l0scl",)])
    P.emit("dve", lambda E: E.tensor_mul(scl[0:1, 3:4], scl[0:1, 0:1],
                                         scl[0:1, 1:2]),
           reads=[("l0scl",)], writes=[("l0sc3",)])
    P.emit("dve", lambda E: E.tensor_mul(scl[0:1, 3:4], scl[0:1, 3:4],
                                         scl[0:1, 2:3]),
           reads=[("l0sc3",), ("l0scl",)], writes=[("l0sc3",)])
    P.emit("dve", lambda E: E.tensor_scalar_mul(scl[0:1, 3:4], scl[0:1, 3:4],
                                                8.0),
           reads=[("l0sc3",)], writes=[("l0sc3",)])
    P.emit("pe", lambda E: E.matmul(psC0[:, 0:1], consts["ones_f32"][0:1, :],
                                    scl[0:1, 3:4], start=True, stop=True),
           reads=[("l0sc3",), ("const", "ones_f32")], writes=[("l0psC0",)])
    P.emit("act", lambda E: E.activation(ccol[:], psC0[:, 0:1], AF.Copy),
           reads=[("l0psC0",)], writes=[("l0ccol",)])

    P.push_region()
    xn = FMT(P, "xn", D, T, BF16)
    emit_ln(P, "ln", x_h, xn.sb, consts, xn_f32_hbm=xn32_h)
    P.dma("sp", [(lambda E, d=xnbf_h[k * 128:(k + 1) * 128, :],
                         s=xn.tile(k): E.dma_start(out=d, in_=s))
                 for k in range(8)],
          reads=[xn.key(k) for k in range(8)], writes=[("xnbf_park",)])
    P.pop_region()

    P.push_region()
    TMF = 512
    z1 = FMT(P, "l0z1", 4096, TMF, BF16)
    z2 = FMT(P, "l0z2", 4096, TMF, BF16)
    wsF = WRing(P, "l0wsF", 34, [128, 1024])
    psF = PsRing(P, "l0psF", 2, [128, TMF])
    gcb = P.sbuf("l0gcb", [128, 512], F32)
    xsF, xsFkey = make_hbm_xtile(P, "l0ffx", xnbf_h, ("xnbf_park",))
    for tm in range(0, T, TMF):
        def epi1(ps, pskey, fm, ta, tb):
            P.emit("act", lambda E, d=z1.tile(fm, 0, tb - ta), s=ps:
                       E.activation(d, s[:, 0:tb - ta], AF.Relu),
                   reads=[pskey], writes=[z1.key(fm)])
        emit_gemm(P, "l0f1", xsF, xsFkey, 8, (wd_h, ("wd_scr",)), 4096,
                  wsF, psF, epi1, t0=tm, t1=tm + TMF, TMAC=TMF)
        def epi2(ps, pskey, fm, ta, tb):
            P.emit("act", lambda E, d=z2.tile(fm, 0, tb - ta), s=ps:
                       E.activation(d, s[:, 0:tb - ta], AF.Relu),
                   reads=[pskey], writes=[z2.key(fm)])
        emit_gemm(P, "l0f2", lambda k, a, b: z1.tile(k, a - tm, b - tm),
                  z1.key, 32, (w1_h, ("w1pm",)), 4096, wsF, psF, epi2,
                  t0=tm, t1=tm + TMF, TMAC=TMF)
        def epi3(ps, pskey, fm, ta, tb):
            P.emit("dve", lambda E, s=ps: E.tensor_scalar(
                       gcb[:], s[:], ccol[:, 0:1], None, OP.mult),
                   reads=[pskey, ("l0ccol",)], writes=[("l0gcb",)])
            P.emit("act", lambda E: E.activation(gcb[:], gcb[:], AF.Silu),
                   reads=[("l0gcb",)], writes=[("l0gcb",)])
            P.dma("sp", lambda E, d=g_h[fm * 128:(fm + 1) * 128, ta:tb]:
                      E.dma_start(out=d, in_=gcb[:]),
                  reads=[("l0gcb",)], writes=[("g",)])
        emit_gemm(P, "l0f3", lambda k, a, b: z2.tile(k, a - tm, b - tm),
                  z2.key, 32, (w2_h, ("w2pm",)), D, wsF, psF, epi3,
                  t0=tm, t1=tm + TMF, TMAC=512)
    P.pop_region()

    P.push_region()
    xn2 = FMT(P, "xn2", D, T, BF16)
    P.push_region()
    TB2 = 512
    xb2 = P.sbuf("l0xb2", [128, 8 * TB2], F32)
    sq2 = P.sbuf("l0sq2", [128, TB2], F32)
    mu2 = P.sbuf("l0mu2", [128, TB2], F32)
    rs2 = P.sbuf("l0rs2", [128, TB2], F32)
    va2 = P.sbuf("l0va2", [128, TB2], F32)
    ps_s2 = P.psum("l0ps_s2", [128, TB2], F32)
    ps_q2 = P.psum("l0ps_q2", [128, TB2], F32)
    eps_ap, eps_key = P.cconst(LN_EPS)
    for tb in range(T // TB2):
        P.dma("sp",
              [(lambda E, d=xb2[:, k * TB2:(k + 1) * TB2],
                       s=xn32_h[k * 128:(k + 1) * 128,
                                tb * TB2:(tb + 1) * TB2]:
                    E.dma_start(out=d, in_=s)) for k in range(8)],
              reads=[("ln", "xn32")], writes=[("l0xb2",)])
        for k in range(8):
            P.emit("act", lambda E, s=xb2[:, k * TB2:(k + 1) * TB2]:
                       E.activation(sq2[:], s, AF.Square),
                   reads=[("l0xb2",)], writes=[("l0sq2",)])
            P.emit("pe", lambda E, s=xb2[:, k * TB2:(k + 1) * TB2],
                          st=(k == 0), sp=(k == 7):
                       E.matmul(ps_s2[:], consts["ones_f32"][:], s,
                                start=st, stop=sp),
                   reads=[("l0xb2",), ("const", "ones_f32")],
                   writes=[("l0ps_s2",)] if k in (0, 7) else [])
            P.emit("pe", lambda E, st=(k == 0), sp=(k == 7):
                       E.matmul(ps_q2[:], consts["ones_f32"][:], sq2[:],
                                start=st, stop=sp),
                   reads=[("l0sq2",), ("const", "ones_f32")],
                   writes=[("l0ps_q2",)] if k in (0, 7) else [])
        P.emit("act", lambda E: E.activation(mu2[:], ps_s2[:], AF.Copy,
                                             scale=1.0 / D),
               reads=[("l0ps_s2",)], writes=[("l0mu2",)])
        P.emit("dve", lambda E: E.tensor_mul(va2[:], mu2[:], mu2[:]),
               reads=[("l0mu2",)], writes=[("l0va2",)])
        P.emit("dve", lambda E: E.scalar_tensor_tensor(
                   va2[:], ps_q2[:], 1.0 / D, va2[:], OP.mult, OP.subtract),
               reads=[("l0ps_q2",), ("l0va2",)], writes=[("l0va2",)])
        P.emit("act", lambda E: E.activation(va2[:], va2[:], AF.Ln,
                                             bias=eps_ap),
               reads=[("l0va2",), eps_key], writes=[("l0va2",)])
        P.emit("act", lambda E: E.activation(rs2[:], va2[:], AF.Exp,
                                             scale=-0.5),
               reads=[("l0va2",)], writes=[("l0rs2",)])
        for k in range(8):
            P.emit("dve", lambda E, s=xb2[:, k * TB2:(k + 1) * TB2]:
                       E.tensor_sub(sq2[:], s, mu2[:]),
                   reads=[("l0xb2",), ("l0mu2",)], writes=[("l0sq2",)])
            P.emit("dve", lambda E: E.tensor_mul(sq2[:], sq2[:], rs2[:]),
                   reads=[("l0sq2",), ("l0rs2",)], writes=[("l0sq2",)])
            P.emit("dve",
                   lambda E, d=xn2.tile(k, tb * TB2, (tb + 1) * TB2),
                          g=cols["ng"][:, k:k + 1], b=cols["nb"][:, k:k + 1]:
                       E.tensor_scalar(d, sq2[:], g, b, OP.mult, OP.add),
                   reads=[("l0sq2",), ("ng",), ("nb",)], writes=[xn2.key(k)])
    P.pop_region()

    TMI = 512
    cbuf = P.sbuf("l0cbuf", [128, 3 + TMI], F32)
    tails = P.sbuf("l0tails", [128, 16 * 3], F32)
    acc = P.sbuf("l0acc", [128, TMI], F32)
    u32 = P.sbuf("l0u32", [128, TMI], F32)
    ubfb = P.sbuf("l0ubfb", [128, TMI], BF16)
    z32 = P.sbuf("l0z32", [128, TMI], F32)
    wsI = WRing(P, "l0wsI", 10, [128, 1024])
    psI = PsRing(P, "l0psI", 2, [128, TMI])
    P.emit("dve", lambda E: E.memset(tails[:], 0.0), writes=[("l0tails",)])

    def epiI(ps, pskey, fm, ta, tb):
        if fm < 16:
            P.emit("act", lambda E, s=ps, d=cbuf[:, 3:3 + TMI]:
                       E.activation(d, s[:], AF.Copy),
                   reads=[pskey], writes=[("l0cbuf",)])
            P.emit("dve", lambda E, t=tails[:, fm * 3:(fm + 1) * 3]:
                       E.tensor_copy(cbuf[:, 0:3], t),
                   reads=[("l0tails",), ("l0cbuf",)], writes=[("l0cbuf",)])
            P.emit("dve", lambda E, w=cols["convw"][:, fm * 4:fm * 4 + 1]:
                       E.tensor_scalar(acc[:], cbuf[:, 0:TMI], w, None,
                                       OP.mult),
                   reads=[("l0cbuf",), ("convw",)], writes=[("l0acc",)])
            for j in range(1, 4):
                P.emit("dve",
                       lambda E, a=cbuf[:, j:j + TMI],
                              w=cols["convw"][:, fm * 4 + j:fm * 4 + j + 1]:
                           E.scalar_tensor_tensor(acc[:], a, w, acc[:],
                                                  OP.mult, OP.add),
                       reads=[("l0cbuf",), ("convw",), ("l0acc",)],
                       writes=[("l0acc",)])
            P.emit("dve", lambda E, t=tails[:, fm * 3:(fm + 1) * 3]:
                       E.tensor_copy(t, cbuf[:, TMI:TMI + 3]),
                   reads=[("l0cbuf",)], writes=[("l0tails",)])
            P.emit("act", lambda E, b=cols["convb"][:, fm:fm + 1]:
                       E.activation(u32[:], acc[:], AF.Silu, bias=b),
                   reads=[("l0acc",), ("convb",)], writes=[("l0u32",)])
            P.emit("dve", lambda E: E.tensor_copy(ubfb[:], u32[:]),
                   reads=[("l0u32",)], writes=[("l0ubfb",)])
            P.dma("sp", [
                (lambda E, d=u_h[fm * 128:(fm + 1) * 128, ta:tb]:
                     E.dma_start(out=d, in_=u32[:])),
                (lambda E, d=ubf_h[fm * 128:(fm + 1) * 128, ta:tb]:
                     E.dma_start(out=d, in_=ubfb[:]))],
                reads=[("l0u32",), ("l0ubfb",)], writes=[("u_park",)])
        else:
            fz = fm - 16
            P.emit("act", lambda E, s=ps: E.activation(z32[:], s[:], AF.Copy),
                   reads=[pskey], writes=[("l0z32",)])
            P.dma("sp", lambda E, d=z_h[fz * 128:(fz + 1) * 128, ta:tb]:
                      E.dma_start(out=d, in_=z32[:]),
                  reads=[("l0z32",)], writes=[("z_park",)])

    emit_gemm(P, "l0in", lambda k, a, b: xn2.tile(k, a, b), xn2.key,
              8, inw_bf, 4096, wsI, psI, epiI, TMAC=TMI)
    P.pop_region()

    BC32 = P.sbuf("l0BC", [32, T], F32)
    P.push_region()
    drbf = P.sbuf("l0drbf", [64, T], BF16)
    d32 = P.sbuf("l0d32", [128, 512], F32)
    dab = P.sbuf("l0dab", [128, 512], F32)
    wsX = WRing(P, "l0wsX", 34, [128, 128])
    psX = PsRing(P, "l0psX", 2, [128, 512])
    xsU, xsUkey = make_hbm_xtile(P, "l0ux", ubf_h, ("u_park",), nslots=18)
    for tb in range(0, T, 512):
        ps, pskey = psX.get()
        xsrc, xsrckey = xpw_bf
        for k in range(16):
            w_ap, wkey = wsX.load_cached(
                (xsrckey, k), xsrc[k * 128:(k + 1) * 128, :],
                reads=[xsrckey])
            xap = xsU(k, tb, tb + 512)
            xdep = xsUkey(k)
            P.emit("pe", lambda E, ps=ps, w=w_ap, x=xap,
                          st=(k == 0), sp=(k == 15):
                       E.matmul(ps[0:96, :], w, x, start=st, stop=sp),
                   reads=[wkey] + ([xdep] if xdep else []),
                   writes=[pskey] if k in (0, 15) else [])
        P.emit("act", lambda E, d=drbf[:, tb:tb + 512], s=ps:
                   E.activation(d, s[0:64, :], AF.Copy),
               reads=[pskey], writes=[("l0drbf",)])
        P.emit("act", lambda E, d=BC32[:, tb:tb + 512], s=ps:
                   E.activation(d, s[64:96, :], AF.Copy),
               reads=[pskey], writes=[("l0BC",)])
    for tb in range(0, T, 512):
        for fm in range(16):
            ps, pskey = psX.get()
            dsrc, dkey = dtw_bf
            w_ap, wkey = wsX.load_cached(
                (dkey, fm), dsrc[0:64, fm * 128:(fm + 1) * 128],
                reads=[dkey])
            P.emit("pe", lambda E, ps=ps, w=w_ap, x=drbf[0:64, tb:tb + 512]:
                       E.matmul(ps[:], w, x, start=True, stop=True),
                   reads=[wkey, ("l0drbf",)], writes=[pskey])
            P.emit("dve", lambda E, s=ps, b=cols["dtb"][:, fm:fm + 1]:
                       E.tensor_scalar_add(d32[:], s[:], b),
                   reads=[pskey, ("dtb",)], writes=[("l0d32",)])
            P.emit("act", lambda E: E.activation(dab[:], d32[:], AF.Abs),
                   reads=[("l0d32",)], writes=[("l0dab",)])
            P.emit("act", lambda E: E.activation(dab[:], dab[:], AF.Exp,
                                                 scale=-1.0),
                   reads=[("l0dab",)], writes=[("l0dab",)])
            one_ap, one_key = P.cconst(1.0)
            P.emit("act", lambda E: E.activation(dab[:], dab[:], AF.Ln,
                                                 bias=one_ap),
                   reads=[("l0dab",), one_key], writes=[("l0dab",)])
            P.emit("dve", lambda E: E.tensor_scalar_max(d32[:], d32[:], 0.0),
                   reads=[("l0d32",)], writes=[("l0d32",)])
            P.emit("dve", lambda E: E.tensor_add(d32[:], d32[:], dab[:]),
                   reads=[("l0d32",), ("l0dab",)], writes=[("l0d32",)])
            P.dma("sp", lambda E, d=dl_h[fm * 128:(fm + 1) * 128, tb:tb + 512]:
                      E.dma_start(out=d, in_=d32[:]),
                  reads=[("l0d32",)], writes=[("dl_park",)])
    P.pop_region()

    P.push_region()
    CHS = 512
    dbuf = P.sbuf("scn_d", [128, 16 * CHS], F32)
    dubuf = P.sbuf("scn_du", [128, 16 * CHS], F32)
    yacc = P.sbuf("scn_y", [128, 16 * CHS], F32)
    brep = [P.sbuf(f"scn_brep{i}", [128, CHS], F32) for i in range(2)]
    crep = [P.sbuf(f"scn_crep{i}", [128, CHS], F32) for i in range(2)]
    da = [P.sbuf(f"scn_da{i}", [128, CHS], F32) for i in range(2)]
    dbu = [P.sbuf(f"scn_dbu{i}", [128, CHS], F32) for i in range(2)]
    hsc = [P.sbuf(f"scn_h{i}", [128, CHS], F32) for i in range(2)]
    chp = [P.sbuf(f"scn_ch{i}", [128, CHS], F32) for i in range(2)]
    carry = P.sbuf("scn_carry", [128, 256], F32)
    utmp = P.sbuf("scn_ut", [128, CHS], F32)
    ztmp = P.sbuf("scn_zt", [128, CHS], F32)
    y2 = P.sbuf("scn_y2", [128, 16 * CHS], BF16)
    psS = PsRing(P, "scn_psS", 2, [128, CHS])
    pso = PsRing(P, "scn_pso", 2, [128, 512])
    wsS = WRing(P, "scn_ws", 12, [128, 256])
    P.emit("dve", lambda E: E.memset(carry[:], 0.0), writes=[("carry",)])
    for ci in range(T // CHS):
        c0 = ci * CHS
        P.dma("sp",
              [(lambda E, d=dbuf[:, s * CHS:(s + 1) * CHS],
                       s_=dl_h[s * 128:(s + 1) * 128, c0:c0 + CHS]:
                    E.dma_start(out=d, in_=s_)) for s in range(16)],
              reads=[("dl_park",)], writes=[("scn_d",)])
        for s in range(16):
            P.dma("sp", lambda E, d=utmp[:],
                         s_=u_h[s * 128:(s + 1) * 128, c0:c0 + CHS]:
                      E.dma_start(out=d, in_=s_),
                  reads=[("u_park",)], writes=[("scn_ut",)])
            P.emit("dve",
                   lambda E, d=dubuf[:, s * CHS:(s + 1) * CHS],
                          dd=dbuf[:, s * CHS:(s + 1) * CHS]:
                       E.tensor_mul(d, dd, utmp[:]),
                   reads=[("scn_ut",), ("scn_d",)], writes=[("scn_du",)])
        P.emit("dve", lambda E: E.memset(yacc[:], 0.0),
               writes=[("scn_y",)])
        def bc_build(n):
            bn = n % 2
            for which in (0, 1):
                dst = (brep, crep)[which]
                ps, pskey = psS.get()
                P.emit("pe",
                       lambda E, ps=ps,
                              sel=consts["sel32"][:, (which * 16 + n) * 128:
                                                  (which * 16 + n + 1) * 128],
                              x=BC32[0:32, c0:c0 + CHS]:
                           E.matmul(ps[:], sel, x, start=True, stop=True),
                       reads=[("l0BC",), ("const", "sel32")], writes=[pskey])
                P.emit("act", lambda E, d=dst[bn], s=ps:
                           E.activation(d[:], s[:], AF.Copy),
                       reads=[pskey],
                       writes=[("scn_brep", bn) if which == 0
                               else ("scn_crep", bn)])

        def front(i):
            n, s = divmod(i, 16)
            pb = i % 2
            bn = n % 2
            if s == 0:
                bc_build(n)
            P.emit("act",
                   lambda E, d=da[pb], src=dbuf[:, s * CHS:(s + 1) * CHS],
                          a=negA[:, s * 16 + n:s * 16 + n + 1]:
                       E.activation(d[:], src, AF.Exp, scale=a),
                   reads=[("scn_d",), ("negA",)], writes=[("scn_da", pb)])
            P.emit("pool",
                   lambda E, d=dbu[pb], du=dubuf[:, s * CHS:(s + 1) * CHS],
                          b=brep[bn]:
                       E.tensor_mul(d[:], du, b[:]),
                   reads=[("scn_du",), ("scn_brep", bn)],
                   writes=[("scn_dbu", pb)])
            P.emit("dve",
                   lambda E, d=hsc[pb], a=da[pb], b=dbu[pb],
                          init=carry[:, s * 16 + n:s * 16 + n + 1]:
                       E.tensor_tensor_scan(d[:], a[:], b[:], init,
                                            OP.mult, OP.add),
                   reads=[("scn_da", pb), ("scn_dbu", pb), ("carry",)],
                   writes=[("scn_h", pb)])

        def back(i):
            n, s = divmod(i, 16)
            pb = i % 2
            bn = n % 2
            P.emit("act",
                   lambda E, d=carry[:, s * 16 + n:s * 16 + n + 1],
                          h=hsc[pb]:
                       E.activation(d, h[:, CHS - 1:CHS], AF.Copy),
                   reads=[("scn_h", pb)], writes=[("carry",)])
            P.emit("dve" if i % 2 == 0 else "pool",
                   lambda E, d=chp[pb], h=hsc[pb], cr=crep[bn]:
                       E.tensor_mul(d[:], h[:], cr[:]),
                   reads=[("scn_h", pb), ("scn_crep", bn)],
                   writes=[("scn_ch", pb)])
            P.emit("dve",
                   lambda E, d=yacc[:, s * CHS:(s + 1) * CHS], c=chp[pb]:
                       E.tensor_add(d, d, c[:]),
                   reads=[("scn_ch", pb), ("scn_y",)],
                   writes=[("scn_y",)])

        for i in range(257):
            if i < 256:
                front(i)
            if i >= 1:
                back(i - 1)
        for s in range(16):
            P.dma("sp", [
                (lambda E, d=utmp[:],
                        s_=u_h[s * 128:(s + 1) * 128, c0:c0 + CHS]:
                     E.dma_start(out=d, in_=s_)),
                (lambda E, d=ztmp[:],
                        s_=z_h[s * 128:(s + 1) * 128, c0:c0 + CHS]:
                     E.dma_start(out=d, in_=s_))],
                reads=[("u_park",), ("z_park",)],
                writes=[("scn_ut",), ("scn_zt",)])
            P.emit("dve",
                   lambda E, y=yacc[:, s * CHS:(s + 1) * CHS],
                          dc=cols["dcol"][:, s:s + 1]:
                       E.scalar_tensor_tensor(y, utmp[:], dc, y,
                                              OP.mult, OP.add),
                   reads=[("scn_ut",), ("dcol",), ("scn_y",)],
                   writes=[("scn_y",)])
            P.emit("act", lambda E: E.activation(ztmp[:], ztmp[:], AF.Silu),
                   reads=[("scn_zt",)], writes=[("scn_zt",)])
            P.emit("dve",
                   lambda E, d=y2[:, s * CHS:(s + 1) * CHS],
                          y=yacc[:, s * CHS:(s + 1) * CHS]:
                       E.tensor_mul(d, y, ztmp[:]),
                   reads=[("scn_y",), ("scn_zt",)], writes=[("scn_y2",)])
        for dm in range(8):
            ps, pskey = pso.get()
            osrc, okey = outw_bf
            w0 = (dm * 128 // 256) * 256
            for k in range(16):
                w_ap, wkey = wsS.load_cached(
                    (okey, k, w0), osrc[k * 128:(k + 1) * 128, w0:w0 + 256],
                    reads=[okey])
                P.emit("pe",
                       lambda E, ps=ps, w=w_ap[:, dm * 128 - w0:
                                               dm * 128 - w0 + 128],
                              x=y2[:, k * CHS:(k + 1) * CHS],
                              st=(k == 0), sp=(k == 15):
                           E.matmul(ps[:], w, x, start=st, stop=sp),
                       reads=[wkey, ("scn_y2",)],
                       writes=[pskey] if k in (0, 15) else [])
            P.dma("sp", lambda E, d=utmp[:],
                         s_=xn32_h[dm * 128:(dm + 1) * 128, c0:c0 + CHS]:
                      E.dma_start(out=d, in_=s_),
                  reads=[("ln", "xn32"), ("scn_ut",)], writes=[("scn_ut",)])
            P.emit("dve", lambda E, s=ps: E.tensor_add(utmp[:], utmp[:],
                                                       s[:]),
                   reads=[pskey, ("scn_ut",)], writes=[("scn_ut",)])
            P.dma("sp", lambda E, d=s_h[dm * 128:(dm + 1) * 128, c0:c0 + CHS]:
                      E.dma_start(out=d, in_=utmp[:]),
                  reads=[("scn_ut",)], writes=[("s",)])
    P.pop_region()

    P.push_region()
    xcc = P.sbuf("l0xcc", [128, 512], F32)
    scc = P.sbuf("l0scc", [128, 512], F32)
    gcc = P.sbuf("l0gcc", [128, 512], F32)
    for fm in range(8):
        for c0 in range(0, T, 512):
            P.dma("sp", [
                (lambda E, d=xcc[:], s=x_h[fm * 128:(fm + 1) * 128,
                                           c0:c0 + 512]:
                     E.dma_start(out=d, in_=s)),
                (lambda E, d=scc[:], s=s_h[fm * 128:(fm + 1) * 128,
                                           c0:c0 + 512]:
                     E.dma_start(out=d, in_=s)),
                (lambda E, d=gcc[:], s=g_h[fm * 128:(fm + 1) * 128,
                                           c0:c0 + 512]:
                     E.dma_start(out=d, in_=s))],
                reads=[("s",), ("g",)], writes=[("l0cmb",), ("l0gcc",)])
            P.emit("dve", lambda E: E.tensor_mul(gcc[:], gcc[:], scc[:]),
                   reads=[("l0cmb",)], writes=[("l0gcc",)])
            P.emit("dve", lambda E: E.tensor_add(gcc[:], gcc[:], xcc[:]),
                   reads=[("l0gcc",), ("l0cmb",)], writes=[("l0gcc",)])
            P.dma("sp", lambda E, d=xo_h[fm * 128:(fm + 1) * 128, c0:c0 + 512]:
                      E.dma_start(out=d, in_=gcc[:]),
                  reads=[("l0gcc",)])
    P.pop_region()


def host_prep_layer0(inputs):
    c = host_constants()
    colsb = lambda v: np.ascontiguousarray(np.asarray(v).reshape(-1, 128).T)
    convw = inputs["L0_attn_conv_w"][:, 0, :]
    convw_cols = np.ascontiguousarray(
        convw.reshape(16, 128, 4).transpose(1, 0, 2).reshape(128, 64))
    alog_cols = np.ascontiguousarray(
        inputs["L0_attn_A_log"].reshape(16, 128, 16).transpose(1, 0, 2)
        .reshape(128, 256))
    base = dict(
        ng=colsb(inputs["L0_attn_norm_g"]),
        nb=colsb(inputs["L0_attn_norm_b"]),
        inw=np.ascontiguousarray(inputs["L0_attn_in_w"].T),
        convw=convw_cols,
        convb=colsb(inputs["L0_attn_conv_b"]),
        xpw=np.ascontiguousarray(inputs["L0_attn_xproj_w"].T),
        dtw=np.ascontiguousarray(inputs["L0_attn_dt_w"].T),
        dtb=colsb(inputs["L0_attn_dt_b"]),
        alog=alog_cols,
        dcol=colsb(inputs["L0_attn_D"]),
        outw=np.ascontiguousarray(inputs["L0_attn_out_w"].T),
        k0=np.ascontiguousarray(inputs["L0_ffn_k0"]).view(np.uint8),
        k1=np.ascontiguousarray(inputs["L0_ffn_k1"]).view(np.uint8),
        k2=np.ascontiguousarray(inputs["L0_ffn_k2"]).view(np.uint8),
        s0=np.asarray(inputs["L0_ffn_s0"], dtype=np.float32).reshape(1, 1),
        s1=np.asarray(inputs["L0_ffn_s1"], dtype=np.float32).reshape(1, 1),
        s2=np.asarray(inputs["L0_ffn_s2"], dtype=np.float32).reshape(1, 1),
        **c)
    return base



def _np_fm(a):
    return np.ascontiguousarray(np.asarray(a).T)


_EMITTERS = [(emit_layer0, host_prep_layer0), (emit_layer1, host_prep_layer1),
             (emit_layer2, host_prep_layer2), (emit_layer3, host_prep_layer3)]

_SHARED_CONSTS = ("c_ident", "c_ident_f32", "c_ones_f32", "c_tri", "c_sel",
                  "c_sel32")


_PRES = [emit_layer0_pre, emit_layer1_pre, emit_layer2_pre, emit_layer3_pre]


def build_fused(nc, es):
    P = Prog(nc, es)
    consts = declare_constants(P)
    x0 = nc.dram_tensor("x", [D, T], F32, kind="ExternalInput").ap()
    xf = nc.dram_tensor("xo", [D, T], F32, kind="ExternalOutput").ap()
    chain = [x0]
    for i in range(3):
        chain.append(nc.dram_tensor(f"xchain{i}", [D, T], F32,
                                    kind="Internal").ap())
    chain.append(xf)
    pres = []
    for i, prefn in enumerate(_PRES):
        P.prefix = f"l{i}_"
        pres.append(prefn(P))
    for i, (emit, _) in enumerate(_EMITTERS):
        P.prefix = f"l{i}_"
        P.push_region()
        emit(P, chain[i], chain[i + 1], consts, pres[i])
        P.pop_region()
    P.prefix = ""
    P.finalize()
    return P


def fused_in_map(inputs):
    base = dict(host_constants())
    for i, (_, prep) in enumerate(_EMITTERS):
        d = prep(inputs)
        for k, v in d.items():
            if k in _SHARED_CONSTS:
                continue
            base[f"l{i}_{k}"] = v
    return base


def kernel(**inputs):
    x = np.asarray(inputs["L0_attn_x"], dtype=np.float32)
    nc = bass.Bass("TRN2", target_bir_lowering=False)
    es = contextlib.ExitStack()
    build_fused(nc, es)
    base = fused_in_map(inputs)
    in_maps = [dict(base, x=np.ascontiguousarray(x[c].T))
               for c in range(NCORE)]
    res = run_bass_kernel_spmd(nc, in_maps, core_ids=list(range(NCORE)))
    es.close()
    out = np.stack([np.asarray(res.results[c]["xo"]).T
                    for c in range(NCORE)], axis=0)
    return np.ascontiguousarray(out, dtype=np.float32)


# version 19
# speedup vs baseline: 1.1533x; 1.1533x over previous
import contextlib
import math
import numpy as np

import concourse.bass as bass
import concourse.mybir as mybir
from concourse.bass_utils import run_bass_kernel_spmd

F32 = mybir.dt.float32
BF16 = mybir.dt.bfloat16
U8 = mybir.dt.uint8
I32 = mybir.dt.int32
AF = mybir.ActivationFunctionType
OP = mybir.AluOpType
AX = mybir.AxisListType

NCORE = 8
T = 4096
D = 1024
LN_EPS = 1e-5

ENG2BLOCK = {"sp": "sync", "act": "scalar", "dve": "vector",
             "pool": "gpsimd", "pe": "tensor"}


class Prog:

    NDMASEM = 48

    def __init__(self, nc, es):
        self.nc = nc
        self.es = es
        self.sems = {e: es.enter_context(nc.semaphore("sem_" + e))
                     for e in ENG2BLOCK}
        self.cnt = {e: 0 for e in ENG2BLOCK}
        self.dsems = [es.enter_context(nc.semaphore(f"dsem{i}"))
                      for i in range(self.NDMASEM)]
        self.dcum = [0] * self.NDMASEM
        self._dma_pool = {"sp": list(range(0, 24)),
                          "pool": list(range(24, 36)),
                          "act": list(range(36, 44)),
                          "dve": list(range(44, 46)),
                          "pe": list(range(46, 48))}
        self._dma_rri = {k: 0 for k in self._dma_pool}
        self.thunks = {e: [] for e in ENG2BLOCK}
        self.deps = {}
        self.wm = {e: {} for e in ENG2BLOCK}
        self.names = set()
        self.regions = []
        self._cpool = self.sbuf("k_cpool", [128, 16], F32)
        self._cvstg = (self.sbuf("g_cvs", [128, 1024], F32),
                       self.sbuf("g_cvb", [128, 1024], BF16))
        self._cvstg2 = (self.sbuf("g_cvs2", [128, 1024], F32),
                        self.sbuf("g_cvb2", [128, 1024], BF16))
        self._cvals = {}

    def cconst(self, value):
        if value in self._cvals:
            return self._cvals[value]
        i = len(self._cvals)
        assert i < 16
        ap = self._cpool[:, i:i + 1]
        key = ("cconst", i)
        self.emit("dve", lambda E, ap=ap, value=value: E.memset(ap, value),
                  writes=[key])
        self._cvals[value] = (ap, key)
        return self._cvals[value]

    def sbuf(self, name, shape, dtype):
        name = getattr(self, "prefix", "") + name
        assert name not in self.names, name
        self.names.add(name)
        st = self.regions[-1] if self.regions else self.es
        return st.enter_context(self.nc.sbuf_tensor(name, shape, dtype))

    def psum(self, name, shape, dtype=F32):
        name = getattr(self, "prefix", "") + name
        assert name not in self.names, name
        self.names.add(name)
        st = self.regions[-1] if self.regions else self.es
        return st.enter_context(self.nc.psum_tensor(name, shape, dtype))

    def dram(self, name, shape, dtype=F32, kind="Internal"):
        name = getattr(self, "prefix", "") + name
        return self.nc.dram_tensor(name, list(shape), dtype, kind=kind).ap()

    def push_region(self):
        self.regions.append(contextlib.ExitStack())
        if not hasattr(self, "_region_marks"):
            self._region_marks = []
        self._region_marks.append((dict(self.cnt), list(self.dcum)))

    def pop_region(self):
        cnt0, dcum0 = self._region_marks.pop()
        for e in ENG2BLOCK:
            for s in ENG2BLOCK:
                if s != e and self.cnt[s] > cnt0[s]:
                    self._wait(e, s, self.cnt[s])
            for i in range(self.NDMASEM):
                if self.dcum[i] > dcum0[i]:
                    self._wait(e, ("d", i), self.dcum[i])
        self.regions.pop().close()

    def _wait(self, eng, src, val):
        if val <= 0:
            return
        if eng == "pe" and src == "pe":
            return
        if self.wm[eng].get(src, 0) >= val:
            return
        self.wm[eng][src] = val
        if getattr(self, "log", None) is not None:
            self._last_waits = getattr(self, "_last_waits", [])
            self._last_waits.append((eng, src, val))
        sem = self.dsems[src[1]] if isinstance(src, tuple) else self.sems[src]
        self.thunks[eng].append(lambda E, sem=sem, val=val: E.wait_ge(sem, val))

    def _dep_waits(self, eng, reads, writes):
        for k in reads:
            d = self.deps.get(k)
            if d and d["w"]:
                self._wait(eng, *d["w"])
        for k in writes:
            d = self.deps.get(k)
            if d:
                if d["w"]:
                    self._wait(eng, *d["w"])
                for src, c in d["r"].items():
                    self._wait(eng, src, c)

    def _record(self, reads, writes, dep):
        src, c = dep
        for k in reads:
            d = self.deps.setdefault(k, {"w": None, "r": {}})
            d["r"][src] = c
        for k in writes:
            self.deps[k] = {"w": dep, "r": {}}

    def emit(self, eng, fn, reads=(), writes=()):
        self._log_pre(eng)
        self._dep_waits(eng, reads, writes)
        self.cnt[eng] += 1
        c = self.cnt[eng]
        sem = self.sems[eng]

        def thunk(E, fn=fn, sem=sem):
            fn(E).then_inc(sem, 1)

        self.thunks[eng].append(thunk)
        self._record(reads, writes, (eng, c))
        self._log(eng, c, reads, writes)
        return c

    def _log_pre(self, eng):
        if getattr(self, "log", None) is not None:
            self._log_mark = len(self.thunks[eng])

    def _log(self, eng, c, reads, writes):
        if getattr(self, "log", None) is None:
            return
        import traceback
        fr = traceback.extract_stack(limit=6)
        site = "|".join(f"{f.filename.split('/')[-1]}:{f.lineno}"
                        for f in fr[:-2][-3:])
        waits = tuple(getattr(self, "_last_waits", []))
        self._last_waits = []
        self.log.append((eng, c, tuple(reads), tuple(writes), waits, site))

    def dma(self, eng, fns, reads=(), writes=()):
        if callable(fns):
            fns = [fns]
        self._log_pre(eng)
        pool = self._dma_pool[eng]
        j = self._dma_rri[eng]
        self._dma_rri[eng] = (j + 1) % len(pool)
        i = pool[j]
        sem = self.dsems[i]
        self._wait(eng, ("d", i), self.dcum[i])
        self._dep_waits(eng, reads, writes)
        self.dcum[i] += 16 * len(fns)
        target = self.dcum[i]

        def thunk(E, fns=tuple(fns), sem=sem):
            for f in fns:
                f(E).then_inc(sem, 16)

        self.thunks[eng].append(thunk)
        self._record(reads, writes, (("d", i), target))
        self._log(eng, ("d", i, target), reads, writes)
        return (("d", i), target)

    def barrier(self, eng, srcs=None):
        for s in (srcs or ENG2BLOCK):
            if s != eng:
                self._wait(eng, s, self.cnt[s])
        if srcs is None:
            for i in range(self.NDMASEM):
                self._wait(eng, ("d", i), self.dcum[i])

    def finalize(self):
        for i in range(self.NDMASEM):
            self._wait("sp", ("d", i), self.dcum[i])
        for e in ENG2BLOCK:
            if self.cnt[e] > 0:
                sem, val = self.sems[e], self.cnt[e]
                self.thunks[e].append(
                    lambda E, sem=sem, val=val: E.wait_ge(sem, val))
        block = self.es.enter_context(self.nc.Block())
        for e, bname in ENG2BLOCK.items():
            fns = self.thunks[e]
            if not fns:
                continue
            deco = getattr(block, bname)

            def body(eng, fns=fns):
                for f in fns:
                    f(eng)
            deco(body)



def host_constants():
    import ml_dtypes
    c = {}
    c["c_ident"] = np.eye(128, dtype=ml_dtypes.bfloat16)
    c["c_ident_f32"] = np.eye(128, dtype=np.float32)
    c["c_ones_f32"] = np.ones((128, 128), dtype=np.float32)
    tri = np.zeros((128, 4 * 512), dtype=ml_dtypes.bfloat16)
    for j in range(4):
        kk = np.arange(128)[:, None]
        qq = np.arange(512)[None, :]
        tri[:, j * 512:(j + 1) * 512] = (kk + 128 * j <= qq).astype(
            ml_dtypes.bfloat16)
    c["c_tri"] = tri
    sel = np.zeros((16, 16 * 128), dtype=ml_dtypes.bfloat16)
    for e in range(16):
        sel[e, e * 128:(e + 1) * 128] = 1.0
    c["c_sel"] = sel
    sel32 = np.zeros((32, 32 * 128), dtype=np.float32)
    for i in range(32):
        sel32[i, i * 128:(i + 1) * 128] = 1.0
    c["c_sel32"] = sel32
    return c


def declare_constants(P):
    nc = P.nc
    cts = {}
    cts["c_ident"] = nc.dram_tensor("c_ident", [128, 128], BF16,
                                    kind="ExternalInput").ap()
    cts["c_ones_f32"] = nc.dram_tensor("c_ones_f32", [128, 128], F32,
                                       kind="ExternalInput").ap()
    cts["c_ident_f32"] = nc.dram_tensor("c_ident_f32", [128, 128], F32,
                                        kind="ExternalInput").ap()
    cts["c_tri"] = nc.dram_tensor("c_tri", [128, 4 * 512], BF16,
                                  kind="ExternalInput").ap()
    cts["c_sel"] = nc.dram_tensor("c_sel", [16, 16 * 128], BF16,
                                  kind="ExternalInput").ap()
    cts["c_sel32"] = nc.dram_tensor("c_sel32", [32, 32 * 128], F32,
                                    kind="ExternalInput").ap()
    sb = {}
    sb["ident"] = P.sbuf("k_ident", [128, 128], BF16)
    sb["ones_f32"] = P.sbuf("k_ones_f32", [128, 128], F32)
    sb["ident_f32"] = P.sbuf("k_ident_f32", [128, 128], F32)
    sb["tri"] = P.sbuf("k_tri", [128, 4 * 512], BF16)
    sb["sel"] = P.sbuf("k_sel", [16, 16 * 128], BF16)
    for k, hbm in cts.items():
        key = k.replace("c_", "")
        if key not in sb:
            continue
        P.dma("sp", lambda E, d=sb[key], s=hbm: E.dma_start(out=d[:], in_=s),
              writes=[("const", key)])
    sb["sel32_hbm"] = cts["c_sel32"]
    return sb


def load_sel32(P, consts):
    consts["sel32"] = P.sbuf("k_sel32", [32, 32 * 128], F32)
    P.dma("sp", lambda E, d=consts["sel32"], s=consts["sel32_hbm"]:
              E.dma_start(out=d[:], in_=s),
          writes=[("const", "sel32")])



def emit_ln(P, name, x_hbm, xn_sb, consts, TB=512, xbuf=None,
            xn_f32_hbm=None):
    nt = T // TB
    if xbuf is None:
        xbuf = P.sbuf(name + "_xb", [128, 2 * 8 * TB], F32)
    sq = P.sbuf(name + "_sq", [128, TB], F32)
    mu = P.sbuf(name + "_mu", [128, 2 * TB], F32)
    rstd = P.sbuf(name + "_rstd", [128, 2 * TB], F32)
    var = P.sbuf(name + "_var", [128, TB], F32)
    ps_s = P.psum(name + "_ps_s", [128, TB], F32)
    ps_q = P.psum(name + "_ps_q", [128, TB], F32)
    ones = consts["ones_f32"]
    eps_ap, eps_key = P.cconst(LN_EPS)
    for tb in range(nt):
        buf = tb % 2
        xks = [xbuf[:, (buf * 8 + k) * TB:(buf * 8 + k + 1) * TB]
               for k in range(8)]
        P.dma("sp",
              [(lambda E, d=xks[k], s=x_hbm[k * 128:(k + 1) * 128,
                                            tb * TB:(tb + 1) * TB]:
                    E.dma_start(out=d, in_=s)) for k in range(8)],
              writes=[(name, "xb", buf, k) for k in range(8)])
        for k in range(8):
            P.emit("act",
                   lambda E, xk=xks[k]: E.activation(sq[:], xk, AF.Square),
                   reads=[(name, "xb", buf, k)], writes=[(name, "sq")])
            P.emit("pe",
                   lambda E, xk=xks[k], st=(k == 0), sp=(k == 7):
                       E.matmul(ps_s[:], ones[:], xk, start=st, stop=sp),
                   reads=[(name, "xb", buf, k), ("const", "ones_f32")],
                   writes=[(name, "ps_s")] if k in (0, 7) else [])
            P.emit("pe",
                   lambda E, st=(k == 0), sp=(k == 7):
                       E.matmul(ps_q[:], ones[:], sq[:], start=st, stop=sp),
                   reads=[(name, "sq"), ("const", "ones_f32")],
                   writes=[(name, "ps_q")] if k in (0, 7) else [])
        mub = mu[:, buf * TB:(buf + 1) * TB]
        rstdb = rstd[:, buf * TB:(buf + 1) * TB]
        P.emit("act", lambda E, mub=mub: E.activation(mub, ps_s[:], AF.Copy,
                                                      scale=1.0 / D),
               reads=[(name, "ps_s")], writes=[(name, "mu", buf)])
        P.emit("dve", lambda E, mub=mub: E.tensor_tensor(var[:], mub, mub,
                                                         OP.mult),
               reads=[(name, "mu", buf)], writes=[(name, "var")])
        P.emit("dve", lambda E: E.scalar_tensor_tensor(
                   var[:], ps_q[:], 1.0 / D, var[:], OP.mult, OP.subtract),
               reads=[(name, "ps_q"), (name, "var")], writes=[(name, "var")])
        P.emit("act", lambda E: E.activation(var[:], var[:], AF.Ln,
                                             bias=eps_ap),
               reads=[(name, "var"), eps_key], writes=[(name, "var")])
        P.emit("act", lambda E, rstdb=rstdb: E.activation(rstdb, var[:],
                                                          AF.Exp, scale=-0.5),
               reads=[(name, "var")], writes=[(name, "rstd", buf)])
        for k in range(8):
            xnk = xn_sb[:, k * T + tb * TB: k * T + (tb + 1) * TB]
            P.emit("dve",
                   lambda E, xk=xks[k], mub=mub: E.tensor_sub(sq[:], xk, mub),
                   reads=[(name, "xb", buf, k), (name, "mu", buf)],
                   writes=[(name, "sq")])
            P.emit("dve",
                   lambda E, xnk=xnk, rstdb=rstdb:
                       E.tensor_mul(xnk, sq[:], rstdb),
                   reads=[(name, "sq"), (name, "rstd", buf)],
                   writes=[(name, "xn", k)])
            if xn_f32_hbm is not None:
                P.emit("dve",
                       lambda E, rstdb=rstdb: E.tensor_mul(sq[:], sq[:], rstdb),
                       reads=[(name, "sq"), (name, "rstd", buf)],
                       writes=[(name, "sq")])
                P.dma("sp",
                      lambda E, d=xn_f32_hbm[k * 128:(k + 1) * 128,
                                             tb * TB:(tb + 1) * TB]:
                          E.dma_start(out=d, in_=sq[:]),
                      reads=[(name, "sq")], writes=[(name, "xn32")])
    return xn_sb



class FMT:

    def __init__(self, P, name, nfeat, Tl, dtype):
        assert nfeat % 128 == 0
        self.P, self.name, self.nk, self.Tl, self.dtype = P, name, nfeat // 128, Tl, dtype
        self.sb = P.sbuf(name, [128, self.nk * Tl], dtype)

    def tile(self, k, a=0, b=None):
        b = self.Tl if b is None else b
        return self.sb[:, k * self.Tl + a: k * self.Tl + b]

    def key(self, k):
        return (self.name, k)

    def keys(self):
        return [(self.name, k) for k in range(self.nk)]


class PsRing:

    def __init__(self, P, name, n, shape, dtype=F32):
        self.ts = [P.psum(f"{name}{i}", shape, dtype) for i in range(n)]
        self.keys = [(name, i) for i in range(n)]
        self.n = n
        self.i = 0

    def get(self):
        i = self.i
        self.i = (self.i + 1) % self.n
        return self.ts[i], self.keys[i]


class WRing:

    def __init__(self, P, name, n, shape, dtype=BF16, q="sp"):
        self.ts = [P.sbuf(f"{name}{i}", shape, dtype) for i in range(n)]
        self.keys = [(name, i) for i in range(n)]
        self.n = n
        self.i = 0
        self.P = P
        self.q = q

    def load(self, src_ap, rows=None, cols=None, reads=()):
        i = self.i
        self.i = (self.i + 1) % self.n
        if hasattr(self, "slot2key"):
            old = self.slot2key.pop(i, None)
            if old is not None:
                self.cache.pop(old, None)
        r = rows if rows is not None else src_ap.shape[0]
        c = cols if cols is not None else src_ap.shape[1]
        dst = self.ts[i][0:r, 0:c]
        self.P.dma(self.q, lambda E, d=dst, s=src_ap: E.dma_start(out=d, in_=s),
                   reads=list(reads), writes=[self.keys[i]])
        return dst, self.keys[i], i

    def load_cached(self, ckey, src_ap, reads=()):
        if not hasattr(self, "slot2key"):
            self.slot2key, self.cache = {}, {}
        hit = self.cache.get(ckey)
        if hit is not None:
            return hit
        ap, dkey, slot = self.load(src_ap, reads=reads)
        self.slot2key[slot] = ckey
        self.cache[ckey] = (ap, dkey)
        return ap, dkey


def emit_bf16_scratch(P, name, src_hbm, rows, cols, eng=None, CW=None,
                      dmaq=None):
    dst = P.dram(name + "_bf", (rows, cols), BF16)
    total = rows * cols
    src_f = src_hbm.rearrange("a b -> (a b)")
    dst_f = dst.rearrange("a b -> (a b)")
    SP = 1 << 21
    fns = []
    for o0 in range(0, total, SP):
        n = min(SP, total - o0)
        fns.append(lambda E, d=dst_f[o0:o0 + n], s=src_f[o0:o0 + n]:
                   E.dma_start(out=d, in_=s))
    for i in range(0, len(fns), 4):
        P.dma("pool", fns[i:i + 4], writes=[(name, "out")])
    return dst, (name, "out")


def emit_gemm(P, name, xtile, xkey, Kt, wscr, F, ws, ps_ring, epi,
              f0=0, t0=0, t1=T, TMAC=1024, NSUB=512, extra_mm=None,
              fm_order=None):
    WWIN = ws.ts[0].shape[1]
    wsrc, wsrckey = wscr
    Ftot = wsrc.shape[1]
    wid = (wsrc.name, wsrc.offset)
    for ta in range(t0, t1, TMAC):
        tb = min(ta + TMAC, t1)
        for fm in (fm_order if fm_order is not None else range(F // 128)):
            ps, pskey = ps_ring.get()
            if extra_mm is not None:
                extra_mm(ps, pskey, fm, ta, tb)
            col = f0 + fm * 128
            w0 = (col // WWIN) * WWIN
            w1 = min(w0 + WWIN, Ftot)
            for k in range(Kt):
                w_ap, wkey = ws.load_cached(
                    (wsrckey, wid, k, w0),
                    wsrc[k * 128:(k + 1) * 128, w0:w1],
                    reads=[wsrckey])
                lhsT = w_ap[:, col - w0:col - w0 + 128]
                for ts0 in range(ta, tb, NSUB):
                    ts1 = min(ts0 + NSUB, tb)
                    st = (k == 0 and extra_mm is None)
                    sp = (k == Kt - 1)
                    xap = xtile(k, ts0, ts1)
                    xdep = xkey(k)
                    P.emit("pe",
                           lambda E, ps=ps, w=lhsT, x=xap,
                                  a=ts0 - ta, bnd=ts1 - ta, st=st, sp=sp:
                               E.matmul(ps[:, a:bnd], w, x, start=st, stop=sp),
                           reads=[wkey] + ([xdep] if xdep else []),
                           writes=[pskey] if (k == 0 or k == Kt - 1) else [])
            epi(ps, pskey, fm, ta, tb)



ATTN_C = 30.0


def emit_attention(P, name, xn, wqkv_scr, bqkv, consts, ctx_fm, nheads=16,
                   dhead=64, head_stride=None, rope=None, qb=512,
                   qk_fm=None, qk_hbm=None, v_off=2):
    nc = P.nc
    q_h = FMT(P, name + "_qh", 128, T, BF16)
    k_h = FMT(P, name + "_kh", 128, T, BF16)
    v_h = FMT(P, name + "_vh", 128, T, BF16)
    v_tm = P.sbuf(name + "_vtm", [128, 32 * 65], BF16)
    l_sb = P.sbuf(name + "_l", [1, qb], F32)
    linv = P.sbuf(name + "_linv", [1, qb], F32)
    ctx32 = P.sbuf(name + "_c32", [64, qb], F32)
    ws = WRing(P, name + "_ws", 9, [128, 3 * dhead])
    psP = PsRing(P, name + "_psP", 2, [128, qb])
    psS = PsRing(P, name + "_psS", 3, [128, qb])
    psC = PsRing(P, name + "_psC", 2, [128, qb])
    negc, negc_key = P.cconst(-ATTN_C)
    ptb = [P.sbuf(name + "_pt0", [128, qb], BF16),
           P.sbuf(name + "_pt1", [128, qb], BF16)]
    ones_col = v_tm[:].rearrange("p (i c) -> p i c", c=65)[:, :, 64:65]
    P.emit("dve", lambda E, a=ones_col: E.memset(a, 1.0),
           writes=[(name, "vtm_ones")])

    for h in range(nheads):
        hs = h * (head_stride if head_stride is not None else 3 * dhead)
        if qk_hbm is not None:
            Qh, Kh, qdep, kdep = qk_hbm
            P.dma("sp", lambda E, d=q_h.tile(0)[0:dhead, :],
                         s=Qh[h * dhead:(h + 1) * dhead, :]:
                      E.dma_start(out=d, in_=s),
                  reads=[qdep], writes=[q_h.key(0)])
            P.dma("sp", lambda E, d=k_h.tile(0)[0:dhead, :],
                         s=Kh[h * dhead:(h + 1) * dhead, :]:
                      E.dma_start(out=d, in_=s),
                  reads=[kdep], writes=[k_h.key(0)])
            def qap(a, b, q_h=q_h):
                return q_h.tile(0, a, b)[0:dhead, :]
            def kap(a, b, k_h=k_h):
                return k_h.tile(0, a, b)[0:dhead, :]
            qkey, kkey = q_h.key(0), k_h.key(0)
            projs = ((v_off, v_h),)
        elif qk_fm is None:
            projs = ((0, q_h), (1, k_h), (v_off, v_h))
            def qap(a, b, q_h=q_h):
                return q_h.tile(0, a, b)[0:dhead, :]
            def kap(a, b, k_h=k_h):
                return k_h.tile(0, a, b)[0:dhead, :]
            qkey, kkey = q_h.key(0), k_h.key(0)
        else:
            Qf, Kf = qk_fm
            ct0 = (h * dhead) // 128
            co0 = (h * dhead) % 128
            def qap(a, b, Qf=Qf, ct0=ct0, co0=co0):
                return Qf.tile(ct0, a, b)[co0:co0 + dhead, :]
            def kap(a, b, Kf=Kf, ct0=ct0, co0=co0):
                return Kf.tile(ct0, a, b)[co0:co0 + dhead, :]
            qkey, kkey = Qf.key(ct0), Kf.key(ct0)
            projs = ((v_off, v_h),)
        for j, dst in projs:
            for tb in range(0, T, qb):
                ps, pskey = psP.get()
                wsrc, wsrckey = wqkv_scr
                for k in range(8):
                    w_ap, wkey = ws.load_cached(
                        (wsrckey, h, j, k),
                        wsrc[k * 128:(k + 1) * 128, hs + j * dhead:
                             hs + (j + 1) * dhead], reads=[wsrckey])
                    P.emit("pe",
                           lambda E, ps=ps, w=w_ap, x=xn.tile(k, tb, tb + qb),
                                  st=(k == 0), sp=(k == 7):
                               E.matmul(ps[0:dhead, :], w, x, start=st, stop=sp),
                           reads=[wkey, xn.key(k)],
                           writes=[pskey] if k in (0, 7) else [])
                if bqkv is not None:
                    bsb, bbase, bkey = bqkv
                    col = (hs + j * dhead) // 128 + bbase
                    off = (hs + j * dhead) % 128
                    P.emit("dve",
                           lambda E, d=dst.tile(0, tb, tb + qb), s=ps,
                                  b=bsb[off:off + dhead, col:col + 1]:
                               E.tensor_scalar_add(d[0:dhead, :],
                                                   s[0:dhead, :], b),
                           reads=[pskey, bkey], writes=[dst.key(0)])
                else:
                    P.emit("act",
                           lambda E, d=dst.tile(0, tb, tb + qb), s=ps:
                               E.activation(d[0:dhead, :], s[0:dhead, :], AF.Copy),
                           reads=[pskey], writes=[dst.key(0)])
        for i in range(32):
            ps, pskey = psS.get()
            psbf = ps[:, 0:32].bitcast(BF16)
            P.emit("pe",
                   lambda E, ps=psbf, s=v_h.tile(0, i * 128, (i + 1) * 128):
                       E.transpose(ps, s[0:64, :], consts["ident"][0:64, 0:64]),
                   reads=[v_h.key(0), ("const", "ident")], writes=[pskey])
            P.emit("act",
                   lambda E, d=v_tm[:, i * 65:i * 65 + 64], s=psbf:
                       E.activation(d, s, AF.Copy),
                   reads=[pskey], writes=[(name, "vtm", i)])
        for qbi in range(T // qb):
            nkt = (qbi + 1) * (qb // 128)
            ctxps, ctxkey = psC.get()
            smm = {}

            def score(kt, qbi=qbi, smm=smm):
                ps, pskey = psS.get()
                smm[kt] = (ps, pskey)
                P.emit("pe",
                       lambda E, ps=ps, w=kap(kt * 128, (kt + 1) * 128),
                              x=qap(qbi * qb, (qbi + 1) * qb):
                           E.matmul(ps[:], w, x, start=True, stop=True),
                       reads=[kkey, qkey], writes=[pskey])

            def ctx(kt, qbi=qbi, nkt=nkt, ctxps=ctxps, ctxkey=ctxkey,
                    smm=smm):
                ps, pskey = smm.pop(kt)
                pt = ptb[kt % 2]
                ptkey = (name, "pt", kt % 2)
                P.emit("act",
                       lambda E, d=pt[:], s=ps: E.activation(
                           d, s[:], AF.Exp, bias=negc, scale=0.125),
                       reads=[pskey, negc_key], writes=[ptkey])
                j = kt - 4 * qbi
                if j >= 0:
                    P.emit("dve",
                           lambda E, d=pt[:],
                                  m=consts["tri"][:, j * 512:(j + 1) * 512]:
                               E.tensor_mul(d, d, m),
                           reads=[ptkey, ("const", "tri")], writes=[ptkey])
                P.emit("pe",
                       lambda E, ps=ctxps, w=v_tm[:, kt * 65:(kt + 1) * 65],
                              x=pt[:], st=(kt == 0), sp=(kt == nkt - 1):
                           E.matmul(ps[0:65, :], w, x, start=st, stop=sp),
                       reads=[(name, "vtm", kt), (name, "vtm_ones"), ptkey],
                       writes=[ctxkey] if kt in (0, nkt - 1) else [])

            for kt in range(nkt + 1):
                if kt < nkt:
                    score(kt)
                if kt >= 1:
                    ctx(kt - 1)
            P.emit("act", lambda E, s=ctxps: E.activation(
                       l_sb[:], s[64:65, :], AF.Ln),
                   reads=[ctxkey], writes=[(name, "l")])
            P.emit("act", lambda E: E.activation(linv[:], l_sb[:], AF.Exp,
                                                 scale=-1.0),
                   reads=[(name, "l")], writes=[(name, "linv")])
            P.emit("act", lambda E, s=ctxps: E.activation(
                       ctx32[:], s[0:64, :], AF.Copy),
                   reads=[ctxkey], writes=[(name, "c32")])
            ps, pskey = psS.get()
            P.emit("pe", lambda E, ps=ps: E.matmul(
                       ps[0:64, :], consts["ones_f32"][0:1, 0:64], linv[:],
                       start=True, stop=True),
                   reads=[(name, "linv"), ("const", "ones_f32")],
                   writes=[pskey])
            ct = h // 2
            coff = (h % 2) * 64
            P.emit("dve",
                   lambda E, d=ctx_fm.tile(ct, qbi * qb, (qbi + 1) * qb),
                          s=ps, coff=coff:
                       E.tensor_mul(d[coff:coff + 64, :], ctx32[:], s[0:64, :]),
                   reads=[pskey, (name, "c32")], writes=[ctx_fm.key(ct)])



def emit_router_l1(P, name, xn, rw_bf, consts, ind_fm, w32):
    rw_sb = P.sbuf(name + "_rw", [128, 8 * 16], BF16)
    rw_ap, rw_key = rw_bf
    for k in range(8):
        P.dma("sp", lambda E, d=rw_sb[:, k * 16:(k + 1) * 16],
                     s=rw_ap[k * 128:(k + 1) * 128, :]: E.dma_start(out=d, in_=s),
              reads=[rw_key], writes=[(name, "rw", k)])
    ltm = P.sbuf(name + "_ltm", [128, 16], F32)
    m = P.sbuf(name + "_m", [128, 1], F32)
    w_tm = P.sbuf(name + "_wtm", [128, 32], F32)
    ind_bf = P.sbuf(name + "_indbf", [128, 16], BF16)
    psR = PsRing(P, name + "_psR", 2, [128, 16])
    psTi = PsRing(P, name + "_psTi", 2, [16, 128], BF16)
    for tt in range(32):
        ps, pskey = psR.get()
        for k in range(8):
            P.emit("pe",
                   lambda E, ps=ps, w=xn.tile(k, tt * 128, (tt + 1) * 128),
                          x=rw_sb[:, k * 16:(k + 1) * 16], st=(k == 0), sp=(k == 7):
                       E.matmul(ps[:], w, x, start=st, stop=sp),
                   reads=[xn.key(k), (name, "rw", k)],
                   writes=[pskey] if k in (0, 7) else [])
        P.emit("act", lambda E, s=ps: E.activation(ltm[:], s[:], AF.Copy),
               reads=[pskey], writes=[(name, "ltm")])
        P.emit("dve", lambda E: E.reduce_max(m[:], ltm[:], AX.X),
               reads=[(name, "ltm")], writes=[(name, "m")])
        P.emit("act", lambda E, d=w_tm[:, tt:tt + 1]: E.activation(
                   d, m[:], AF.Sigmoid),
               reads=[(name, "m")], writes=[(name, "wtm")])
        P.emit("dve", lambda E: E.tensor_single_scalar(
                   ind_bf[:], ltm[:], m[:, 0:1], OP.is_ge),
               reads=[(name, "ltm"), (name, "m")], writes=[(name, "indbf")])
        psT, psTkey = psTi.get()
        P.emit("pe", lambda E, ps=psT: E.transpose(
                   ps[:], ind_bf[:], consts["ident"][:]),
               reads=[(name, "indbf"), ("const", "ident")], writes=[psTkey])
        P.emit("act", lambda E, d=ind_fm[0:16, tt * 128:(tt + 1) * 128], s=psT:
                   E.activation(d, s[:], AF.Copy),
               reads=[psTkey], writes=[(name, "ind_fm")])
    psWT = P.psum(name + "_psWT", [32, 128], F32)
    P.emit("pe", lambda E: E.transpose(psWT[:], w_tm[:],
                                       consts["ident_f32"][:]),
           reads=[(name, "wtm"), ("const", "ident_f32")],
           writes=[(name, "psWT")])
    P.emit("act", lambda E: E.activation(w32[:], psWT[:], AF.Copy),
           reads=[(name, "psWT")], writes=[(name, "w32")])


def emit_moe_l1(P, name, xn, w1_bf, w2_bf, b1_sb, b1_key, b2_sb, b2_key,
                ind_fm, w32, consts, acc, w_rep, TMAC=1024):
    hh = FMT(P, name + "_hh", 2048, TMAC, BF16)
    mask = P.sbuf(name + "_mask", [128, TMAC], BF16)
    ind32 = P.sbuf(name + "_ind32", [16, TMAC], F32)
    ws = WRing(P, name + "_ws", 18, [128, 1024], q="pool")
    psA = PsRing(P, name + "_psA", 2, [128, TMAC])
    psB = PsRing(P, name + "_psB", 2, [128, 512])
    psM = PsRing(P, name + "_psM", 2, [128, 512])

    def run(finish):
        for tm in range(0, T, TMAC):
            P.emit("dve", lambda E: E.memset(acc.sb[:], 0.0),
                   writes=[acc.key(k) for k in range(acc.nk)])
            P.emit("act", lambda E, s=ind_fm[0:16, tm:tm + TMAC]:
                       E.activation(ind32[:], s, AF.Copy),
                   reads=[("l1r", "ind_fm")], writes=[(name, "ind32")])
            for j in range(TMAC // 128):
                tt = (tm // 128) + j
                ps, pskey = psM.get()
                P.emit("pe", lambda E, ps=ps,
                              s=consts["sel32"][:, tt * 128:(tt + 1) * 128]:
                           E.matmul(ps[:, 0:128], s, w32[:],
                                    start=True, stop=True),
                       reads=[("l1r", "w32"), ("const", "sel32")],
                       writes=[pskey])
                P.emit("act", lambda E, d=w_rep[:, j * 128:(j + 1) * 128], s=ps:
                           E.activation(d, s[:, 0:128], AF.Copy),
                       reads=[pskey], writes=[(name, "w_rep")])
            for e in range(16):
                for half in range(TMAC // 512):
                    ps, pskey = psM.get()
                    P.emit("pe",
                           lambda E, ps=ps,
                                  s=consts["sel"][:, e * 128:(e + 1) * 128],
                                  x=ind_fm[0:16, tm + half * 512:tm + (half + 1) * 512]:
                               E.matmul(ps[:], s, x, start=True, stop=True),
                           reads=[("l1r", "ind_fm"), ("const", "sel")],
                           writes=[pskey])
                    P.emit("act",
                           lambda E, d=mask[:, half * 512:(half + 1) * 512], s=ps:
                               E.activation(d, s[:], AF.Copy),
                           reads=[pskey], writes=[(name, "mask")])
                def epiA(ps, pskey, fm, ta, tb, e=e):
                    P.emit("act",
                           lambda E, d=hh.tile(fm), s=ps,
                                  b=b1_sb[:, e * 16 + fm:e * 16 + fm + 1]:
                               E.activation(d, s[:], AF.Gelu, bias=b),
                           reads=[pskey, b1_key], writes=[hh.key(fm)])
                    P.emit("dve",
                           lambda E, d=hh.tile(fm): E.tensor_mul(d, d, mask[:]),
                           reads=[hh.key(fm), (name, "mask")],
                           writes=[hh.key(fm)])
                emit_gemm(P, name + "A", lambda k, a, b: xn.tile(k, a, b),
                          xn.key, 8,
                          (w1_bf[0][e * 1024:(e + 1) * 1024, :], w1_bf[1]),
                          2048, ws, psA, epiA, t0=tm, t1=tm + TMAC, TMAC=TMAC)
                def extraB(ps, pskey, fm, ta, tb, tm=tm, e=e):
                    if e != 0:
                        return
                    P.emit("pe",
                           lambda E, ps=ps, w=b2_sb[:, fm * 128:(fm + 1) * 128],
                                  x=ind32[0:16, ta - tm:tb - tm]:
                               E.matmul(ps[:, 0:tb - ta], w, x,
                                        start=True, stop=False),
                           reads=[b2_key, (name, "ind32")], writes=[pskey])

                def epiB(ps, pskey, fm, ta, tb):
                    P.emit("dve",
                           lambda E, d=acc.tile(fm, ta - tm, tb - tm), s=ps:
                               E.tensor_add(d, d, s[:, 0:tb - ta]),
                           reads=[pskey, acc.key(fm)], writes=[acc.key(fm)])
                emit_gemm(P, name + "B",
                          lambda k, a, b: hh.tile(k, a - tm, b - tm), hh.key,
                          16,
                          (w2_bf[0][e * 2048:(e + 1) * 2048, :], w2_bf[1]),
                          1024, ws, psB, epiB, t0=tm, t1=tm + TMAC, TMAC=512,
                          extra_mm=None if e != 0 else extraB)
            finish(tm)
    return run


def emit_layer1_pre(P):
    f = lambda n, shape: P.dram(n, shape, F32, kind="ExternalInput")
    pre = {}
    pre["W1_bf"] = emit_bf16_scratch(P, "W1s", f("W1", (D, 3 * D)), D, 3 * D)
    pre["W2_bf"] = emit_bf16_scratch(P, "W2s", f("W2", (D, D)), D, D)
    pre["rw_bf"] = emit_bf16_scratch(P, "rws", f("rw", (D, 16)), D, 16)
    pre["w1_bf"] = emit_bf16_scratch(P, "w1s", f("w1", (16 * D, 2048)),
                                     16 * D, 2048)
    pre["w2_bf"] = emit_bf16_scratch(P, "w2s", f("w2", (16 * 2048, D)),
                                     16 * 2048, D)
    return pre


def emit_layer1(P, x_h, xo_h, consts, pre, debug=False):
    nc = P.nc
    f = lambda n, shape: P.dram(n, shape, F32, kind="ExternalInput")
    b1a_h = f("b1a_h", (128, 24))
    b2a_h = f("b2a_h", (128, 8))
    b1m_h = f("b1m_h", (128, 256))
    b2m_h = f("b2m_h", (16, D))
    s_h = P.dram("s_park", (D, T), F32,
                 kind="ExternalOutput" if debug else "Internal")
    if debug:
        dbg_ind = P.dram("dbg_ind", (16, T), BF16, kind="ExternalOutput")
        dbg_w = P.dram("dbg_w", (32, 128), F32, kind="ExternalOutput")
    W1_bf, W2_bf, rw_bf, w1_bf, w2_bf = (pre["W1_bf"], pre["W2_bf"],
                                         pre["rw_bf"], pre["w1_bf"],
                                         pre["w2_bf"])

    b1a_sb = P.sbuf("b1a", [128, 24], F32)
    P.dma("sp", lambda E: E.dma_start(out=b1a_sb[:], in_=b1a_h),
          writes=[("b1a",)])
    b2a_sb = P.sbuf("b2a", [128, 8], F32)
    P.dma("sp", lambda E: E.dma_start(out=b2a_sb[:], in_=b2a_h),
          writes=[("b2a",)])
    b1m_sb = P.sbuf("b1m", [128, 256], F32)
    P.dma("sp", lambda E: E.dma_start(out=b1m_sb[:], in_=b1m_h),
          writes=[("b1m",)])
    b2m_sb = P.sbuf("b2m", [16, D], F32)
    P.dma("sp", lambda E: E.dma_start(out=b2m_sb[:], in_=b2m_h),
          writes=[("b2m",)])

    xn = FMT(P, "xn", D, T, BF16)
    ind_fm = P.sbuf("ind_fm", [16, T], BF16)
    w32 = P.sbuf("w32g", [32, 128], F32)

    P.push_region()
    emit_ln(P, "ln", x_h, xn.sb, consts)
    P.pop_region()

    P.push_region()
    emit_router_l1(P, "l1r", xn, rw_bf, consts, ind_fm, w32)
    P.pop_region()
    if debug:
        P.dma("sp", lambda E: E.dma_start(out=dbg_ind, in_=ind_fm[:]),
              reads=[("l1r", "ind_fm")])
        P.dma("sp", lambda E: E.dma_start(out=dbg_w, in_=w32[:]),
              reads=[("l1r", "w32")])

    P.push_region()
    ctx_fm = FMT(P, "ctx", D, T, BF16)
    P.push_region()
    emit_attention(P, "att", xn, W1_bf, (b1a_sb, 0, ("b1a",)), consts, ctx_fm,
                   nheads=16, dhead=64, head_stride=192)
    P.pop_region()
    s32 = P.sbuf("s32", [128, 512], F32)
    wsO = WRing(P, "wsO", 9, [128, 512], q="pool")
    psO = PsRing(P, "psO", 2, [128, 512])

    def epiO(ps, pskey, fm, ta, tb):
        P.emit("dve",
               lambda E, s=ps, b=b2a_sb[:, fm:fm + 1]:
                   E.tensor_scalar_add(s32[:], s[:], b),
               reads=[pskey, ("b2a",)], writes=[("s32",)])
        P.dma("sp", lambda E, d=s_h[fm * 128:(fm + 1) * 128, ta:tb]:
                  E.dma_start(out=d, in_=s32[:]),
              reads=[("s32",)])
    emit_gemm(P, "outp", lambda k, a, b: ctx_fm.tile(k, a, b), ctx_fm.key,
              8, W2_bf, D, wsO, psO, epiO, TMAC=512)
    P.pop_region()

    P.push_region()
    load_sel32(P, consts)
    TMAC = 512
    acc = FMT(P, "acc", D, TMAC, F32)
    w_rep = P.sbuf("w_rep", [128, TMAC], F32)
    xc = P.sbuf("xc", [128, 512], F32)
    sc = P.sbuf("sc", [128, 512], F32)
    gc = P.sbuf("gc", [128, 512], F32)

    def finish(tm):
        for fm in range(8):
            for c0 in range(tm, tm + TMAC, 512):
                P.dma("sp", lambda E, d=xc[:],
                             s=x_h[fm * 128:(fm + 1) * 128, c0:c0 + 512]:
                          E.dma_start(out=d, in_=s), writes=[("xc",)])
                P.dma("sp", lambda E, d=sc[:],
                             s=s_h[fm * 128:(fm + 1) * 128, c0:c0 + 512]:
                          E.dma_start(out=d, in_=s), writes=[("sc",)])
                P.emit("dve",
                       lambda E, a=acc.tile(fm, c0 - tm, c0 - tm + 512),
                              w=w_rep[:, c0 - tm:c0 - tm + 512]:
                           E.tensor_mul(gc[:], a, w),
                       reads=[acc.key(fm), ("moe", "w_rep")], writes=[("gc",)])
                P.emit("act", lambda E: E.activation(gc[:], gc[:], AF.Silu),
                       reads=[("gc",)], writes=[("gc",)])
                P.emit("dve", lambda E: E.tensor_mul(gc[:], gc[:], sc[:]),
                       reads=[("gc",), ("sc",)], writes=[("gc",)])
                P.emit("dve", lambda E: E.tensor_add(gc[:], gc[:], xc[:]),
                       reads=[("gc",), ("xc",)], writes=[("gc",)])
                P.dma("sp", lambda E, d=xo_h[fm * 128:(fm + 1) * 128, c0:c0 + 512]:
                          E.dma_start(out=d, in_=gc[:]),
                      reads=[("gc",)], writes=[("xo", fm, c0)])
    run = emit_moe_l1(P, "moe", xn, w1_bf, w2_bf, b1m_sb, ("b1m",),
                      b2m_sb, ("b2m",), ind_fm, w32, consts, acc, w_rep,
                      TMAC=TMAC)
    run(finish)
    P.pop_region()


def host_prep_layer1(inputs):
    c = host_constants()
    base = dict(
        W1=np.ascontiguousarray(inputs["L1_attn_W1"]),
        b1a_h=np.ascontiguousarray(inputs["L1_attn_b1"].reshape(24, 128).T),
        W2=np.ascontiguousarray(inputs["L1_attn_W2"]),
        b2a_h=np.ascontiguousarray(inputs["L1_attn_b2"].reshape(8, 128).T),
        rw=np.ascontiguousarray(inputs["L1_ffn_router_w"]),
        w1=np.ascontiguousarray(inputs["L1_ffn_w1"].reshape(16 * 1024, 2048)),
        b1m_h=np.ascontiguousarray(
            inputs["L1_ffn_b1"].reshape(16, 16, 128).transpose(2, 0, 1)
            .reshape(128, 256)),
        w2=np.ascontiguousarray(inputs["L1_ffn_w2"].reshape(16 * 2048, 1024)),
        b2m_h=np.ascontiguousarray(inputs["L1_ffn_b2"]),
        **c)
    return base



def emit_rope_qk(P, name, xn, w_scr, b_sb, b_key, out_hbm, sincos, consts,
                 TBR=512):
    sin_h, cos_h = sincos
    ws = WRing(P, name + "_ws", 3, [128, 128])
    ps = PsRing(P, name + "_ps", 2, [128, TBR])
    raw = P.sbuf(name + "_raw", [128, 8 * TBR], F32)
    trig = P.sbuf(name + "_trig", [128, 2 * 4 * TBR], F32)
    m1 = P.sbuf(name + "_m1", [128, TBR], F32)
    m2 = P.sbuf(name + "_m2", [128, TBR], F32)
    rot = P.sbuf(name + "_rot", [128, 8 * TBR], BF16)
    for c0 in range(0, T, TBR):
        P.dma("sp",
              [(lambda E, d=trig[:, (j * 4 + i) * TBR:(j * 4 + i + 1) * TBR],
                       s=(sin_h if j == 0 else cos_h)[i * 128:(i + 1) * 128,
                                                      c0:c0 + TBR]:
                    E.dma_start(out=d, in_=s))
               for j in range(2) for i in range(4)],
              writes=[(name, "trig")])
        def epiP(psap, pskey, fm, ta, tb):
            P.emit("dve",
                   lambda E, d=raw[:, fm * TBR:(fm + 1) * TBR], s=psap,
                          b=b_sb[:, fm:fm + 1]:
                       E.tensor_scalar_add(d, s[:], b),
                   reads=[pskey, b_key], writes=[(name, "raw", fm)])
        emit_gemm(P, name + "P", lambda k, a, b: xn.tile(k, a, b), xn.key,
                  8, w_scr, 1024, ws, ps, epiP, t0=c0, t1=c0 + TBR, TMAC=TBR)
        for i in range(4):
            e_ap = raw[:, i * TBR:(i + 1) * TBR]
            o_ap = raw[:, (i + 4) * TBR:(i + 5) * TBR]
            s_ap = trig[:, i * TBR:(i + 1) * TBR]
            c_ap = trig[:, (4 + i) * TBR:(5 + i) * TBR]
            t1k, t2k = (name, "m1"), (name, "m2")
            P.emit("dve", lambda E, a=e_ap, b=c_ap: E.tensor_mul(m1[:], a, b),
                   reads=[(name, "raw", i), (name, "trig")], writes=[t1k])
            P.emit("pool", lambda E, a=o_ap, b=s_ap: E.tensor_mul(m2[:], a, b),
                   reads=[(name, "raw", i + 4), (name, "trig")], writes=[t2k])
            P.emit("dve",
                   lambda E, d=rot[:, i * TBR:(i + 1) * TBR]:
                       E.tensor_sub(d, m1[:], m2[:]),
                   reads=[t1k, t2k], writes=[(name, "rot", i)])
            P.emit("dve", lambda E, a=e_ap, b=s_ap: E.tensor_mul(m1[:], a, b),
                   reads=[(name, "raw", i), (name, "trig"), t1k], writes=[t1k])
            P.emit("pool", lambda E, a=o_ap, b=c_ap: E.tensor_mul(m2[:], a, b),
                   reads=[(name, "raw", i + 4), (name, "trig"), t2k],
                   writes=[t2k])
            P.emit("dve",
                   lambda E, d=rot[:, (i + 4) * TBR:(i + 5) * TBR]:
                       E.tensor_sub(d, m1[:], m2[:]),
                   reads=[t1k, t2k], writes=[(name, "rot", i + 4)])
        P.dma("sp",
              [(lambda E, d=out_hbm[i * 128:(i + 1) * 128, c0:c0 + TBR],
                       s=rot[:, i * TBR:(i + 1) * TBR]:
                    E.dma_start(out=d, in_=s)) for i in range(8)],
              reads=[(name, "rot", i) for i in range(8)],
              writes=[(name, "park")])


def emit_router_l2(P, name, xn, wg_bf, we_bf, gb_h, eb_h, consts, wse_fm):
    wall = P.sbuf(name + "_wall", [128, 10 * 8], BF16)
    for k in range(8):
        P.dma("sp", [
            (lambda E, d=wall[:, k * 10:k * 10 + 2],
                    s=wg_bf[0][k * 128:(k + 1) * 128, :]:
                 E.dma_start(out=d, in_=s)),
            (lambda E, d=wall[:, k * 10 + 2:k * 10 + 10],
                    s=we_bf[0][k * 128:(k + 1) * 128, :]:
                 E.dma_start(out=d, in_=s))], reads=[wg_bf[1], we_bf[1]],
            writes=[(name, "wall", k)])
    gbeb = P.sbuf(name + "_gbeb", [1, 10], F32)
    P.dma("sp", [lambda E: E.dma_start(out=gbeb[0:1, 0:2], in_=gb_h),
                 lambda E: E.dma_start(out=gbeb[0:1, 2:10], in_=eb_h)],
          writes=[(name, "gbeb")])
    gbeb_rep = P.sbuf(name + "_gbebr", [128, 10], F32)
    psb = P.psum(name + "_psb", [128, 16])
    P.emit("pe", lambda E: E.matmul(psb[:, 0:10], consts["ones_f32"][0:1, :],
                                    gbeb[:], start=True, stop=True),
           reads=[(name, "gbeb"), ("const", "ones_f32")],
           writes=[(name, "psb")])
    P.emit("act", lambda E: E.activation(gbeb_rep[:], psb[:, 0:10], AF.Copy),
           reads=[(name, "psb")], writes=[(name, "gbebr")])
    gl = P.sbuf(name + "_gl", [128, 2], F32)
    el = P.sbuf(name + "_el", [128, 8], F32)
    mg = P.sbuf(name + "_mg", [128, 1], F32)
    tmp8 = P.sbuf(name + "_tmp8", [128, 8], F32)
    tmp8b = P.sbuf(name + "_t8b", [128, 8], F32)
    sc1 = P.sbuf(name + "_sc1", [128, 1], F32)
    wse_bf = P.sbuf(name + "_wsebf", [128, 8], BF16)
    psR = PsRing(P, name + "_psR", 2, [128, 16])
    psT = PsRing(P, name + "_psT", 2, [8, 128], BF16)
    for tt in range(32):
        ps, pskey = psR.get()
        for k in range(8):
            P.emit("pe",
                   lambda E, ps=ps, w=xn.tile(k, tt * 128, (tt + 1) * 128),
                          x=wall[:, k * 10:(k + 1) * 10], st=(k == 0),
                          sp=(k == 7):
                       E.matmul(ps[:, 0:10], w, x, start=st, stop=sp),
                   reads=[xn.key(k), (name, "wall", k)],
                   writes=[pskey] if k in (0, 7) else [])
        P.emit("dve", lambda E, s=ps: E.tensor_add(
                   gl[:], s[:, 0:2], gbeb_rep[:, 0:2]),
               reads=[pskey, (name, "gbebr")], writes=[(name, "gl")])
        P.emit("dve", lambda E, s=ps: E.tensor_add(
                   el[:], s[:, 2:10], gbeb_rep[:, 2:10]),
               reads=[pskey, (name, "gbebr")], writes=[(name, "el")])
        P.emit("dve", lambda E: E.reduce_max(mg[:], gl[:], AX.X),
               reads=[(name, "gl")], writes=[(name, "mg")])
        P.emit("dve", lambda E: E.tensor_scalar(
                   gl[:], gl[:], mg[:, 0:1], None, OP.subtract),
               reads=[(name, "gl"), (name, "mg")], writes=[(name, "gl")])
        P.emit("act", lambda E: E.activation(gl[:], gl[:], AF.Exp,
                                             accum_out=sc1[:, 0:1]),
               reads=[(name, "gl")], writes=[(name, "gl"), (name, "sc1")])
        P.emit("act", lambda E: E.activation(sc1[:], sc1[:], AF.Ln),
               reads=[(name, "sc1")], writes=[(name, "sc1")])
        P.emit("act", lambda E: E.activation(sc1[:], sc1[:], AF.Exp,
                                             scale=-1.0),
               reads=[(name, "sc1")], writes=[(name, "sc1")])
        P.emit("dve", lambda E: E.tensor_single_scalar(
                   gl[:], gl[:], 1.0, OP.is_ge),
               reads=[(name, "gl")], writes=[(name, "gl")])
        P.emit("dve", lambda E: E.reduce_max(mg[:], el[:], AX.X),
               reads=[(name, "el")], writes=[(name, "mg")])
        P.emit("dve", lambda E: E.tensor_scalar(
                   tmp8[:], el[:], mg[:, 0:1], None, OP.subtract),
               reads=[(name, "el"), (name, "mg")], writes=[(name, "tmp8")])
        P.emit("act", lambda E: E.activation(tmp8[:], tmp8[:], AF.Exp),
               reads=[(name, "tmp8")], writes=[(name, "tmp8")])
        P.emit("dve", lambda E,
                      m3=gl[:].unsqueeze(2).broadcast_to([128, 2, 4]),
                      o3=tmp8[:].rearrange("p (g j) -> p g j", g=2):
                   E.tensor_tensor(o3, o3, m3, OP.mult),
               reads=[(name, "tmp8"), (name, "gl")], writes=[(name, "tmp8")])
        P.emit("dve", lambda E: E.reduce_sum(mg[:].unsqueeze(2),
                                             tmp8[:].unsqueeze(1), AX.X),
               reads=[(name, "tmp8")], writes=[(name, "mg")])
        P.emit("act", lambda E: E.activation(mg[:], mg[:], AF.Ln),
               reads=[(name, "mg")], writes=[(name, "mg")])
        P.emit("act", lambda E: E.activation(mg[:], mg[:], AF.Exp, scale=-1.0),
               reads=[(name, "mg")], writes=[(name, "mg")])
        P.emit("dve", lambda E: E.tensor_mul(sc1[:], sc1[:], mg[:]),
               reads=[(name, "sc1"), (name, "mg")], writes=[(name, "sc1")])
        P.emit("dve", lambda E: E.tensor_scalar(
                   tmp8[:], tmp8[:], sc1[:, 0:1], None, OP.mult),
               reads=[(name, "tmp8"), (name, "sc1")], writes=[(name, "tmp8")])
        P.emit("dve", lambda E: E.reduce_max(mg[:], tmp8[:], AX.X),
               reads=[(name, "tmp8")], writes=[(name, "mg")])
        P.emit("dve", lambda E: E.tensor_single_scalar(
                   tmp8b[:], tmp8[:], mg[:, 0:1], OP.is_lt),
               reads=[(name, "tmp8"), (name, "mg")], writes=[(name, "t8b")])
        P.emit("dve", lambda E: E.tensor_mul(tmp8b[:], tmp8b[:], tmp8[:]),
               reads=[(name, "t8b"), (name, "tmp8")], writes=[(name, "t8b")])
        P.emit("dve", lambda E: E.reduce_max(mg[:], tmp8b[:], AX.X),
               reads=[(name, "t8b")], writes=[(name, "mg")])
        P.emit("dve", lambda E: E.tensor_single_scalar(
                   tmp8b[:], tmp8[:], mg[:, 0:1], OP.is_ge),
               reads=[(name, "tmp8"), (name, "mg")], writes=[(name, "t8b")])
        P.emit("dve", lambda E: E.tensor_mul(tmp8b[:], tmp8b[:], tmp8[:]),
               reads=[(name, "t8b"), (name, "tmp8")], writes=[(name, "t8b")])
        P.emit("dve", lambda E: E.tensor_copy(wse_bf[:], tmp8b[:]),
               reads=[(name, "t8b")], writes=[(name, "wsebf")])
        psTp, psTkey = psT.get()
        P.emit("pe", lambda E, ps=psTp: E.transpose(
                   ps[:], wse_bf[:], consts["ident"][:]),
               reads=[(name, "wsebf"), ("const", "ident")], writes=[psTkey])
        P.emit("act", lambda E, d=wse_fm[0:8, tt * 128:(tt + 1) * 128], s=psTp:
                   E.activation(d, s[:], AF.Copy),
               reads=[psTkey], writes=[(name, "wse_fm")])


def emit_layer2_pre(P):
    f = lambda n, shape: P.dram(n, shape, F32, kind="ExternalInput")
    pre = {}
    pre["Wq_bf"] = emit_bf16_scratch(P, "Wqs", f("Wq", (D, D)), D, D)
    pre["Wk_bf"] = emit_bf16_scratch(P, "Wks", f("Wk", (D, D)), D, D)
    pre["Wv_bf"] = emit_bf16_scratch(P, "Wvs", f("Wv", (D, D)), D, D)
    pre["Wo_bf"] = emit_bf16_scratch(P, "Wos", f("Wo", (D, D)), D, D)
    pre["w_in_bf"] = emit_bf16_scratch(P, "wins", f("w_in", (D, 4096)),
                                       D, 4096)
    pre["w_out_bf"] = emit_bf16_scratch(P, "wouts", f("w_out", (2048, D)),
                                        2048, D)
    pre["w1_bf"] = emit_bf16_scratch(P, "w1s", f("w1", (D, 4096)), D, 4096)
    pre["w2_bf"] = emit_bf16_scratch(P, "w2s", f("w2", (8 * 2048, D)),
                                     8 * 2048, D)
    pre["wg_bf"] = emit_bf16_scratch(P, "wgs", f("wg", (D, 2)), D, 2)
    pre["we_bf"] = emit_bf16_scratch(P, "wes", f("we", (D, 8)), D, 8)
    return pre


def emit_layer2(P, x_h, xo_h, consts, pre, debug=False):
    nc = P.nc
    f = lambda n, shape: P.dram(n, shape, F32, kind="ExternalInput")
    bq_h = f("bq", (128, 8))
    bk_h = f("bk", (128, 8))
    bv_h = f("bv", (128, 8))
    bo_h = f("bo", (128, 8))
    gb_h = f("gb", (1, 2))
    eb_h = f("eb", (1, 8))
    sin_h = f("c_sin", (512, T))
    cos_h = f("c_cos", (512, T))
    s_h = P.dram("s_park", (D, T), F32,
                 kind="ExternalOutput" if debug else "Internal")
    if debug:
        dbg_wse = P.dram("dbg_wse", (8, T), BF16, kind="ExternalOutput")
    (Wq_bf, Wk_bf, Wv_bf, Wo_bf, w_in_bf, w_out_bf, w1_bf, w2_bf, wg_bf,
     we_bf) = (pre["Wq_bf"], pre["Wk_bf"], pre["Wv_bf"], pre["Wo_bf"],
               pre["w_in_bf"], pre["w_out_bf"], pre["w1_bf"], pre["w2_bf"],
               pre["wg_bf"], pre["we_bf"])

    bsb = {}
    for nm, hb in (("bq", bq_h), ("bk", bk_h), ("bv", bv_h), ("bo", bo_h)):
        bsb[nm] = P.sbuf("sb_" + nm, [128, 8], F32)
        P.dma("sp", lambda E, d=bsb[nm], s=hb: E.dma_start(out=d[:], in_=s),
              writes=[(nm,)])

    xn = FMT(P, "xn", D, T, BF16)
    wse_fm = P.sbuf("wse_fm", [8, T], BF16)

    P.push_region()
    emit_ln(P, "ln", x_h, xn.sb, consts)
    P.pop_region()

    P.push_region()
    emit_router_l2(P, "l2r", xn, wg_bf, we_bf, gb_h, eb_h, consts, wse_fm)
    P.pop_region()
    if debug:
        P.dma("sp", lambda E: E.dma_start(out=dbg_wse, in_=wse_fm[:]),
              reads=[("l2r", "wse_fm")])

    P.push_region()
    Qr_h = P.dram("Qr_park", (D, T), BF16)
    Kr_h = P.dram("Kr_park", (D, T), BF16)
    P.push_region()
    emit_rope_qk(P, "ropq", xn, Wq_bf, bsb["bq"], ("bq",), Qr_h,
                 (sin_h, cos_h), consts)
    P.pop_region()
    P.push_region()
    emit_rope_qk(P, "ropk", xn, Wk_bf, bsb["bk"], ("bk",), Kr_h,
                 (sin_h, cos_h), consts)
    P.pop_region()
    ctx_fm = FMT(P, "ctx", D, T, BF16)
    P.push_region()
    emit_attention(P, "att", xn, Wv_bf, (bsb["bv"], 0, ("bv",)), consts,
                   ctx_fm, nheads=16, dhead=64, head_stride=64,
                   qk_hbm=(Qr_h, Kr_h, ("ropq", "park"), ("ropk", "park")),
                   v_off=0)
    P.pop_region()
    s32 = P.sbuf("s32", [128, 512], F32)
    wsO = WRing(P, "wsO", 9, [128, 512], q="pool")
    psO = PsRing(P, "psO", 2, [128, 512])

    def epiO(ps, pskey, fm, ta, tb):
        P.emit("dve",
               lambda E, s=ps, b=bsb["bo"][:, fm:fm + 1]:
                   E.tensor_scalar_add(s32[:], s[:], b),
               reads=[pskey, ("bo",)], writes=[("s32",)])
        P.dma("sp", lambda E, d=s_h[fm * 128:(fm + 1) * 128, ta:tb]:
                  E.dma_start(out=d, in_=s32[:]),
              reads=[("s32",)])
    emit_gemm(P, "outp", lambda k, a, b: ctx_fm.tile(k, a, b), ctx_fm.key,
              8, Wo_bf, D, wsO, psO, epiO, TMAC=512)
    P.pop_region()

    P.push_region()
    TMAC = 512
    acc = FMT(P, "acc", D, TMAC, F32)
    hsh = FMT(P, "hsh", 2048, TMAC, BF16)
    hrt = FMT(P, "hrt", 2048, TMAC, BF16)
    hmk = hsh
    mask = P.sbuf("l2mask", [128, TMAC], BF16)
    swi = P.sbuf("l2swi", [128, TMAC], F32)
    ws = WRing(P, "l2ws", 18, [128, 1024], q="pool")
    psA = PsRing(P, "l2psA", 2, [128, TMAC])
    psB = PsRing(P, "l2psB", 2, [128, 512])
    psM = PsRing(P, "l2psM", 2, [128, 512])
    xc = P.sbuf("xc", [128, 512], F32)
    sc = P.sbuf("sc", [128, 512], F32)
    gc = P.sbuf("gc", [128, 512], F32)

    SWIGLU_ORDER = [v for i in range(16) for v in (i, i + 16)]

    def swiglu_epi(hdst):
        def epi(ps, pskey, fm, ta, tb, hdst=hdst):
            if fm < 16:
                P.emit("act", lambda E, s=ps,
                              d=swi[:, 0:tb - ta]:
                           E.activation(d, s[:, 0:tb - ta], AF.Silu),
                       reads=[pskey], writes=[("l2swi",)])
            else:
                P.emit("dve",
                       lambda E, s=ps, d=hdst.tile(fm - 16, 0, tb - ta),
                              a=swi[:, 0:tb - ta]:
                           E.tensor_mul(d, a, s[:, 0:tb - ta]),
                       reads=[pskey, ("l2swi",)],
                       writes=[hdst.key(fm - 16)])
        return epi

    def run_moe(finish):
        for tm in range(0, T, TMAC):
            P.emit("dve", lambda E: E.memset(acc.sb[:], 0.0),
                   writes=[acc.key(k) for k in range(acc.nk)])
            emit_gemm(P, "l2sh", lambda k, a, b: xn.tile(k, a, b), xn.key,
                      8, w_in_bf, 4096, ws, psA, swiglu_epi(hsh),
                      t0=tm, t1=tm + TMAC, TMAC=TMAC, fm_order=SWIGLU_ORDER)
            emit_gemm(P, "l2rt", lambda k, a, b: xn.tile(k, a, b), xn.key,
                      8, w1_bf, 4096, ws, psA, swiglu_epi(hrt),
                      t0=tm, t1=tm + TMAC, TMAC=TMAC, fm_order=SWIGLU_ORDER)
            def epiAcc(ps, pskey, fm, ta, tb):
                P.emit("dve",
                       lambda E, d=acc.tile(fm, ta - tm, tb - tm), s=ps:
                           E.tensor_add(d, d, s[:, 0:tb - ta]),
                       reads=[pskey, acc.key(fm)], writes=[acc.key(fm)])
            emit_gemm(P, "l2so", lambda k, a, b: hsh.tile(k, a - tm, b - tm),
                      hsh.key, 16, w_out_bf, D, ws, psB, epiAcc,
                      t0=tm, t1=tm + TMAC, TMAC=512)
            for e in range(8):
                for half in range(TMAC // 512):
                    ps, pskey = psM.get()
                    P.emit("pe",
                           lambda E, ps=ps,
                                  s=consts["sel"][0:8, e * 128:(e + 1) * 128],
                                  x=wse_fm[0:8, tm + half * 512:
                                           tm + (half + 1) * 512]:
                               E.matmul(ps[:], s, x, start=True, stop=True),
                           reads=[("l2r", "wse_fm"), ("const", "sel")],
                           writes=[pskey])
                    P.emit("act",
                           lambda E, d=mask[:, half * 512:(half + 1) * 512],
                                  s=ps:
                               E.activation(d, s[:], AF.Copy),
                           reads=[pskey], writes=[("l2mask",)])
                for fm in range(16):
                    P.emit("dve",
                           lambda E, d=hmk.tile(fm), s=hrt.tile(fm):
                               E.tensor_mul(d, s, mask[:]),
                           reads=[hrt.key(fm), ("l2mask",)],
                           writes=[hmk.key(fm)])
                emit_gemm(P, "l2e", lambda k, a, b: hmk.tile(k, a - tm, b - tm),
                          hmk.key, 16,
                          (w2_bf[0][e * 2048:(e + 1) * 2048, :], w2_bf[1]),
                          D, ws, psB, epiAcc, t0=tm, t1=tm + TMAC, TMAC=512)
            finish(tm)

    def finish(tm):
        for fm in range(8):
            for c0 in range(tm, tm + TMAC, 512):
                P.dma("sp", lambda E, d=xc[:],
                             s=x_h[fm * 128:(fm + 1) * 128, c0:c0 + 512]:
                          E.dma_start(out=d, in_=s), writes=[("xc",)])
                P.dma("sp", lambda E, d=sc[:],
                             s=s_h[fm * 128:(fm + 1) * 128, c0:c0 + 512]:
                          E.dma_start(out=d, in_=s), writes=[("sc",)])
                P.emit("act",
                       lambda E, a=acc.tile(fm, c0 - tm, c0 - tm + 512):
                           E.activation(gc[:], a, AF.Silu),
                       reads=[acc.key(fm)], writes=[("gc",)])
                P.emit("dve", lambda E: E.tensor_mul(gc[:], gc[:], sc[:]),
                       reads=[("gc",), ("sc",)], writes=[("gc",)])
                P.emit("dve", lambda E: E.tensor_add(gc[:], gc[:], xc[:]),
                       reads=[("gc",), ("xc",)], writes=[("gc",)])
                P.dma("sp", lambda E, d=xo_h[fm * 128:(fm + 1) * 128,
                                             c0:c0 + 512]:
                          E.dma_start(out=d, in_=gc[:]),
                      reads=[("gc",)])
    run_moe(finish)
    P.pop_region()


def host_prep_layer2(inputs):
    c = host_constants()
    perm = np.concatenate([np.arange(0, D, 2), np.arange(1, D, 2)])
    fr = np.arange(D // 2, dtype=np.float64)
    fc = np.power(10000.0, -2.0 * fr / D)
    pos = np.arange(T, dtype=np.float64)
    fr2 = np.outer(pos, fc)
    sin_t = np.sin(fr2).astype(np.float32).T.copy()
    cos_t = np.cos(fr2).astype(np.float32).T.copy()
    colsb = lambda v: np.ascontiguousarray(v.reshape(8, 128).T)
    base = dict(
        Wq=np.ascontiguousarray(inputs["L2_attn_Wq"][:, perm]),
        bq=colsb(inputs["L2_attn_bq"][perm]),
        Wk=np.ascontiguousarray(inputs["L2_attn_Wk"][:, perm]),
        bk=colsb(inputs["L2_attn_bk"][perm]),
        Wv=np.ascontiguousarray(inputs["L2_attn_Wv"]),
        bv=colsb(inputs["L2_attn_bv"]),
        Wo=np.ascontiguousarray(inputs["L2_attn_Wo"]),
        bo=colsb(inputs["L2_attn_bo"]),
        w_in=np.ascontiguousarray(inputs["L2_ffn_w_in"].T),
        w_out=np.ascontiguousarray(inputs["L2_ffn_w_out"].T),
        w1=np.ascontiguousarray(inputs["L2_ffn_w1"].T),
        w2=np.ascontiguousarray(
            inputs["L2_ffn_w2"].transpose(0, 2, 1).reshape(8 * 2048, D)),
        wg=np.ascontiguousarray(inputs["L2_ffn_wg"].T),
        we=np.ascontiguousarray(inputs["L2_ffn_we"].T),
        gb=np.ascontiguousarray(inputs["L2_ffn_gb"].reshape(1, 2)),
        eb=np.ascontiguousarray(inputs["L2_ffn_eb"].reshape(1, 8)),
        c_sin=sin_t, c_cos=cos_t,
        **c)
    return base



def make_hbm_xtile(P, name, hbm_ap, dep, nslots=10, W=512, dtype=BF16):
    ring = WRing(P, name + "_xr", nslots, [128, W], dtype)
    cache = {}
    lastk = {}

    def xtile(k, a, b):
        key = (k, a, b)
        if key in cache:
            lastk[k] = cache[key][1]
            return cache[key][0]
        ap, rkey, _slot = ring.load(hbm_ap[k * 128:(k + 1) * 128, a:b], reads=[dep])
        for kk, (aap, rk) in list(cache.items()):
            if rk == rkey:
                del cache[kk]
        cache[key] = (ap, rkey)
        lastk[k] = rkey
        return ap

    def xkey(k):
        return lastk.get(k)

    return xtile, xkey


def emit_rms(P, name, src_hbm, src_dep, w_sb, w_key, out_fm, consts,
             eps=1e-6, TB=512, out_hbm=None, out32_hbm=None):
    xb = P.sbuf(name + "_xb", [128, 8 * TB], F32)
    sq = P.sbuf(name + "_sq", [128, TB], F32)
    rstd = P.sbuf(name + "_rstd", [128, TB], F32)
    ob = P.sbuf(name + "_ob", [128, TB], F32) if (out_hbm or out32_hbm) else None
    obb = P.sbuf(name + "_obb", [128, TB], BF16) if out_hbm else None
    ps = P.psum(name + "_ps", [128, TB], F32)
    eps_ap, eps_key = P.cconst(eps)
    ones = consts["ones_f32"]
    for tb in range(T // TB):
        P.dma("sp",
              [(lambda E, d=xb[:, k * TB:(k + 1) * TB],
                       s=src_hbm[k * 128:(k + 1) * 128, tb * TB:(tb + 1) * TB]:
                    E.dma_start(out=d, in_=s)) for k in range(8)],
              reads=[src_dep], writes=[(name, "xb")])
        for k in range(8):
            P.emit("act", lambda E, s=xb[:, k * TB:(k + 1) * TB]:
                       E.activation(sq[:], s, AF.Square),
                   reads=[(name, "xb")], writes=[(name, "sq")])
            P.emit("pe", lambda E, st=(k == 0), sp=(k == 7):
                       E.matmul(ps[:], ones[:], sq[:],
                                start=st, stop=sp),
                   reads=[(name, "sq"), ("const", "ones_f32")],
                   writes=[(name, "ps")] if k in (0, 7) else [])
        P.emit("act", lambda E: E.activation(rstd[:], ps[:], AF.Ln,
                                             scale=1.0 / D, bias=eps_ap),
               reads=[(name, "ps"), eps_key], writes=[(name, "rstd")])
        P.emit("act", lambda E: E.activation(rstd[:], rstd[:], AF.Exp,
                                             scale=-0.5),
               reads=[(name, "rstd")], writes=[(name, "rstd")])
        for k in range(8):
            dsts = []
            if out_fm is not None:
                dsts.append((out_fm.tile(k, tb * TB, (tb + 1) * TB),
                             out_fm.key(k), True))
            srcap = xb[:, k * TB:(k + 1) * TB]
            if w_sb is not None:
                P.emit("dve", lambda E, s=srcap: E.tensor_mul(sq[:], s, rstd[:]),
                       reads=[(name, "xb"), (name, "rstd")],
                       writes=[(name, "sq")])
                for dap, dkey, _ in dsts:
                    P.emit("dve",
                           lambda E, d=dap, w=w_sb[:, k:k + 1]:
                               E.tensor_scalar_mul(d, sq[:], w),
                           reads=[(name, "sq"), w_key], writes=[dkey])
                if ob is not None:
                    P.emit("dve",
                           lambda E, w=w_sb[:, k:k + 1]:
                               E.tensor_scalar_mul(ob[:], sq[:], w),
                           reads=[(name, "sq"), w_key], writes=[(name, "ob")])
            else:
                for dap, dkey, _ in dsts:
                    P.emit("dve", lambda E, d=dap, s=srcap:
                               E.tensor_mul(d, s, rstd[:]),
                           reads=[(name, "xb"), (name, "rstd")], writes=[dkey])
                if ob is not None:
                    P.emit("dve", lambda E, s=srcap:
                               E.tensor_mul(ob[:], s, rstd[:]),
                           reads=[(name, "xb"), (name, "rstd")],
                           writes=[(name, "ob")])
            if out32_hbm is not None:
                P.dma("sp", lambda E, d=out32_hbm[k * 128:(k + 1) * 128,
                                                  tb * TB:(tb + 1) * TB]:
                          E.dma_start(out=d, in_=ob[:]),
                      reads=[(name, "ob")])
            if out_hbm is not None:
                P.emit("dve", lambda E: E.tensor_copy(obb[:], ob[:]),
                       reads=[(name, "ob")], writes=[(name, "obb")])
                P.dma("sp", lambda E, d=out_hbm[k * 128:(k + 1) * 128,
                                                tb * TB:(tb + 1) * TB]:
                          E.dma_start(out=d, in_=obb[:]),
                      reads=[(name, "obb")])


def emit_gla(P, name, h_fm, scr, cols, consts, o_hbm):
    CH = 64
    NH = 4
    dk, dv = 128, 256
    q_h = P.sbuf(name + "_qh", [128, T], BF16)
    k_h = P.sbuf(name + "_kh", [128, T], BF16)
    v_h = FMT(P, name + "_vh", 256, T, BF16)
    g_h = FMT(P, name + "_gh", 256, T, BF16)
    v_tm = P.sbuf(name + "_vtm", [64, 64 * 256], BF16)
    t1 = P.sbuf(name + "_t1", [16, T], BF16)
    glog = P.sbuf(name + "_glog", [128, T], F32)
    gl32 = P._cvstg[0][:, 0:512]
    gabs = P._cvstg[0][:, 512:1024]
    S32 = P.sbuf(name + "_S", [128, 256], F32)
    Sbf = P.sbuf(name + "_Sbf", [128, 256], BF16)
    onesc = P.sbuf(name + "_ones", [128, CH], F32)
    bchunk = P.sbuf(name + "_b", [128, CH], F32)
    ebs = P.sbuf(name + "_ebs", [128, CH], F32)
    qe = P.sbuf(name + "_qe", [128, CH], BF16)
    ke = P.sbuf(name + "_ke", [128, CH], BF16)
    kd = P.sbuf(name + "_kd", [128, CH], BF16)
    kdtm = P.sbuf(name + "_kdtm", [64, 128], BF16)
    atb = P.sbuf(name + "_atb", [64, CH], BF16)
    ebl = P.sbuf(name + "_ebl", [128, 1], F32)
    osq = P.sbuf(name + "_osq", [64, 256], F32)
    ossq = P.sbuf(name + "_ossq", [64, 1], F32)
    o32 = P.sbuf(name + "_o32", [64, 256], F32)
    gsil = P.sbuf(name + "_gsil", [64, 256], F32)
    otm = P.sbuf(name + "_otm", [64, 256], BF16)
    obf = P.sbuf(name + "_obf", [128, 2 * CH], BF16)
    gnr = P.sbuf(name + "_gnr", [128, 256], F32)
    ws = WRing(P, name + "_ws", 10, [128, 128])
    psP = PsRing(P, name + "_psP", 2, [128, 512])
    psA = PsRing(P, name + "_psA", 2, [64, 64])
    psO = PsRing(P, name + "_psO", 2, [64, 256])
    psSd = PsRing(P, name + "_psSd", 2, [128, 256])
    lns = math.log(dk ** -0.5)
    lns_ap, lns_key = P.cconst(lns)
    P.emit("dve", lambda E: E.memset(onesc[:], 1.0), writes=[(name, "onesc")])
    psg, psgkey = psP.get()
    P.emit("pe", lambda E, ps=psg: E.matmul(
               ps[:, 0:256], consts["ones_f32"][0:1, :], cols["gnorm_row"],
               start=True, stop=True),
           reads=[("gnorm_row",), ("const", "ones_f32")], writes=[psgkey])
    P.emit("act", lambda E, s=psg: E.activation(gnr[:], s[:, 0:256], AF.Copy),
           reads=[psgkey], writes=[(name, "gnr")])

    for h in range(NH):
        def proj(dst_ap_fn, dkey, wname, f0):
            for tb in range(0, T, 512):
                ps, pskey = psP.get()
                wsrc, wsrckey = scr[wname]
                for k in range(8):
                    w_ap, wkey = ws.load_cached(
                        (wsrckey, f0, k),
                        wsrc[k * 128:(k + 1) * 128, f0:f0 + 128],
                        reads=[wsrckey])
                    P.emit("pe",
                           lambda E, ps=ps, w=w_ap,
                                  x=h_fm.tile(k, tb, tb + 512),
                                  st=(k == 0), sp=(k == 7):
                               E.matmul(ps[:], w, x, start=st, stop=sp),
                           reads=[wkey, h_fm.key(k)],
                           writes=[pskey] if k in (0, 7) else [])
                dst_ap_fn(ps, pskey, tb)
        for nm, dst, dkey in (("Wq", q_h, (name, "qh")),
                              ("Wk", k_h, (name, "kh"))):
            def mk(dst=dst, dkey=dkey):
                def f(ps, pskey, tb):
                    P.emit("act", lambda E, d=dst[:, tb:tb + 512], s=ps:
                               E.activation(d, s[:], AF.Copy),
                           reads=[pskey], writes=[dkey])
                return f
            proj(mk(), dkey, nm, h * 128)
        for nm, dst in (("Wv", v_h), ("Wg", g_h)):
            for fi in range(2):
                def mkf(dst=dst, fi=fi):
                    def f(ps, pskey, tb):
                        P.emit("act",
                               lambda E, d=dst.tile(fi, tb, tb + 512), s=ps:
                                   E.activation(d, s[:], AF.Copy),
                               reads=[pskey], writes=[dst.key(fi)])
                    return f
                proj(mkf(), None, nm, h * 256 + fi * 128)
        for tb in range(0, T, 512):
            ps, pskey = psP.get()
            wsrc, wsrckey = scr["Wgk1"]
            for k in range(8):
                w_ap, wkey = ws.load_cached(
                    (wsrckey, k), wsrc[k * 128:(k + 1) * 128, 0:16],
                    reads=[wsrckey])
                P.emit("pe",
                       lambda E, ps=ps, w=w_ap, x=h_fm.tile(k, tb, tb + 512),
                              st=(k == 0), sp=(k == 7):
                           E.matmul(ps[0:16, :], w, x, start=st, stop=sp),
                       reads=[wkey, h_fm.key(k)],
                       writes=[pskey] if k in (0, 7) else [])
            P.emit("act", lambda E, d=t1[0:16, tb:tb + 512], s=ps:
                       E.activation(d, s[0:16, :], AF.Copy),
                   reads=[pskey], writes=[(name, "t1")])
        for tb in range(0, T, 512):
            ps, pskey = psP.get()
            wsrc, wsrckey = scr["Wgk2"]
            w_ap, wkey = ws.load_cached(
                (wsrckey, "gk2", h), wsrc[0:16, h * 128:(h + 1) * 128],
                reads=[wsrckey])
            P.emit("pe", lambda E, ps=ps, w=w_ap, x=t1[0:16, tb:tb + 512]:
                       E.matmul(ps[:], w, x, start=True, stop=True),
                   reads=[wkey, (name, "t1")], writes=[pskey])
            P.emit("dve", lambda E, s=ps, b=cols["bgk2"][:, h:h + 1]:
                       E.tensor_scalar_add(gl32, s[:], b),
                   reads=[pskey, ("bgk2",)], writes=[(name, "gl32")])
            P.emit("act", lambda E: E.activation(gabs, gl32, AF.Abs),
                   reads=[(name, "gl32")], writes=[(name, "gabs")])
            P.emit("act", lambda E: E.activation(gabs, gabs,
                                                 AF.Exp, scale=-1.0),
                   reads=[(name, "gabs")], writes=[(name, "gabs")])
            one_ap, one_key = P.cconst(1.0)
            P.emit("act", lambda E: E.activation(gabs, gabs,
                                                 AF.Ln, bias=one_ap),
                   reads=[(name, "gabs"), one_key], writes=[(name, "gabs")])
            P.emit("dve", lambda E: E.tensor_scalar_min(gl32, gl32, 0.0),
                   reads=[(name, "gl32")], writes=[(name, "gl32")])
            P.emit("dve", lambda E, d=glog[:, tb:tb + 512]:
                       E.tensor_sub(d, gl32, gabs),
                   reads=[(name, "gl32"), (name, "gabs")],
                   writes=[(name, "glog")])
            P.emit("dve", lambda E, d=glog[:, tb:tb + 512]:
                       E.tensor_scalar_mul(d, d, 1.0 / 16.0),
                   reads=[(name, "glog")], writes=[(name, "glog")])
        for i in range(64):
            for fi in range(2):
                ps, pskey = psP.get()
                psbf = ps[0:64, 0:64].bitcast(BF16)
                P.emit("pe",
                       lambda E, ps=psbf,
                              s=v_h.tile(fi, i * 64, (i + 1) * 64):
                           E.transpose(ps, s, consts["ident"][:]),
                       reads=[v_h.key(fi), ("const", "ident")],
                       writes=[pskey])
                P.emit("act",
                       lambda E, d=v_tm[:, i * 256 + fi * 128:
                                        i * 256 + (fi + 1) * 128], s=psbf:
                           E.activation(d, s, AF.Copy),
                       reads=[pskey], writes=[(name, "vtm", i)])
        P.emit("dve", lambda E: E.memset(S32[:], 0.0), writes=[(name, "S")])
        for ci in range(64):
            c0 = ci * CH
            P.emit("dve", lambda E, g=glog[:, c0:c0 + CH]:
                       E.tensor_tensor_scan(bchunk[:], onesc[:], g, 0.0,
                                            OP.mult, OP.add),
                   reads=[(name, "glog"), (name, "onesc")],
                   writes=[(name, "b")])
            P.emit("act", lambda E: E.activation(ebs[:], bchunk[:], AF.Exp,
                                                 bias=lns_ap),
                   reads=[(name, "b"), lns_key], writes=[(name, "ebs")])
            P.emit("dve", lambda E, q=q_h[:, c0:c0 + CH]:
                       E.tensor_mul(qe[:], q, ebs[:]),
                   reads=[(name, "qh"), (name, "ebs")], writes=[(name, "qe")])
            P.emit("act", lambda E: E.activation(ebs[:], bchunk[:], AF.Exp,
                                                 scale=-1.0),
                   reads=[(name, "b"), (name, "qe")], writes=[(name, "ebs")])
            P.emit("dve", lambda E, k=k_h[:, c0:c0 + CH]:
                       E.tensor_mul(ke[:], k, ebs[:]),
                   reads=[(name, "kh"), (name, "ebs")], writes=[(name, "ke")])
            P.emit("act", lambda E, bl=bchunk[:, CH - 1:CH]:
                       E.activation(ebs[:], bchunk[:], AF.Exp, scale=-1.0,
                                    bias=bl),
                   reads=[(name, "b"), (name, "ke")], writes=[(name, "ebs")])
            P.emit("dve", lambda E, k=k_h[:, c0:c0 + CH]:
                       E.tensor_mul(kd[:], k, ebs[:]),
                   reads=[(name, "kh"), (name, "ebs")], writes=[(name, "kd")])
            P.emit("act", lambda E, bl=bchunk[:, CH - 1:CH]:
                       E.activation(ebl[:], bl, AF.Exp),
                   reads=[(name, "b")], writes=[(name, "ebl")])
            ps, pskey = psA.get()
            P.emit("pe", lambda E, ps=ps: E.matmul(ps[:], ke[:], qe[:],
                                                   start=True, stop=True),
                   reads=[(name, "ke"), (name, "qe")], writes=[pskey])
            P.emit("dve", lambda E, s=ps,
                          m=consts["tri"][0:64, 0:64]:
                       E.tensor_mul(atb[:], s[:], m),
                   reads=[pskey, ("const", "tri")], writes=[(name, "atb")])
            P.emit("dve", lambda E: E.tensor_copy(Sbf[:], S32[:]),
                   reads=[(name, "S")], writes=[(name, "Sbf")])
            po, pokey = psO.get()
            P.emit("pe",
                   lambda E, ps=po,
                          v=v_tm[:, ci * 256:(ci + 1) * 256]:
                       E.matmul(ps[:], atb[:], v, start=True, stop=False),
                   reads=[(name, "atb"), (name, "vtm", ci)],
                   writes=[pokey])
            P.emit("pe", lambda E, ps=po: E.matmul(ps[:], qe[:], Sbf[:],
                                                   start=False, stop=True),
                   reads=[(name, "qe"), (name, "Sbf")], writes=[pokey])
            ps2, ps2key = psA.get()
            psbf = ps2[:, 0:64].bitcast(BF16)
            P.emit("pe", lambda E, ps=psbf: E.transpose(
                       ps, kd[:], consts["ident"][:]),
                   reads=[(name, "kd"), ("const", "ident")], writes=[ps2key])
            P.emit("act", lambda E, s=psbf: E.activation(kdtm[:], s, AF.Copy),
                   reads=[ps2key], writes=[(name, "kdtm")])
            psd, psdkey = psSd.get()
            P.emit("pe",
                   lambda E, ps=psd,
                          v=v_tm[:, ci * 256:(ci + 1) * 256]:
                       E.matmul(ps[:], kdtm[:], v, start=True, stop=True),
                   reads=[(name, "kdtm"), (name, "vtm", ci)],
                   writes=[psdkey])
            P.emit("dve", lambda E: E.tensor_scalar_mul(S32[:], S32[:],
                                                        ebl[:, 0:1]),
                   reads=[(name, "S"), (name, "ebl"), (name, "Sbf")],
                   writes=[(name, "S")])
            P.emit("dve", lambda E, s=psd: E.tensor_add(S32[:], S32[:], s[:]),
                   reads=[(name, "S"), psdkey], writes=[(name, "S")])
            P.emit("act", lambda E, s=po: E.activation(
                       osq[:], s[:], AF.Square, accum_out=ossq[:, 0:1]),
                   reads=[pokey], writes=[(name, "ossq")])
            P.emit("act", lambda E, s=po: E.activation(o32[:], s[:], AF.Copy),
                   reads=[pokey], writes=[(name, "o32")])
            eps_ap, eps_key = P.cconst(1e-6)
            P.emit("act", lambda E, b=eps_ap[0:64, :]:
                       E.activation(ossq[:], ossq[:], AF.Ln,
                                    scale=1.0 / 256, bias=b),
                   reads=[(name, "ossq"), eps_key], writes=[(name, "ossq")])
            P.emit("act", lambda E: E.activation(ossq[:], ossq[:], AF.Exp,
                                                 scale=-0.5),
                   reads=[(name, "ossq")], writes=[(name, "ossq")])
            P.emit("dve", lambda E: E.tensor_scalar_mul(o32[:], o32[:],
                                                        ossq[:, 0:1]),
                   reads=[(name, "o32"), (name, "ossq")],
                   writes=[(name, "o32")])
            P.emit("dve", lambda E: E.tensor_mul(o32[:], o32[:],
                                                 gnr[0:64, :]),
                   reads=[(name, "o32"), (name, "gnr")],
                   writes=[(name, "o32")])
            pg, pgkey = psO.get()
            for fi in range(2):
                psbf2 = pg[:, 64 * fi:64 * fi + 64].bitcast(BF16)
                P.emit("pe",
                       lambda E, ps=psbf2,
                              s=g_h.tile(fi, c0, c0 + CH):
                           E.transpose(ps, s, consts["ident"][:]),
                       reads=[g_h.key(fi), ("const", "ident")],
                       writes=[pgkey] if fi == 1 else [])
            P.emit("act", lambda E, s=pg[:, 0:128].bitcast(BF16):
                       E.activation(gsil[:], s, AF.Silu),
                   reads=[pgkey], writes=[(name, "gsil")])
            P.emit("dve", lambda E: E.tensor_mul(otm[:], o32[:], gsil[:]),
                   reads=[(name, "o32"), (name, "gsil")],
                   writes=[(name, "otm")])
            for fi in range(2):
                ps3, ps3key = psSd.get()
                psbf3 = ps3[:, 0:32].bitcast(BF16)
                P.emit("pe",
                       lambda E, ps=psbf3, s=otm[:, fi * 128:(fi + 1) * 128]:
                           E.transpose(ps, s, consts["ident"][0:64, 0:64]),
                       reads=[(name, "otm"), ("const", "ident")],
                       writes=[ps3key])
                P.emit("act",
                       lambda E, d=obf[:, fi * CH:(fi + 1) * CH], s=psbf3:
                           E.activation(d, s, AF.Copy),
                       reads=[ps3key], writes=[(name, "obf", fi)])
            P.dma("sp",
                  [(lambda E, d=o_hbm[h * 256 + fi * 128:
                                      h * 256 + (fi + 1) * 128, c0:c0 + CH],
                           s=obf[:, fi * CH:(fi + 1) * CH]:
                        E.dma_start(out=d, in_=s)) for fi in range(2)],
                  reads=[(name, "obf", 0), (name, "obf", 1)],
                  writes=[(name, "opark")])
    return (name, "opark")


def emit_layer3_pre(P):
    f = lambda n, shape: P.dram(n, shape, F32, kind="ExternalInput")
    scr = {}
    for nm, r, c in (("Wq", D, 512), ("Wk", D, 512), ("Wv", D, D),
                     ("Wg", D, D), ("Wgk1", D, 16), ("Wgk2", 16, 512),
                     ("Wo", D, D), ("Wgate", D, 2816), ("Wup", D, 2816),
                     ("Wdown", 2816, D), ("la", D, 16), ("lb", 16, D)):
        scr[nm] = emit_bf16_scratch(P, nm + "s", f(nm, (r, c)), r, c)
    return scr


def emit_layer3(P, x_h, xo_h, consts, scr, debug=False):
    nc = P.nc
    f = lambda n, shape: P.dram(n, shape, F32, kind="ExternalInput")
    anw_h = f("anw", (128, 8))
    bgk2_h = f("bgk2", (128, 4))
    gnw_h = f("gnw", (1, 256))
    mnw_h = f("mnw", (128, 8))
    qk_h = f("qkern", (D, D))
    qb_h = f("qbias", (128, 8))
    s_h = P.dram("s_park", (D, T), F32,
                 kind="ExternalOutput" if debug else "Internal")
    xn32_h = P.dram("xn32_park", (D, T), F32)
    xnbf_h = P.dram("xnbf_park", (D, T), BF16)
    o_park = P.dram("o_park", (D, T), BF16)
    x2_h = P.dram("x2_park", (D, T), F32)
    wqs_h = P.dram("wq_scr", (D, D), BF16)

    cols = {}
    for nm, hb, w in (("anw", anw_h, 8), ("bgk2", bgk2_h, 4),
                      ("mnw", mnw_h, 8), ("qbias", qb_h, 8)):
        cols[nm] = P.sbuf("sb_" + nm, [128, w], F32)
        P.dma("sp", lambda E, d=cols[nm], s=hb: E.dma_start(out=d[:], in_=s),
              writes=[(nm,)])
    gnorm_row = P.sbuf("sb_gnr", [1, 256], F32)
    P.dma("sp", lambda E: E.dma_start(out=gnorm_row[:], in_=gnw_h),
          writes=[("gnorm_row",)])
    cols["gnorm_row"] = gnorm_row[:]

    P.push_region()
    xn = FMT(P, "xn", D, T, BF16)
    emit_ln(P, "ln", x_h, xn.sb, consts, xn_f32_hbm=xn32_h)
    P.dma("sp", [(lambda E, d=xnbf_h[k * 128:(k + 1) * 128, :],
                         s=xn.tile(k): E.dma_start(out=d, in_=s))
                 for k in range(8)],
          reads=[xn.key(k) for k in range(8)], writes=[("xnbf_park",)])
    P.pop_region()

    P.push_region()
    h_fm = FMT(P, "h_fm", D, T, BF16)
    P.push_region()
    emit_rms(P, "rms1", xn32_h, ("ln", "xn32"), cols["anw"], ("anw",),
             h_fm, consts, eps=1e-6)
    P.pop_region()
    opark_key = emit_gla(P, "gla", h_fm, scr, cols, consts, o_park)
    P.pop_region()

    P.push_region()
    xt, xk = make_hbm_xtile(P, "oat", o_park, opark_key)
    wsO = WRing(P, "l3wsO", 9, [128, 1024], q="pool")
    psO2 = PsRing(P, "l3psO2", 2, [128, 512])
    a32 = P.sbuf("l3a32", [128, 512], F32)
    xc0 = P.sbuf("l3xc0", [128, 512], F32)

    def epiO(ps, pskey, fm, ta, tb):
        P.dma("sp", lambda E, d=xc0[:],
                     s=xn32_h[fm * 128:(fm + 1) * 128, ta:tb]:
                  E.dma_start(out=d, in_=s), writes=[("l3xc0",)])
        P.emit("dve", lambda E, s=ps: E.tensor_add(a32[:], s[:], xc0[:]),
               reads=[pskey, ("l3xc0",)], writes=[("l3a32",)])
        P.dma("sp", lambda E, d=x2_h[fm * 128:(fm + 1) * 128, ta:tb]:
                  E.dma_start(out=d, in_=a32[:]),
              reads=[("l3a32",)], writes=[("x2",)])
    emit_gemm(P, "l3o", xt, xk, 8, scr["Wo"], D, wsO, psO2, epiO, TMAC=512)
    P.pop_region()

    P.push_region()
    n_fm = FMT(P, "n_fm", D, T, BF16)
    P.push_region()
    emit_rms(P, "rms2", x2_h, ("x2",), cols["mnw"], ("mnw",), n_fm, consts,
             eps=1e-6)
    P.pop_region()
    TM3 = 1024
    mm = FMT(P, "l3mm", 2816, TM3, BF16)
    swi3 = P.sbuf("l3swi", [128, TM3], F32)
    ws3 = WRing(P, "l3ws3", 18, [128, 1024], q="pool")
    psA3 = PsRing(P, "l3psA3", 2, [128, TM3])
    psB3 = PsRing(P, "l3psB3", 2, [128, 512])
    sc3 = P.sbuf("l3sc", [128, 512], F32)

    for tm in range(0, T, TM3):
        def epiG(ps, pskey, fm, ta, tb):
            P.emit("act", lambda E, s=ps: E.activation(
                       swi3[:, 0:tb - ta], s[:, 0:tb - ta], AF.Silu),
                   reads=[pskey], writes=[("l3swi",)])
        def epiU(ps, pskey, fm, ta, tb):
            P.emit("dve",
                   lambda E, s=ps, d=mm.tile(fm, 0, tb - ta):
                       E.tensor_mul(d, swi3[:, 0:tb - ta], s[:, 0:tb - ta]),
                   reads=[pskey, ("l3swi",)], writes=[mm.key(fm)])
        for fm in range(22):
            emit_gemm(P, "l3g", lambda k, a, b: n_fm.tile(k, a, b), n_fm.key,
                      8, scr["Wgate"], 128, ws3, psA3, 
                      lambda ps, pk, _fm, ta, tb, fm=fm: epiG(ps, pk, fm, ta, tb),
                      f0=fm * 128, t0=tm, t1=tm + TM3, TMAC=TM3)
            emit_gemm(P, "l3u", lambda k, a, b: n_fm.tile(k, a, b), n_fm.key,
                      8, scr["Wup"], 128, ws3, psA3,
                      lambda ps, pk, _fm, ta, tb, fm=fm: epiU(ps, pk, fm, ta, tb),
                      f0=fm * 128, t0=tm, t1=tm + TM3, TMAC=TM3)
        def epiD(ps, pskey, fm, ta, tb):
            P.dma("sp", lambda E, d=sc3[:],
                         s=x2_h[fm * 128:(fm + 1) * 128, ta:tb]:
                      E.dma_start(out=d, in_=s), writes=[("l3sc",)])
            P.emit("dve", lambda E, s=ps: E.tensor_add(sc3[:], sc3[:], s[:]),
                   reads=[pskey, ("l3sc",)], writes=[("l3sc",)])
            P.dma("sp", lambda E, d=s_h[fm * 128:(fm + 1) * 128, ta:tb]:
                      E.dma_start(out=d, in_=sc3[:]),
                  reads=[("l3sc",)], writes=[("s",)])
        emit_gemm(P, "l3d", lambda k, a, b: mm.tile(k, a - tm, b - tm),
                  mm.key, 22, scr["Wdown"], D, ws3, psB3, epiD,
                  t0=tm, t1=tm + TM3, TMAC=512)
    P.pop_region()

    P.push_region()
    qt = P.sbuf("l3qt", [128, 2 * D], F32)
    qbf = P.sbuf("l3qbf", [128, D], BF16)
    gmax = P.sbuf("l3gmax", [128, 1], F32)
    tmax = P.sbuf("l3tmax", [128, 1], F32)
    scol = P.sbuf("l3scol", [128, 2], F32)
    psQ = P.psum("l3psQ", [128, 128], F32)
    for kk in range(8):
        b = kk % 2
        P.dma("sp", lambda E, d=qt[:, b * D:(b + 1) * D],
                     s=qk_h[kk * 128:(kk + 1) * 128, :]:
                  E.dma_start(out=d, in_=s), writes=[("l3qt", b)])
        P.emit("dve", lambda E, s=qt[:, b * D:(b + 1) * D]:
                   E.reduce_max(tmax[:], s, AX.X, apply_absolute_value=True),
               reads=[("l3qt", b)], writes=[("l3tmax",)])
        if kk == 0:
            P.emit("dve", lambda E: E.tensor_copy(gmax[:], tmax[:]),
                   reads=[("l3tmax",)], writes=[("l3gmax",)])
        else:
            P.emit("dve", lambda E: E.tensor_max(gmax[:], gmax[:], tmax[:]),
                   reads=[("l3tmax",), ("l3gmax",)], writes=[("l3gmax",)])
    P.emit("pe", lambda E: E.transpose(psQ[0:1, 0:128], gmax[:],
                                       consts["ident_f32"][:]),
           reads=[("l3gmax",), ("const", "ident_f32")], writes=[("l3psQ",)])
    rowb = P.sbuf("l3rowb", [1, 128], F32)
    P.emit("act", lambda E: E.activation(rowb[:], psQ[0:1, 0:128], AF.Copy),
           reads=[("l3psQ",)], writes=[("l3rowb",)])
    rowm = P.sbuf("l3rowm", [1, 1], F32)
    P.emit("dve", lambda E: E.reduce_max(rowm[:], rowb[:], AX.X),
           reads=[("l3rowb",)], writes=[("l3rowm",)])
    P.emit("pe", lambda E: E.matmul(psQ[:, 0:1], consts["ones_f32"][0:1, :],
                                    rowm[:], start=True, stop=True),
           reads=[("l3rowm",), ("const", "ones_f32")], writes=[("l3psQ",)])
    P.emit("act", lambda E: E.activation(scol[:, 0:1], psQ[:, 0:1], AF.Copy,
                                         scale=1.0 / 7.0),
           reads=[("l3psQ",)], writes=[("l3scol",)])
    P.emit("dve", lambda E: E.reciprocal(scol[:, 1:2], scol[:, 0:1]),
           reads=[("l3scol",)], writes=[("l3srec",)])
    RNE = 12582912.0
    for kk in range(8):
        b = kk % 2
        P.dma("sp", lambda E, d=qt[:, b * D:(b + 1) * D],
                     s=qk_h[kk * 128:(kk + 1) * 128, :]:
                  E.dma_start(out=d, in_=s), writes=[("l3qt", b)])
        ap = qt[:, 0:D] if b == 0 else qt[:, D:2 * D]
        P.emit("dve", lambda E, a=ap: E.tensor_scalar(
                   a, a, scol[:, 1:2], None, OP.mult),
               reads=[("l3qt", b), ("l3srec",)], writes=[("l3qt", b)])
        P.emit("dve", lambda E, a=ap: E.tensor_scalar(
                   a, a, RNE, RNE, OP.add, OP.subtract),
               reads=[("l3qt", b)], writes=[("l3qt", b)])
        P.emit("dve", lambda E, a=ap: E.tensor_scalar(
                   qbf[:], a, scol[:, 0:1], None, OP.mult),
               reads=[("l3qt", b), ("l3scol",)], writes=[("l3qbf",)])
        P.dma("sp", lambda E, d=wqs_h[kk * 128:(kk + 1) * 128, :]:
                  E.dma_start(out=d, in_=qbf[:]),
              reads=[("l3qbf",)], writes=[("wq_scr",)])
    tl = P.sbuf("l3tl", [16, T], BF16)
    xs, xskey = make_hbm_xtile(P, "l3x", xnbf_h, ("xnbf_park",))
    wsF = WRing(P, "l3wsF", 10, [128, 1024], q="pool")
    psF = PsRing(P, "l3psF", 2, [128, 512])
    for tb in range(0, T, 512):
        ps, pskey = psF.get()
        lsrc, lkey = scr["la"]
        for k in range(8):
            w_ap, wkey = wsF.load_cached(
                (lkey, k), lsrc[k * 128:(k + 1) * 128, 0:16],
                reads=[lkey])
            xap = xs(k, tb, tb + 512)
            xdep = xskey(k)
            P.emit("pe", lambda E, ps=ps, w=w_ap, x=xap,
                          st=(k == 0), sp=(k == 7):
                       E.matmul(ps[0:16, :], w, x, start=st, stop=sp),
                   reads=[wkey] + ([xdep] if xdep else []),
                   writes=[pskey] if k in (0, 7) else [])
        P.emit("act", lambda E, d=tl[0:16, tb:tb + 512], s=ps:
                   E.activation(d, s[0:16, :], AF.Copy),
               reads=[pskey], writes=[("l3tl",)])
    gc3 = P.sbuf("l3gc", [128, 512], F32)
    xcc = P.sbuf("l3xcc", [128, 512], F32)
    scc = P.sbuf("l3scc", [128, 512], F32)

    def extraL(ps, pskey, fm, ta, tb):
        lbsrc, lbkey = scr["lb"]
        w_ap, wkey = wsF.load_cached(
            (lbkey, "lb", fm), lbsrc[0:16, fm * 128:(fm + 1) * 128],
            reads=[lbkey])
        P.emit("pe", lambda E, ps=ps, w=w_ap, x=tl[0:16, ta:tb]:
                   E.matmul(ps[:, 0:tb - ta], w, x, start=True, stop=False),
               reads=[wkey, ("l3tl",)], writes=[pskey])

    def epiF(ps, pskey, fm, ta, tb):
        P.dma("sp", [(lambda E, d=xcc[:],
                             s=x_h[fm * 128:(fm + 1) * 128, ta:tb]:
                          E.dma_start(out=d, in_=s)),
                     (lambda E, d=scc[:],
                             s=s_h[fm * 128:(fm + 1) * 128, ta:tb]:
                          E.dma_start(out=d, in_=s))],
              reads=[("s",)], writes=[("l3xcc",)])
        P.emit("dve", lambda E, s=ps, b=cols["qbias"][:, fm:fm + 1]:
                   E.tensor_scalar_add(gc3[:], s[:], b),
               reads=[pskey, ("qbias",)], writes=[("l3gc",)])
        P.emit("act", lambda E: E.activation(gc3[:], gc3[:], AF.Silu),
               reads=[("l3gc",)], writes=[("l3gc",)])
        P.emit("dve", lambda E: E.tensor_mul(gc3[:], gc3[:], scc[:]),
               reads=[("l3gc",), ("l3xcc",)], writes=[("l3gc",)])
        P.emit("dve", lambda E: E.tensor_add(gc3[:], gc3[:], xcc[:]),
               reads=[("l3gc",), ("l3xcc",)], writes=[("l3gc",)])
        P.dma("sp", lambda E, d=xo_h[fm * 128:(fm + 1) * 128, ta:tb]:
                  E.dma_start(out=d, in_=gc3[:]),
              reads=[("l3gc",)])
    xs2, xs2key = make_hbm_xtile(P, "l3x2", xnbf_h, ("xnbf_park",))
    emit_gemm(P, "l3f", xs2, xs2key, 8, (wqs_h, ("wq_scr",)), D, wsF, psF,
              epiF, TMAC=512, extra_mm=extraL)
    P.pop_region()


def host_prep_layer3(inputs):
    c = host_constants()
    colsb = lambda v: np.ascontiguousarray(np.asarray(v).reshape(-1, 128).T)
    base = dict(
        anw=colsb(inputs["L3_attn_attn_norm_w"]),
        Wq=np.ascontiguousarray(inputs["L3_attn_Wq"]),
        Wk=np.ascontiguousarray(inputs["L3_attn_Wk"]),
        Wv=np.ascontiguousarray(inputs["L3_attn_Wv"]),
        Wg=np.ascontiguousarray(inputs["L3_attn_Wg"]),
        Wgk1=np.ascontiguousarray(inputs["L3_attn_Wgk1"]),
        Wgk2=np.ascontiguousarray(inputs["L3_attn_Wgk2"]),
        bgk2=colsb(inputs["L3_attn_bgk2"]),
        gnw=np.ascontiguousarray(inputs["L3_attn_gnorm_w"].reshape(1, 256)),
        Wo=np.ascontiguousarray(inputs["L3_attn_Wo"]),
        mnw=colsb(inputs["L3_attn_mlp_norm_w"]),
        Wgate=np.ascontiguousarray(inputs["L3_attn_W_gate"]),
        Wup=np.ascontiguousarray(inputs["L3_attn_W_up"]),
        Wdown=np.ascontiguousarray(inputs["L3_attn_W_down"]),
        qkern=np.ascontiguousarray(inputs["L3_ffn_kernel"]),
        qbias=colsb(inputs["L3_ffn_bias"]),
        la=np.ascontiguousarray(inputs["L3_ffn_a"]),
        lb=np.ascontiguousarray(inputs["L3_ffn_b"]),
        **c)
    return base



def emit_layer0_pre(P):
    f = lambda n, shape, dt=F32: P.dram(n, shape, dt, kind="ExternalInput")
    pre = {}
    pre["inw_bf"] = emit_bf16_scratch(P, "inws", f("inw", (D, 4096)), D, 4096)
    pre["xpw_bf"] = emit_bf16_scratch(P, "xpws", f("xpw", (2048, 96)),
                                      2048, 96)
    pre["dtw_bf"] = emit_bf16_scratch(P, "dtws", f("dtw", (64, 2048)),
                                      64, 2048)
    pre["outw_bf"] = emit_bf16_scratch(P, "outws", f("outw", (2048, D)),
                                       2048, D)
    k0_h = f("k0", (2048, 4096), U8)
    k1_h = f("k1", (4096, 4096), U8)
    k2_h = f("k2", (4096, D), U8)
    wd_h = P.dram("wd_scr", (D, 4096), BF16)
    w1_h = P.dram("w1pm_scr", (4096, 4096), BF16)
    w2_h = P.dram("w2pm_scr", (4096, D), BF16)
    cva = P.sbuf("l0cva", [128, 2 * 1024], U8)
    cvo = P.sbuf("l0cvo", [128, 2 * 512], BF16)
    nb = 0
    for r0 in range(0, 1024, 128):
        for c0 in range(0, 4096, 512):
            b = nb % 2
            nb += 1
            a_ap = cva[:, b * 1024:(b + 1) * 1024]
            o_ap = cvo[:, b * 512:(b + 1) * 512]
            P.dma("pool", [
                (lambda E, d=a_ap[0:128, 0:512],
                        s=k0_h[r0:r0 + 128, c0:c0 + 512]:
                     E.dma_start(out=d, in_=s)),
                (lambda E, d=a_ap[0:128, 512:1024],
                        s=k0_h[1024 + r0:1024 + r0 + 128, c0:c0 + 512]:
                     E.dma_start(out=d, in_=s))], writes=[("l0cv", b)])
            P.emit("pool", lambda E, a=a_ap, o=o_ap:
                       E.tensor_sub(o, a[0:128, 0:512], a[0:128, 512:1024]),
                   reads=[("l0cv", b)], writes=[("l0cvo", b)])
            P.dma("pool", lambda E, d=wd_h[r0:r0 + 128, c0:c0 + 512], s=o_ap:
                      E.dma_start(out=d, in_=s),
                  reads=[("l0cvo", b)], writes=[("wd_scr",)])
    for (kh, wh, rows, colsn, wkey) in ((k1_h, w1_h, 4096, 4096, "w1pm"),
                                        (k2_h, w2_h, 4096, D, "w2pm")):
        for r0 in range(0, rows, 128):
            for c0 in range(0, colsn, 512):
                cc = min(512, colsn - c0)
                b = nb % 2
                nb += 1
                a_ap = cva[:, b * 1024:b * 1024 + cc]
                o_ap = cvo[:, b * 512:b * 512 + cc]
                P.dma("pool", lambda E, d=a_ap, s=kh[r0:r0 + 128, c0:c0 + cc]:
                          E.dma_start(out=d, in_=s), writes=[("l0cv", b)])
                P.emit("pool", lambda E, a=a_ap, o=o_ap: E.tensor_scalar(
                           o, a, 2.0, -1.0, OP.mult, OP.add),
                       reads=[("l0cv", b)], writes=[("l0cvo", b)])
                P.dma("pool", lambda E, d=wh[r0:r0 + 128, c0:c0 + cc], s=o_ap:
                          E.dma_start(out=d, in_=s),
                      reads=[("l0cvo", b)], writes=[(wkey,)])
    pre["wd"] = (wd_h, ("wd_scr",))
    pre["w1pm"] = (w1_h, ("w1pm",))
    pre["w2pm"] = (w2_h, ("w2pm",))
    return pre


def emit_layer0(P, x_h, xo_h, consts, pre, debug=False):
    nc = P.nc
    f = lambda n, shape, dt=F32: P.dram(n, shape, dt, kind="ExternalInput")
    ng_h = f("ng", (128, 8))
    nb_h = f("nb", (128, 8))
    convw_h = f("convw", (128, 64))
    convb_h = f("convb", (128, 16))
    dtb_h = f("dtb", (128, 16))
    alog_h = f("alog", (128, 256))
    dcol_h = f("dcol", (128, 16))
    s0_h = f("s0", (1, 1))
    s1_h = f("s1", (1, 1))
    s2_h = f("s2", (1, 1))
    s_h = P.dram("s_park", (D, T), F32,
                 kind="ExternalOutput" if debug else "Internal")
    g_h = P.dram("g_park", (D, T), F32,
                 kind="ExternalOutput" if debug else "Internal")
    xn32_h = P.dram("xn32_park", (D, T), F32)
    xnbf_h = P.dram("xnbf_park", (D, T), BF16)
    u_h = P.dram("u_park", (2048, T), F32)
    ubf_h = P.dram("ubf_park", (2048, T), BF16)
    z_h = P.dram("z_park", (2048, T), F32)
    dl_h = P.dram("dl_park", (2048, T), F32)
    inw_bf, xpw_bf, dtw_bf, outw_bf = (pre["inw_bf"], pre["xpw_bf"],
                                       pre["dtw_bf"], pre["outw_bf"])
    wd_h, wd_key = pre["wd"]
    w1_h, w1pm_key = pre["w1pm"]
    w2_h, w2pm_key = pre["w2pm"]

    load_sel32(P, consts)

    cols = {}
    for nm, hb, w in (("ng", ng_h, 8), ("nb", nb_h, 8), ("convw", convw_h, 64),
                      ("convb", convb_h, 16), ("dtb", dtb_h, 16),
                      ("alog", alog_h, 256), ("dcol", dcol_h, 16)):
        cols[nm] = P.sbuf("sb_" + nm, [128, w], F32)
        P.dma("sp", lambda E, d=cols[nm], s=hb: E.dma_start(out=d[:], in_=s),
              writes=[(nm,)])
    negA = P.sbuf("negA", [128, 256], F32)
    P.emit("act", lambda E: E.activation(negA[:], cols["alog"][:], AF.Exp),
           reads=[("alog",)], writes=[("negA",)])
    P.emit("dve", lambda E: E.tensor_scalar_mul(negA[:], negA[:], -1.0),
           reads=[("negA",)], writes=[("negA",)])

    scl = P.sbuf("l0scl", [1, 4], F32)
    ccol = P.sbuf("l0ccol", [128, 1], F32)
    psC0 = P.psum("l0psC0", [128, 8], F32)
    P.dma("sp", [lambda E: E.dma_start(out=scl[0:1, 0:1], in_=s0_h),
                 lambda E: E.dma_start(out=scl[0:1, 1:2], in_=s1_h),
                 lambda E: E.dma_start(out=scl[0:1, 2:3], in_=s2_h)],
          writes=[("l0scl",)])
    P.emit("dve", lambda E: E.tensor_mul(scl[0:1, 3:4], scl[0:1, 0:1],
                                         scl[0:1, 1:2]),
           reads=[("l0scl",)], writes=[("l0sc3",)])
    P.emit("dve", lambda E: E.tensor_mul(scl[0:1, 3:4], scl[0:1, 3:4],
                                         scl[0:1, 2:3]),
           reads=[("l0sc3",), ("l0scl",)], writes=[("l0sc3",)])
    P.emit("dve", lambda E: E.tensor_scalar_mul(scl[0:1, 3:4], scl[0:1, 3:4],
                                                8.0),
           reads=[("l0sc3",)], writes=[("l0sc3",)])
    P.emit("pe", lambda E: E.matmul(psC0[:, 0:1], consts["ones_f32"][0:1, :],
                                    scl[0:1, 3:4], start=True, stop=True),
           reads=[("l0sc3",), ("const", "ones_f32")], writes=[("l0psC0",)])
    P.emit("act", lambda E: E.activation(ccol[:], psC0[:, 0:1], AF.Copy),
           reads=[("l0psC0",)], writes=[("l0ccol",)])

    P.push_region()
    xn = FMT(P, "xn", D, T, BF16)
    emit_ln(P, "ln", x_h, xn.sb, consts, xn_f32_hbm=xn32_h)
    P.dma("sp", [(lambda E, d=xnbf_h[k * 128:(k + 1) * 128, :],
                         s=xn.tile(k): E.dma_start(out=d, in_=s))
                 for k in range(8)],
          reads=[xn.key(k) for k in range(8)], writes=[("xnbf_park",)])
    P.pop_region()

    P.push_region()
    TMF = 512
    z1 = FMT(P, "l0z1", 4096, TMF, BF16)
    z2 = FMT(P, "l0z2", 4096, TMF, BF16)
    wsF = WRing(P, "l0wsF", 34, [128, 1024], q="pool")
    psF = PsRing(P, "l0psF", 2, [128, TMF])
    gcb = P.sbuf("l0gcb", [128, 512], F32)
    xsF, xsFkey = make_hbm_xtile(P, "l0ffx", xnbf_h, ("xnbf_park",))
    for tm in range(0, T, TMF):
        def epi1(ps, pskey, fm, ta, tb):
            P.emit("act", lambda E, d=z1.tile(fm, 0, tb - ta), s=ps:
                       E.activation(d, s[:, 0:tb - ta], AF.Relu),
                   reads=[pskey], writes=[z1.key(fm)])
        emit_gemm(P, "l0f1", xsF, xsFkey, 8, (wd_h, ("wd_scr",)), 4096,
                  wsF, psF, epi1, t0=tm, t1=tm + TMF, TMAC=TMF)
        def epi2(ps, pskey, fm, ta, tb):
            P.emit("act", lambda E, d=z2.tile(fm, 0, tb - ta), s=ps:
                       E.activation(d, s[:, 0:tb - ta], AF.Relu),
                   reads=[pskey], writes=[z2.key(fm)])
        emit_gemm(P, "l0f2", lambda k, a, b: z1.tile(k, a - tm, b - tm),
                  z1.key, 32, (w1_h, ("w1pm",)), 4096, wsF, psF, epi2,
                  t0=tm, t1=tm + TMF, TMAC=TMF)
        def epi3(ps, pskey, fm, ta, tb):
            P.emit("dve", lambda E, s=ps: E.tensor_scalar(
                       gcb[:], s[:], ccol[:, 0:1], None, OP.mult),
                   reads=[pskey, ("l0ccol",)], writes=[("l0gcb",)])
            P.emit("act", lambda E: E.activation(gcb[:], gcb[:], AF.Silu),
                   reads=[("l0gcb",)], writes=[("l0gcb",)])
            P.dma("sp", lambda E, d=g_h[fm * 128:(fm + 1) * 128, ta:tb]:
                      E.dma_start(out=d, in_=gcb[:]),
                  reads=[("l0gcb",)], writes=[("g",)])
        emit_gemm(P, "l0f3", lambda k, a, b: z2.tile(k, a - tm, b - tm),
                  z2.key, 32, (w2_h, ("w2pm",)), D, wsF, psF, epi3,
                  t0=tm, t1=tm + TMF, TMAC=512)
    P.pop_region()

    P.push_region()
    xn2 = FMT(P, "xn2", D, T, BF16)
    P.push_region()
    TB2 = 512
    xb2 = P.sbuf("l0xb2", [128, 8 * TB2], F32)
    sq2 = P.sbuf("l0sq2", [128, TB2], F32)
    mu2 = P.sbuf("l0mu2", [128, TB2], F32)
    rs2 = P.sbuf("l0rs2", [128, TB2], F32)
    va2 = P.sbuf("l0va2", [128, TB2], F32)
    ps_s2 = P.psum("l0ps_s2", [128, TB2], F32)
    ps_q2 = P.psum("l0ps_q2", [128, TB2], F32)
    eps_ap, eps_key = P.cconst(LN_EPS)
    for tb in range(T // TB2):
        P.dma("sp",
              [(lambda E, d=xb2[:, k * TB2:(k + 1) * TB2],
                       s=xn32_h[k * 128:(k + 1) * 128,
                                tb * TB2:(tb + 1) * TB2]:
                    E.dma_start(out=d, in_=s)) for k in range(8)],
              reads=[("ln", "xn32")], writes=[("l0xb2",)])
        for k in range(8):
            P.emit("act", lambda E, s=xb2[:, k * TB2:(k + 1) * TB2]:
                       E.activation(sq2[:], s, AF.Square),
                   reads=[("l0xb2",)], writes=[("l0sq2",)])
            P.emit("pe", lambda E, s=xb2[:, k * TB2:(k + 1) * TB2],
                          st=(k == 0), sp=(k == 7):
                       E.matmul(ps_s2[:], consts["ones_f32"][:], s,
                                start=st, stop=sp),
                   reads=[("l0xb2",), ("const", "ones_f32")],
                   writes=[("l0ps_s2",)] if k in (0, 7) else [])
            P.emit("pe", lambda E, st=(k == 0), sp=(k == 7):
                       E.matmul(ps_q2[:], consts["ones_f32"][:], sq2[:],
                                start=st, stop=sp),
                   reads=[("l0sq2",), ("const", "ones_f32")],
                   writes=[("l0ps_q2",)] if k in (0, 7) else [])
        P.emit("act", lambda E: E.activation(mu2[:], ps_s2[:], AF.Copy,
                                             scale=1.0 / D),
               reads=[("l0ps_s2",)], writes=[("l0mu2",)])
        P.emit("dve", lambda E: E.tensor_mul(va2[:], mu2[:], mu2[:]),
               reads=[("l0mu2",)], writes=[("l0va2",)])
        P.emit("dve", lambda E: E.scalar_tensor_tensor(
                   va2[:], ps_q2[:], 1.0 / D, va2[:], OP.mult, OP.subtract),
               reads=[("l0ps_q2",), ("l0va2",)], writes=[("l0va2",)])
        P.emit("act", lambda E: E.activation(va2[:], va2[:], AF.Ln,
                                             bias=eps_ap),
               reads=[("l0va2",), eps_key], writes=[("l0va2",)])
        P.emit("act", lambda E: E.activation(rs2[:], va2[:], AF.Exp,
                                             scale=-0.5),
               reads=[("l0va2",)], writes=[("l0rs2",)])
        for k in range(8):
            P.emit("dve", lambda E, s=xb2[:, k * TB2:(k + 1) * TB2]:
                       E.tensor_sub(sq2[:], s, mu2[:]),
                   reads=[("l0xb2",), ("l0mu2",)], writes=[("l0sq2",)])
            P.emit("dve", lambda E: E.tensor_mul(sq2[:], sq2[:], rs2[:]),
                   reads=[("l0sq2",), ("l0rs2",)], writes=[("l0sq2",)])
            P.emit("dve",
                   lambda E, d=xn2.tile(k, tb * TB2, (tb + 1) * TB2),
                          g=cols["ng"][:, k:k + 1], b=cols["nb"][:, k:k + 1]:
                       E.tensor_scalar(d, sq2[:], g, b, OP.mult, OP.add),
                   reads=[("l0sq2",), ("ng",), ("nb",)], writes=[xn2.key(k)])
    P.pop_region()

    TMI = 512
    cbuf = P.sbuf("l0cbuf", [128, 3 + TMI], F32)
    tails = P.sbuf("l0tails", [128, 16 * 3], F32)
    acc = P.sbuf("l0acc", [128, TMI], F32)
    u32 = P.sbuf("l0u32", [128, TMI], F32)
    ubfb = P.sbuf("l0ubfb", [128, TMI], BF16)
    z32 = P.sbuf("l0z32", [128, TMI], F32)
    wsI = WRing(P, "l0wsI", 10, [128, 1024], q="pool")
    psI = PsRing(P, "l0psI", 2, [128, TMI])
    P.emit("dve", lambda E: E.memset(tails[:], 0.0), writes=[("l0tails",)])

    def epiI(ps, pskey, fm, ta, tb):
        if fm < 16:
            P.emit("act", lambda E, s=ps, d=cbuf[:, 3:3 + TMI]:
                       E.activation(d, s[:], AF.Copy),
                   reads=[pskey], writes=[("l0cbuf",)])
            P.emit("dve", lambda E, t=tails[:, fm * 3:(fm + 1) * 3]:
                       E.tensor_copy(cbuf[:, 0:3], t),
                   reads=[("l0tails",), ("l0cbuf",)], writes=[("l0cbuf",)])
            P.emit("dve", lambda E, w=cols["convw"][:, fm * 4:fm * 4 + 1]:
                       E.tensor_scalar(acc[:], cbuf[:, 0:TMI], w, None,
                                       OP.mult),
                   reads=[("l0cbuf",), ("convw",)], writes=[("l0acc",)])
            for j in range(1, 4):
                P.emit("dve",
                       lambda E, a=cbuf[:, j:j + TMI],
                              w=cols["convw"][:, fm * 4 + j:fm * 4 + j + 1]:
                           E.scalar_tensor_tensor(acc[:], a, w, acc[:],
                                                  OP.mult, OP.add),
                       reads=[("l0cbuf",), ("convw",), ("l0acc",)],
                       writes=[("l0acc",)])
            P.emit("dve", lambda E, t=tails[:, fm * 3:(fm + 1) * 3]:
                       E.tensor_copy(t, cbuf[:, TMI:TMI + 3]),
                   reads=[("l0cbuf",)], writes=[("l0tails",)])
            P.emit("act", lambda E, b=cols["convb"][:, fm:fm + 1]:
                       E.activation(u32[:], acc[:], AF.Silu, bias=b),
                   reads=[("l0acc",), ("convb",)], writes=[("l0u32",)])
            P.emit("dve", lambda E: E.tensor_copy(ubfb[:], u32[:]),
                   reads=[("l0u32",)], writes=[("l0ubfb",)])
            P.dma("sp", [
                (lambda E, d=u_h[fm * 128:(fm + 1) * 128, ta:tb]:
                     E.dma_start(out=d, in_=u32[:])),
                (lambda E, d=ubf_h[fm * 128:(fm + 1) * 128, ta:tb]:
                     E.dma_start(out=d, in_=ubfb[:]))],
                reads=[("l0u32",), ("l0ubfb",)], writes=[("u_park",)])
        else:
            fz = fm - 16
            P.emit("act", lambda E, s=ps: E.activation(z32[:], s[:], AF.Copy),
                   reads=[pskey], writes=[("l0z32",)])
            P.dma("sp", lambda E, d=z_h[fz * 128:(fz + 1) * 128, ta:tb]:
                      E.dma_start(out=d, in_=z32[:]),
                  reads=[("l0z32",)], writes=[("z_park",)])

    emit_gemm(P, "l0in", lambda k, a, b: xn2.tile(k, a, b), xn2.key,
              8, inw_bf, 4096, wsI, psI, epiI, TMAC=TMI)
    P.pop_region()

    BC32 = P.sbuf("l0BC", [32, T], F32)
    P.push_region()
    drbf = P.sbuf("l0drbf", [64, T], BF16)
    d32 = P.sbuf("l0d32", [128, 512], F32)
    dab = P.sbuf("l0dab", [128, 512], F32)
    wsX = WRing(P, "l0wsX", 34, [128, 128])
    psX = PsRing(P, "l0psX", 2, [128, 512])
    xsU, xsUkey = make_hbm_xtile(P, "l0ux", ubf_h, ("u_park",), nslots=18)
    for tb in range(0, T, 512):
        ps, pskey = psX.get()
        xsrc, xsrckey = xpw_bf
        for k in range(16):
            w_ap, wkey = wsX.load_cached(
                (xsrckey, k), xsrc[k * 128:(k + 1) * 128, :],
                reads=[xsrckey])
            xap = xsU(k, tb, tb + 512)
            xdep = xsUkey(k)
            P.emit("pe", lambda E, ps=ps, w=w_ap, x=xap,
                          st=(k == 0), sp=(k == 15):
                       E.matmul(ps[0:96, :], w, x, start=st, stop=sp),
                   reads=[wkey] + ([xdep] if xdep else []),
                   writes=[pskey] if k in (0, 15) else [])
        P.emit("act", lambda E, d=drbf[:, tb:tb + 512], s=ps:
                   E.activation(d, s[0:64, :], AF.Copy),
               reads=[pskey], writes=[("l0drbf",)])
        P.emit("act", lambda E, d=BC32[:, tb:tb + 512], s=ps:
                   E.activation(d, s[64:96, :], AF.Copy),
               reads=[pskey], writes=[("l0BC",)])
    for tb in range(0, T, 512):
        for fm in range(16):
            ps, pskey = psX.get()
            dsrc, dkey = dtw_bf
            w_ap, wkey = wsX.load_cached(
                (dkey, fm), dsrc[0:64, fm * 128:(fm + 1) * 128],
                reads=[dkey])
            P.emit("pe", lambda E, ps=ps, w=w_ap, x=drbf[0:64, tb:tb + 512]:
                       E.matmul(ps[:], w, x, start=True, stop=True),
                   reads=[wkey, ("l0drbf",)], writes=[pskey])
            P.emit("dve", lambda E, s=ps, b=cols["dtb"][:, fm:fm + 1]:
                       E.tensor_scalar_add(d32[:], s[:], b),
                   reads=[pskey, ("dtb",)], writes=[("l0d32",)])
            P.emit("act", lambda E: E.activation(dab[:], d32[:], AF.Abs),
                   reads=[("l0d32",)], writes=[("l0dab",)])
            P.emit("act", lambda E: E.activation(dab[:], dab[:], AF.Exp,
                                                 scale=-1.0),
                   reads=[("l0dab",)], writes=[("l0dab",)])
            one_ap, one_key = P.cconst(1.0)
            P.emit("act", lambda E: E.activation(dab[:], dab[:], AF.Ln,
                                                 bias=one_ap),
                   reads=[("l0dab",), one_key], writes=[("l0dab",)])
            P.emit("dve", lambda E: E.tensor_scalar_max(d32[:], d32[:], 0.0),
                   reads=[("l0d32",)], writes=[("l0d32",)])
            P.emit("dve", lambda E: E.tensor_add(d32[:], d32[:], dab[:]),
                   reads=[("l0d32",), ("l0dab",)], writes=[("l0d32",)])
            P.dma("sp", lambda E, d=dl_h[fm * 128:(fm + 1) * 128, tb:tb + 512]:
                      E.dma_start(out=d, in_=d32[:]),
                  reads=[("l0d32",)], writes=[("dl_park",)])
    P.pop_region()

    P.push_region()
    CHS = 512
    dbuf = P.sbuf("scn_d", [128, 16 * CHS], F32)
    dubuf = P.sbuf("scn_du", [128, 16 * CHS], F32)
    yacc = P.sbuf("scn_y", [128, 16 * CHS], F32)
    brep = [P.sbuf(f"scn_brep{i}", [128, CHS], F32) for i in range(2)]
    crep = [P.sbuf(f"scn_crep{i}", [128, CHS], F32) for i in range(2)]
    da = [P.sbuf(f"scn_da{i}", [128, CHS], F32) for i in range(2)]
    dbu = [P.sbuf(f"scn_dbu{i}", [128, CHS], F32) for i in range(2)]
    hsc = [P.sbuf(f"scn_h{i}", [128, CHS], F32) for i in range(2)]
    chp = [P.sbuf(f"scn_ch{i}", [128, CHS], F32) for i in range(2)]
    carry = P.sbuf("scn_carry", [128, 256], F32)
    utmp = P.sbuf("scn_ut", [128, CHS], F32)
    ztmp = P.sbuf("scn_zt", [128, CHS], F32)
    y2 = P.sbuf("scn_y2", [128, 16 * CHS], BF16)
    psS = PsRing(P, "scn_psS", 2, [128, CHS])
    pso = PsRing(P, "scn_pso", 2, [128, 512])
    wsS = WRing(P, "scn_ws", 12, [128, 256])
    P.emit("dve", lambda E: E.memset(carry[:], 0.0), writes=[("carry",)])
    for ci in range(T // CHS):
        c0 = ci * CHS
        P.dma("sp",
              [(lambda E, d=dbuf[:, s * CHS:(s + 1) * CHS],
                       s_=dl_h[s * 128:(s + 1) * 128, c0:c0 + CHS]:
                    E.dma_start(out=d, in_=s_)) for s in range(16)],
              reads=[("dl_park",)], writes=[("scn_d",)])
        for s in range(16):
            P.dma("sp", lambda E, d=utmp[:],
                         s_=u_h[s * 128:(s + 1) * 128, c0:c0 + CHS]:
                      E.dma_start(out=d, in_=s_),
                  reads=[("u_park",)], writes=[("scn_ut",)])
            P.emit("dve",
                   lambda E, d=dubuf[:, s * CHS:(s + 1) * CHS],
                          dd=dbuf[:, s * CHS:(s + 1) * CHS]:
                       E.tensor_mul(d, dd, utmp[:]),
                   reads=[("scn_ut",), ("scn_d",)], writes=[("scn_du",)])
        P.emit("dve", lambda E: E.memset(yacc[:], 0.0),
               writes=[("scn_y",)])
        def bc_build(n):
            bn = n % 2
            for which in (0, 1):
                dst = (brep, crep)[which]
                ps, pskey = psS.get()
                P.emit("pe",
                       lambda E, ps=ps,
                              sel=consts["sel32"][:, (which * 16 + n) * 128:
                                                  (which * 16 + n + 1) * 128],
                              x=BC32[0:32, c0:c0 + CHS]:
                           E.matmul(ps[:], sel, x, start=True, stop=True),
                       reads=[("l0BC",), ("const", "sel32")], writes=[pskey])
                P.emit("act", lambda E, d=dst[bn], s=ps:
                           E.activation(d[:], s[:], AF.Copy),
                       reads=[pskey],
                       writes=[("scn_brep", bn) if which == 0
                               else ("scn_crep", bn)])

        def front(i):
            n, s = divmod(i, 16)
            pb = i % 2
            bn = n % 2
            if s == 0:
                bc_build(n)
            P.emit("act",
                   lambda E, d=da[pb], src=dbuf[:, s * CHS:(s + 1) * CHS],
                          a=negA[:, s * 16 + n:s * 16 + n + 1]:
                       E.activation(d[:], src, AF.Exp, scale=a),
                   reads=[("scn_d",), ("negA",)], writes=[("scn_da", pb)])
            P.emit("pool",
                   lambda E, d=dbu[pb], du=dubuf[:, s * CHS:(s + 1) * CHS],
                          b=brep[bn]:
                       E.tensor_mul(d[:], du, b[:]),
                   reads=[("scn_du",), ("scn_brep", bn)],
                   writes=[("scn_dbu", pb)])
            P.emit("dve",
                   lambda E, d=hsc[pb], a=da[pb], b=dbu[pb],
                          init=carry[:, s * 16 + n:s * 16 + n + 1]:
                       E.tensor_tensor_scan(d[:], a[:], b[:], init,
                                            OP.mult, OP.add),
                   reads=[("scn_da", pb), ("scn_dbu", pb), ("carry",)],
                   writes=[("scn_h", pb)])

        def back(i):
            n, s = divmod(i, 16)
            pb = i % 2
            bn = n % 2
            P.emit("act",
                   lambda E, d=carry[:, s * 16 + n:s * 16 + n + 1],
                          h=hsc[pb]:
                       E.activation(d, h[:, CHS - 1:CHS], AF.Copy),
                   reads=[("scn_h", pb)], writes=[("carry",)])
            P.emit("dve" if i % 2 == 0 else "pool",
                   lambda E, d=chp[pb], h=hsc[pb], cr=crep[bn]:
                       E.tensor_mul(d[:], h[:], cr[:]),
                   reads=[("scn_h", pb), ("scn_crep", bn)],
                   writes=[("scn_ch", pb)])
            P.emit("dve",
                   lambda E, d=yacc[:, s * CHS:(s + 1) * CHS], c=chp[pb]:
                       E.tensor_add(d, d, c[:]),
                   reads=[("scn_ch", pb), ("scn_y",)],
                   writes=[("scn_y",)])

        for i in range(257):
            if i < 256:
                front(i)
            if i >= 1:
                back(i - 1)
        for s in range(16):
            P.dma("sp", [
                (lambda E, d=utmp[:],
                        s_=u_h[s * 128:(s + 1) * 128, c0:c0 + CHS]:
                     E.dma_start(out=d, in_=s_)),
                (lambda E, d=ztmp[:],
                        s_=z_h[s * 128:(s + 1) * 128, c0:c0 + CHS]:
                     E.dma_start(out=d, in_=s_))],
                reads=[("u_park",), ("z_park",)],
                writes=[("scn_ut",), ("scn_zt",)])
            P.emit("dve",
                   lambda E, y=yacc[:, s * CHS:(s + 1) * CHS],
                          dc=cols["dcol"][:, s:s + 1]:
                       E.scalar_tensor_tensor(y, utmp[:], dc, y,
                                              OP.mult, OP.add),
                   reads=[("scn_ut",), ("dcol",), ("scn_y",)],
                   writes=[("scn_y",)])
            P.emit("act", lambda E: E.activation(ztmp[:], ztmp[:], AF.Silu),
                   reads=[("scn_zt",)], writes=[("scn_zt",)])
            P.emit("dve",
                   lambda E, d=y2[:, s * CHS:(s + 1) * CHS],
                          y=yacc[:, s * CHS:(s + 1) * CHS]:
                       E.tensor_mul(d, y, ztmp[:]),
                   reads=[("scn_y",), ("scn_zt",)], writes=[("scn_y2",)])
        for dm in range(8):
            ps, pskey = pso.get()
            osrc, okey = outw_bf
            w0 = (dm * 128 // 256) * 256
            for k in range(16):
                w_ap, wkey = wsS.load_cached(
                    (okey, k, w0), osrc[k * 128:(k + 1) * 128, w0:w0 + 256],
                    reads=[okey])
                P.emit("pe",
                       lambda E, ps=ps, w=w_ap[:, dm * 128 - w0:
                                               dm * 128 - w0 + 128],
                              x=y2[:, k * CHS:(k + 1) * CHS],
                              st=(k == 0), sp=(k == 15):
                           E.matmul(ps[:], w, x, start=st, stop=sp),
                       reads=[wkey, ("scn_y2",)],
                       writes=[pskey] if k in (0, 15) else [])
            P.dma("sp", lambda E, d=utmp[:],
                         s_=xn32_h[dm * 128:(dm + 1) * 128, c0:c0 + CHS]:
                      E.dma_start(out=d, in_=s_),
                  reads=[("ln", "xn32"), ("scn_ut",)], writes=[("scn_ut",)])
            P.emit("dve", lambda E, s=ps: E.tensor_add(utmp[:], utmp[:],
                                                       s[:]),
                   reads=[pskey, ("scn_ut",)], writes=[("scn_ut",)])
            P.dma("sp", lambda E, d=s_h[dm * 128:(dm + 1) * 128, c0:c0 + CHS]:
                      E.dma_start(out=d, in_=utmp[:]),
                  reads=[("scn_ut",)], writes=[("s",)])
    P.pop_region()

    P.push_region()
    xcc = P.sbuf("l0xcc", [128, 512], F32)
    scc = P.sbuf("l0scc", [128, 512], F32)
    gcc = P.sbuf("l0gcc", [128, 512], F32)
    for fm in range(8):
        for c0 in range(0, T, 512):
            P.dma("sp", [
                (lambda E, d=xcc[:], s=x_h[fm * 128:(fm + 1) * 128,
                                           c0:c0 + 512]:
                     E.dma_start(out=d, in_=s)),
                (lambda E, d=scc[:], s=s_h[fm * 128:(fm + 1) * 128,
                                           c0:c0 + 512]:
                     E.dma_start(out=d, in_=s)),
                (lambda E, d=gcc[:], s=g_h[fm * 128:(fm + 1) * 128,
                                           c0:c0 + 512]:
                     E.dma_start(out=d, in_=s))],
                reads=[("s",), ("g",)], writes=[("l0cmb",), ("l0gcc",)])
            P.emit("dve", lambda E: E.tensor_mul(gcc[:], gcc[:], scc[:]),
                   reads=[("l0cmb",)], writes=[("l0gcc",)])
            P.emit("dve", lambda E: E.tensor_add(gcc[:], gcc[:], xcc[:]),
                   reads=[("l0gcc",), ("l0cmb",)], writes=[("l0gcc",)])
            P.dma("sp", lambda E, d=xo_h[fm * 128:(fm + 1) * 128, c0:c0 + 512]:
                      E.dma_start(out=d, in_=gcc[:]),
                  reads=[("l0gcc",)])
    P.pop_region()


def host_prep_layer0(inputs):
    c = host_constants()
    colsb = lambda v: np.ascontiguousarray(np.asarray(v).reshape(-1, 128).T)
    convw = inputs["L0_attn_conv_w"][:, 0, :]
    convw_cols = np.ascontiguousarray(
        convw.reshape(16, 128, 4).transpose(1, 0, 2).reshape(128, 64))
    alog_cols = np.ascontiguousarray(
        inputs["L0_attn_A_log"].reshape(16, 128, 16).transpose(1, 0, 2)
        .reshape(128, 256))
    base = dict(
        ng=colsb(inputs["L0_attn_norm_g"]),
        nb=colsb(inputs["L0_attn_norm_b"]),
        inw=np.ascontiguousarray(inputs["L0_attn_in_w"].T),
        convw=convw_cols,
        convb=colsb(inputs["L0_attn_conv_b"]),
        xpw=np.ascontiguousarray(inputs["L0_attn_xproj_w"].T),
        dtw=np.ascontiguousarray(inputs["L0_attn_dt_w"].T),
        dtb=colsb(inputs["L0_attn_dt_b"]),
        alog=alog_cols,
        dcol=colsb(inputs["L0_attn_D"]),
        outw=np.ascontiguousarray(inputs["L0_attn_out_w"].T),
        k0=np.ascontiguousarray(inputs["L0_ffn_k0"]).view(np.uint8),
        k1=np.ascontiguousarray(inputs["L0_ffn_k1"]).view(np.uint8),
        k2=np.ascontiguousarray(inputs["L0_ffn_k2"]).view(np.uint8),
        s0=np.asarray(inputs["L0_ffn_s0"], dtype=np.float32).reshape(1, 1),
        s1=np.asarray(inputs["L0_ffn_s1"], dtype=np.float32).reshape(1, 1),
        s2=np.asarray(inputs["L0_ffn_s2"], dtype=np.float32).reshape(1, 1),
        **c)
    return base



def _np_fm(a):
    return np.ascontiguousarray(np.asarray(a).T)


_EMITTERS = [(emit_layer0, host_prep_layer0), (emit_layer1, host_prep_layer1),
             (emit_layer2, host_prep_layer2), (emit_layer3, host_prep_layer3)]

_SHARED_CONSTS = ("c_ident", "c_ident_f32", "c_ones_f32", "c_tri", "c_sel",
                  "c_sel32")


_PRES = [emit_layer0_pre, emit_layer1_pre, emit_layer2_pre, emit_layer3_pre]


def build_fused(nc, es):
    P = Prog(nc, es)
    consts = declare_constants(P)
    x0 = nc.dram_tensor("x", [D, T], F32, kind="ExternalInput").ap()
    xf = nc.dram_tensor("xo", [D, T], F32, kind="ExternalOutput").ap()
    chain = [x0]
    for i in range(3):
        chain.append(nc.dram_tensor(f"xchain{i}", [D, T], F32,
                                    kind="Internal").ap())
    chain.append(xf)
    pres = []
    for i, prefn in enumerate(_PRES):
        P.prefix = f"l{i}_"
        pres.append(prefn(P))
    for i, (emit, _) in enumerate(_EMITTERS):
        P.prefix = f"l{i}_"
        P.push_region()
        emit(P, chain[i], chain[i + 1], consts, pres[i])
        P.pop_region()
    P.prefix = ""
    P.finalize()
    return P


def fused_in_map(inputs):
    base = dict(host_constants())
    for i, (_, prep) in enumerate(_EMITTERS):
        d = prep(inputs)
        for k, v in d.items():
            if k in _SHARED_CONSTS:
                continue
            base[f"l{i}_{k}"] = v
    return base


def kernel(**inputs):
    x = np.asarray(inputs["L0_attn_x"], dtype=np.float32)
    nc = bass.Bass("TRN2", target_bir_lowering=False)
    es = contextlib.ExitStack()
    build_fused(nc, es)
    base = fused_in_map(inputs)
    in_maps = [dict(base, x=np.ascontiguousarray(x[c].T))
               for c in range(NCORE)]
    res = run_bass_kernel_spmd(nc, in_maps, core_ids=list(range(NCORE)))
    es.close()
    out = np.stack([np.asarray(res.results[c]["xo"]).T
                    for c in range(NCORE)], axis=0)
    return np.ascontiguousarray(out, dtype=np.float32)
